# version 13
# baseline (speedup 1.0000x reference)
import numpy as np
from contextlib import ExitStack

import concourse.bass as bass
import concourse.mybir as mybir
from concourse.bass_utils import run_bass_kernel_spmd

F32 = mybir.dt.float32
BF16 = mybir.dt.bfloat16
AF = mybir.ActivationFunctionType
ALU = mybir.AluOpType
AX = mybir.AxisListType.X

S = 2048
D = 1024
HK = 1024
NH = 16
I = 2816
NT = 16
NDC = 8
NIC = 22
EPS_LN1 = 1e-3
EPS_LN = 1e-5
IH = 1408


class Sem:
    def __init__(self, h):
        self.h = h
        self.n = 0

    def inc(self, inst, k=1):
        inst.then_inc(self.h, k)
        self.n += k
        return self.n


def _pieces(width, step=512):
    out, off = [], 0
    while off < width:
        w = min(step, width - off)
        out.append((off, w))
        off += w
    return out


def emit_quant(nc, DV, AC, x_ap, out_ap, W, sc, war_dv=None):
    G = W // 16
    v = nc.vector
    M, Sb, R = sc["M"], sc["S"], sc["R"]
    absx, a, c1, c2, sg = sc["absx"], sc["a"], sc["c1"], sc["c2"], sc["sg"]
    nc.scalar.wait_ge(DV.h, DV.n if war_dv is None else war_dv)
    AC.inc(nc.scalar.sign(out=sg[:, 0:W], in_=x_ap))
    ac_sign = AC.n
    AC.inc(nc.scalar.activation(out=absx[:, 0:W], in_=x_ap, func=AF.Abs))
    ac_absx = AC.n
    xv = x_ap.rearrange("p (g k) -> p g k", k=16)
    DV.inc(v.tensor_reduce(out=M[:, 0:G], in_=xv, axis=AX, op=ALU.max,
                           apply_absolute_value=True))
    v.wait_ge(DV.h, DV.n)
    DV.inc(v.tensor_scalar(out=Sb[:, 0:G], in0=M[:, 0:G], scalar1=1.0 / 6.0,
                           scalar2=1e-12, op0=ALU.mult, op1=ALU.max))
    v.wait_ge(DV.h, DV.n)
    DV.inc(v.reciprocal(out=R[:, 0:G], in_=Sb[:, 0:G]))
    v.wait_ge(DV.h, DV.n)
    v.wait_ge(AC.h, ac_absx)
    Rb = R[:, 0:G].unsqueeze(2).broadcast_to([128, G, 16])
    Sbb = Sb[:, 0:G].unsqueeze(2).broadcast_to([128, G, 16])
    v.tensor_tensor(out=a[:, 0:W].rearrange("p (g k) -> p g k", k=16),
                    in0=absx[:, 0:W].rearrange("p (g k) -> p g k", k=16),
                    in1=Rb, op=ALU.mult)
    v.tensor_scalar(out=c1[:, 0:W], in0=a[:, 0:W], scalar1=0.25, scalar2=None,
                    op0=ALU.is_gt)
    v.scalar_tensor_tensor(out=c2[:, 0:W], in0=a[:, 0:W], scalar=0.75,
                           in1=c1[:, 0:W], op0=ALU.is_gt, op1=ALU.add)
    v.scalar_tensor_tensor(out=c1[:, 0:W], in0=a[:, 0:W], scalar=1.25,
                           in1=c2[:, 0:W], op0=ALU.is_gt, op1=ALU.add)
    v.scalar_tensor_tensor(out=c2[:, 0:W], in0=a[:, 0:W], scalar=1.75,
                           in1=c1[:, 0:W], op0=ALU.is_gt, op1=ALU.add)
    v.tensor_scalar(out=c1[:, 0:W], in0=a[:, 0:W], scalar1=5.0, scalar2=2.0,
                    op0=ALU.is_gt, op1=ALU.mult)
    v.scalar_tensor_tensor(out=absx[:, 0:W], in0=a[:, 0:W], scalar=3.5,
                           in1=c1[:, 0:W], op0=ALU.is_gt, op1=ALU.add)
    v.scalar_tensor_tensor(out=c1[:, 0:W], in0=a[:, 0:W], scalar=2.5,
                           in1=absx[:, 0:W], op0=ALU.is_gt, op1=ALU.add)
    v.scalar_tensor_tensor(out=absx[:, 0:W], in0=c2[:, 0:W], scalar=0.5,
                           in1=c1[:, 0:W], op0=ALU.mult, op1=ALU.add)
    v.scalar_tensor_tensor(out=a[:, 0:W].rearrange("p (g k) -> p g k", k=16),
                           in0=absx[:, 0:W].rearrange("p (g k) -> p g k", k=16),
                           scalar=1.0, in1=Sbb, op0=ALU.mult, op1=ALU.mult)
    v.wait_ge(AC.h, ac_sign)
    DV.inc(v.tensor_mul(out=out_ap, in0=a[:, 0:W], in1=sg[:, 0:W]))
    return DV.n, ac_sign


def emit_ln(nc, DV, AC, y_ap, z_out, W, eps, B):
    v, s = nc.vector, nc.scalar
    s.wait_ge(DV.h, DV.n)
    nst = W // 512
    for j in range(nst):
        DV.inc(v.bn_stats(out=B["st6"][:, j * 6:(j + 1) * 6],
                          in_=y_ap[:, j * 512:(j + 1) * 512]))
    v.wait_ge(DV.h, DV.n)
    DV.inc(v.bn_aggr(out=B["ag"][:, 0:2], in_=B["st6"][:, 0:6 * nst]))
    v.wait_ge(DV.h, DV.n)
    DV.inc(v.tensor_scalar(out=B["t0"][:], in0=B["ag"][:, 1:2], scalar1=eps,
                           scalar2=None, op0=ALU.add))
    v.wait_ge(DV.h, DV.n)
    DV.inc(v.reciprocal(out=B["t1"][:], in_=B["t0"][:]))
    v.wait_ge(DV.h, DV.n)
    DV.inc(v.tensor_scalar(out=B["mu_neg"][:], in0=B["ag"][:, 0:1],
                           scalar1=-1.0, scalar2=None, op0=ALU.mult))
    DV.inc(v.tensor_copy(out=B["junk"][:], in_=B["t1"][:]))
    DV.inc(v.tensor_copy(out=B["junk"][:], in_=B["mu_neg"][:]))
    s.wait_ge(DV.h, DV.n)
    AC.inc(s.activation(out=B["t0"][:], in_=B["t1"][:], func=AF.Sqrt))
    s.wait_ge(AC.h, AC.n)
    AC.inc(s.activation(out=B["t1"][:], in_=B["mu_neg"][:], func=AF.Identity,
                        scale=B["t0"][:, 0:1]))
    s.wait_ge(AC.h, AC.n)
    AC.inc(s.activation(out=z_out, in_=y_ap, func=AF.Identity,
                        scale=B["t0"][:, 0:1], bias=B["t1"][:, 0:1]))
    AC.inc(s.activation(out=B["fj"][:], in_=B["fj"][:], func=AF.Square))
    return AC.n


class P0Stream:

    def __init__(self, nc, c, stk, LD, GD, PE, AC, DV):
        e = stk.enter_context
        self.nc, self.c = nc, c
        self.LD, self.GD, self.PE, self.AC, self.DV = LD, GD, PE, AC, DV
        self.raw = [e(nc.sbuf_tensor(f"p0raw{i}", [128, 1024], F32))
                    for i in range(2)]
        self.qsc = {k: e(nc.sbuf_tensor(
            "p0q" + k, [128, 64 if k in ("M", "S", "R") else 1024],
            BF16 if k == "sg" else F32))
            for k in ("M", "S", "R", "absx", "a", "c1", "c2", "sg")}
        self.qsc["fj"] = c["junkf"]
        self.wbf = [e(nc.sbuf_tensor(f"p0wbf{i}", [128, 1024], BF16))
                    for i in range(2)]
        self.stg = e(nc.sbuf_tensor("p0stg", [128, 1024], BF16))
        self.tps = e(nc.psum_tensor("p0tps", [128, 2048], BF16))
        self.units = []
        for w_d, wT_s in ((c["wg_d"], c["wgT_s"]), (c["wu_d"], c["wuT_s"])):
            for r in range(I // 128):
                self.units.append((
                    w_d[r * 128:(r + 1) * 128, :], 8,
                    wT_s[0:1024, r * 128:(r + 1) * 128]
                    .rearrange("(j a) b -> a j b", j=8)))
        for r in range(8):
            for j0, j1 in ((0, 8), (8, 16), (16, 22)):
                self.units.append((
                    c["wd_d"][r * 128:(r + 1) * 128, j0 * 128:j1 * 128],
                    j1 - j0,
                    c["wdT_s"][j0 * 128:j1 * 128, r * 128:(r + 1) * 128]
                    .rearrange("(j a) b -> a j b", j=j1 - j0)))
        self.n_units = len(self.units)
        self.k = 0
        self.ld = [0] * self.n_units
        self.dvv = [0] * self.n_units
        self.acv = [0] * self.n_units
        self.pe_tr = [0, 0]
        self.evac_dv = [0, 0]
        self.gd = [0, 0]
        sy = nc.sync
        for i in range(min(2, self.n_units)):
            src, nblk, _ = self.units[i]
            sy.dma_start(out=self.raw[i % 2][:, 0:nblk * 128],
                         in_=src).then_inc(LD.h, 16)
            LD.n += 16
            self.ld[i] = LD.n

    @property
    def done(self):
        return self.k > self.n_units

    def step(self):
        if self.done:
            return
        nc = self.nc
        v, t, g, sy = nc.vector, nc.tensor, nc.gpsimd, nc.sync
        s = nc.scalar
        LD, GD, PE, AC, DV = self.LD, self.GD, self.PE, self.AC, self.DV
        k = self.k
        if k < self.n_units:
            src, nblk, _ = self.units[k]
            w = nblk * 128
            v.wait_ge(LD.h, self.ld[k])
            s.wait_ge(LD.h, self.ld[k])
            v.wait_ge(PE.h, self.pe_tr[k % 2])
            dvv, acv = emit_quant(nc, DV, AC, self.raw[k % 2][:, 0:w],
                                  self.wbf[k % 2][:, 0:w], w, self.qsc,
                                  war_dv=self.dvv[k - 1] if k else 0)
            self.dvv[k], self.acv[k] = dvv, acv
            if k + 2 < self.n_units:
                src2, nblk2, _ = self.units[k + 2]
                sy.wait_ge(DV.h, dvv)
                sy.wait_ge(AC.h, acv)
                sy.dma_start(out=self.raw[k % 2][:, 0:nblk2 * 128],
                             in_=src2).then_inc(LD.h, 16)
                LD.n += 16
                self.ld[k + 2] = LD.n
        if k >= 1:
            j = k - 1
            _, nblk, out_ap = self.units[j]
            w = nblk * 128
            par = j % 2
            t.wait_ge(DV.h, self.dvv[j])
            for b in range(nblk):
                PE.inc(t.transpose(
                    self.tps[:, par * 1024 + b * 128:par * 1024 + (b + 1) * 128],
                    self.wbf[par][:, b * 128:(b + 1) * 128],
                    self.c["ident"][:]))
            self.pe_tr[par] = PE.n
            v.wait_ge(PE.h, self.pe_tr[par])
            v.wait_ge(GD.h, self.gd[0])
            DV.inc(v.tensor_copy(out=self.stg[:, 0:w],
                                 in_=self.tps[:, par * 1024:par * 1024 + w]))
            self.evac_dv[par] = DV.n
            g.wait_ge(DV.h, self.evac_dv[par])
            g.dma_start(out=out_ap,
                        in_=self.stg[:, 0:w]
                        .rearrange("p (j b) -> p j b", j=nblk)
                        ).then_inc(GD.h, 16)
            GD.n += 16
            self.gd[0] = GD.n
        self.k += 1


def emit_all(nc, c):
    stop_after = c.get("stop_after")
    LD, GD, PE, AC, DV, GP = (c["LD"], c["GD"], c["PE"], c["AC"], c["DV"],
                              c["GP"])
    v, s, t, g, sy = nc.vector, nc.scalar, nc.tensor, nc.gpsimd, nc.sync
    ident, ones_f, ones_b, Mh = c["ident"], c["ones_f"], c["ones_b"], c["Mh"]

    def barrier():
        PE.inc(t.nop())
        AC.inc(s.nop())
        DV.inc(v.nop())
        GP.inc(g.nop())
        vals = [(PE, PE.n), (AC, AC.n), (DV, DV.n), (GP, GP.n),
                (LD, LD.n), (GD, GD.n)]
        own = {id(t): PE, id(s): AC, id(v): DV, id(g): GP}
        for eng in (t, s, v, g, sy):
            mine = own.get(id(eng))
            for sem, val in vals:
                if sem is mine:
                    continue
                eng.wait_ge(sem.h, val)

    def act_flush():
        AC.inc(s.activation(out=c["junkf"][:], in_=c["junkf"][:],
                            func=AF.Square))
        return AC.n

    def dve_flush():
        DV.inc(v.tensor_copy(out=c["junkv"][:], in_=c["junkv"][:]))
        return DV.n

    def dma(eng, sem, out, in_):
        eng.dma_start(out=out, in_=in_).then_inc(sem.h, 16)
        sem.n += 16
        return sem.n

    g.memset(ident[:], 0.0)
    g.affine_select(out=ident[:], in_=ident[:], compare_op=ALU.not_equal,
                    fill=1.0, base=0, pattern=[[-1, 128]], channel_multiplier=1)
    g.memset(ones_f[:], 1.0)
    g.memset(c["junkf"][:], 1.0)
    g.memset(c["junkv"][:], 1.0)
    GP.inc(g.memset(ones_b[:], 1.0))
    dma(sy, LD, c["bqc"][:], c["bq_d"][:])
    dma(sy, LD, c["bkc"][:], c["bk_d"][:])

    if stop_after == "INIT":
        barrier()
        return
    p0stk = ExitStack()
    p0 = P0Stream(nc, c, p0stk, LD, GD, PE, AC, DV)
    if stop_after in ("P0", "P0w1"):
        while not p0.done:
            p0.step()
        barrier()
        p0stk.close()
        return

    with ExitStack() as p2:
        e = p2.enter_context
        gb = e(nc.sbuf_tensor("gb", [128, D], F32))
        bb = e(nc.sbuf_tensor("bb", [128, D], F32))
        pqkv = e(ExitStack())
        eq = pqkv.enter_context
        qh = eq(nc.sbuf_tensor("qh", [128, NDC * S], BF16))
        kh = eq(nc.sbuf_tensor("kh", [128, NDC * S], BF16))
        vsb = eq(nc.sbuf_tensor("vsb", [128, NT * NH * 65], BF16))
        mh_ready = 0

        with ExitStack() as p2a:
            e2 = p2a.enter_context
            qn = e2(nc.sbuf_tensor("qn", [128, 1], F32))
            kn = e2(nc.sbuf_tensor("kn", [128, 1], F32))
            m2 = e2(nc.sbuf_tensor("m2", [128, 1], F32))
            msq = e2(nc.sbuf_tensor("msq", [128, 1], F32))
            mhe = e2(nc.sbuf_tensor("mhe", [128, 16], F32))
            xT_b = e2(nc.sbuf_tensor("xT_b", [128, NDC * S], BF16))
            wsl = e2(nc.sbuf_tensor("wsl", [128, 8 * 1024], BF16))
            xTf = e2(nc.sbuf_tensor("xTf", [128, 1024], F32))
            q2 = e2(nc.sbuf_tensor("q2", [128, 1024], BF16))
            masks = e2(nc.sbuf_tensor("masks", [128, 8 * 128], BF16))
            dma(g, GD, c["bv_b"][:], c["bv_d"][:])
            dma(g, GD, c["bo_b"][:], c["bo_d"][:])

            ldg = dma(sy, LD, xTf[0:1, :], c["gam_d"][:])
            with nc.psum_tensor("pgb", [128, 1024], F32) as pgb:
                t.wait_ge(LD.h, ldg)
                t.wait_ge(GP.h, GP.n)
                for p in range(2):
                    PE.inc(t.matmul(pgb[:, p * 512:(p + 1) * 512], ones_f[:],
                                    xTf[0:1, p * 512:(p + 1) * 512],
                                    start=True, stop=True))
                v.wait_ge(PE.h, PE.n)
                DV.inc(v.tensor_copy(out=gb[:], in_=pgb[:]))
                sy.wait_ge(PE.h, PE.n)
                dma(sy, LD, xTf[0:1, :], c["bet_d"][:])
                ldb = dma(sy, LD, c["rowtmp_bo"][:], c["bo_d"][:])
                t.wait_ge(DV.h, DV.n)
                t.wait_ge(LD.h, ldb)
                for p in range(2):
                    PE.inc(t.matmul(pgb[:, p * 512:(p + 1) * 512], ones_f[:],
                                    xTf[0:1, p * 512:(p + 1) * 512],
                                    start=True, stop=True))
                v.wait_ge(PE.h, PE.n)
                DV.inc(v.tensor_copy(out=bb[:], in_=pgb[:]))
                gbb_pe = PE.n

            g.memset(masks[:], 0.0)
            for cc in range(8):
                g.memset(masks[0:64, cc * 128 + 2 * cc:
                               cc * 128 + 2 * cc + 1], 1.0)
                g.memset(masks[64:128, cc * 128 + 2 * cc + 1:
                               cc * 128 + 2 * cc + 2], 1.0)
            vv = vsb[:].rearrange("p (sc h w) -> p sc h w", h=NH, w=65)
            GP.inc(g.memset(vv[:, :, :, 64:65], 1.0))

            sy.wait_ge(PE.h, gbb_pe)
            for dcc in range(NDC):
                for hf2 in range(2):
                    if dcc + hf2 >= 1:
                        sy.wait_ge(AC.h, AC.n)
                    ldv = dma(sy, LD, xTf[:],
                              c["xT_d"][dcc * 128:(dcc + 1) * 128,
                                        hf2 * 1024:(hf2 + 1) * 1024])
                    s.wait_ge(LD.h, ldv)
                    AC.inc(s.activation(
                        out=xT_b[:, dcc * S + hf2 * 1024:
                                 dcc * S + (hf2 + 1) * 1024],
                        in_=xTf[:], func=AF.Copy))
            for dcc in range(NDC):
                dma(g, GD, wsl[:, dcc * 1024:(dcc + 1) * 1024],
                    c["wq_d"][dcc * 128:(dcc + 1) * 128, :])

            t.wait_ge(AC.h, AC.n)
            t.wait_ge(GD.h, GD.n)
            v.wait_ge(LD.h, LD.n)
            qk_dv = 0
            with nc.psum_tensor("pq", [128, 2048], F32) as pq:
                for wi, (dst, bias) in enumerate(((qh, c["bqc"]),
                                                  (kh, c["bkc"]))):
                    if wi == 1:
                        g.wait_ge(PE.h, PE.n)
                        for dcc in range(NDC):
                            dma(g, GD, wsl[:, dcc * 1024:(dcc + 1) * 1024],
                                c["wk_d"][dcc * 128:(dcc + 1) * 128, :])
                        t.wait_ge(GD.h, GD.n)
                    for cc in range(NDC):
                        if wi > 0 or cc > 0:
                            t.wait_ge(DV.h, qk_dv)
                        for dcc in range(NDC):
                            for p in range(4):
                                PE.inc(t.matmul(
                                    pq[:, p * 512:(p + 1) * 512],
                                    wsl[:, dcc * 1024 + cc * 128:
                                        dcc * 1024 + cc * 128 + 128],
                                    xT_b[:, dcc * S + p * 512:
                                         dcc * S + (p + 1) * 512],
                                    start=(dcc == 0), stop=(dcc == NDC - 1)))
                        v.wait_ge(PE.h, PE.n)
                        for p in range(4):
                            DV.inc(v.tensor_scalar(
                                out=dst[:, cc * S + p * 512:
                                        cc * S + (p + 1) * 512],
                                in0=pq[:, p * 512:(p + 1) * 512],
                                scalar1=bias[:, cc:cc + 1], scalar2=None,
                                op0=ALU.add))
                        qk_dv = DV.n
                        p0.step()

            g.wait_ge(PE.h, PE.n)
            for dcc in range(NDC):
                dma(g, GD, wsl[:, dcc * 1024:(dcc + 1) * 1024],
                    c["wv_d"][dcc * 128:(dcc + 1) * 128, :])
            wv_gd = GD.n

            with ExitStack() as pnx:
                pn = pnx.enter_context(nc.psum_tensor("pn", [128, 2048], F32))
                pm = pnx.enter_context(nc.psum_tensor("pm", [128, 16], F32))
                pmt = pnx.enter_context(nc.psum_tensor("pmt", [1, 128], BF16))
                mb = pnx.enter_context(nc.sbuf_tensor("mb", [128, 1], BF16))
                mrow_b = pnx.enter_context(nc.sbuf_tensor("mrow_b", [1, 128], BF16))
                for src, dstn in ((qh, qn), (kh, kn)):
                    for cc in range(NDC):
                        for h2 in range(2):
                            if cc + h2 > 0:
                                v.wait_ge(PE.h, PE.n)
                            DV.inc(v.tensor_mul(
                                out=q2[:],
                                in0=src[:, cc * S + h2 * 1024:
                                        cc * S + (h2 + 1) * 1024],
                                in1=src[:, cc * S + h2 * 1024:
                                        cc * S + (h2 + 1) * 1024]))
                            t.wait_ge(DV.h, DV.n)
                            for p in range(2):
                                PE.inc(t.matmul(
                                    pn[:, h2 * 1024 + p * 512:
                                       h2 * 1024 + (p + 1) * 512],
                                    masks[:, cc * 128:(cc + 1) * 128],
                                    q2[:, p * 512:(p + 1) * 512],
                                    start=(cc == 0),
                                    stop=(cc == NDC - 1)))
                    v.wait_ge(PE.h, PE.n)
                    DV.inc(v.tensor_reduce(out=dstn[:], in_=pn[:],
                                           axis=AX, op=ALU.max))
                    v.wait_ge(DV.h, DV.n)
                if c.get("dbg"):
                    g.wait_ge(DV.h, DV.n)
                    dma(g, GD, c["dbg_qn"][:], qn[0:16, 0:1])
                    dma(g, GD, c["dbg_kn"][:], kn[0:16, 0:1])
                DV.inc(v.tensor_add(out=msq[:], in0=qn[:], in1=kn[:]))
                v.wait_ge(DV.h, DV.n)
                DV.inc(v.tensor_scalar(out=m2[:], in0=msq[:],
                                       scalar1=-0.0625, scalar2=None,
                                       op0=ALU.mult))
                v.wait_ge(DV.h, DV.n)
                v.tensor_copy(out=mb[:], in_=m2[:])
                DV.inc(v.tensor_copy(out=q2[:, 0:1024], in_=qh[:, 0:1024]))
                t.wait_ge(DV.h, DV.n)
                PE.inc(t.transpose(pmt[0:1, 0:128], mb[:, 0:1],
                                   ident[:]))
                v.wait_ge(PE.h, PE.n)
                v.tensor_copy(out=mrow_b[:], in_=pmt[0:1, 0:128])
                DV.inc(v.tensor_copy(out=q2[:, 0:1024], in_=qh[:, 0:1024]))
                t.wait_ge(DV.h, DV.n)
                PE.inc(t.matmul(pm[:, 0:16], ones_b[:], mrow_b[0:1, 0:16],
                                start=True, stop=True))
                if c.get("dbg"):
                    g.wait_ge(DV.h, DV.n)
                    dma(g, GD, c["dbg_m2"][:], mrow_b[0:1, 0:16])
                v.wait_ge(PE.h, PE.n)
                DV.inc(v.tensor_copy(out=Mh[:], in_=pm[:]))
                DV.inc(v.tensor_copy(out=mhe[:], in_=Mh[:]))
                mh_ready = DV.n

            t.wait_ge(GD.h, wv_gd)
            vh_dv = 0
            with nc.psum_tensor("pv", [128, 1024], F32) as pv:
                for sc in range(NT):
                    t.wait_ge(DV.h, vh_dv)
                    for dcc in range(NDC):
                        for p in range(2):
                            PE.inc(t.matmul(
                                pv[:, p * 512:(p + 1) * 512],
                                xT_b[:, dcc * S + sc * 128:
                                     dcc * S + sc * 128 + 128],
                                wsl[:, dcc * 1024 + p * 512:
                                    dcc * 1024 + (p + 1) * 512],
                                start=(dcc == 0), stop=False))
                    for p in range(2):
                        PE.inc(t.matmul(pv[:, p * 512:(p + 1) * 512],
                                        ones_b[:],
                                        c["bv_b"][0:1, p * 512:(p + 1) * 512],
                                        start=False, stop=True))
                    v.wait_ge(PE.h, PE.n)
                    v.wait_ge(GP.h, GP.n)
                    DV.inc(v.tensor_copy(
                        out=vsb[:, sc * NH * 65:(sc + 1) * NH * 65]
                        .rearrange("p (h w) -> p h w", w=65)[:, :, 0:64],
                        in_=pv[:].rearrange("p (h w) -> p h w", w=64)))
                    vh_dv = DV.n
                    if sc % 2 == 0:
                        p0.step()
            barrier()
            if stop_after == "P2":
                return

        if c.get("dbg"):
            dma(g, GD, c["dbg_qh"][:], qh[:])
            dma(g, GD, c["dbg_kh"][:], kh[:])
            dma(g, GD, c["dbg_vsb"][:], vsb[:])
            dma(g, GD, c["dbg_mh"][:], Mh[:])
            g.wait_ge(GD.h, GD.n)

        with ExitStack() as p34:
            p3 = p34.enter_context(ExitStack())
            e3 = p3.enter_context
            pT = e3(nc.sbuf_tensor("pT", [128, 2 * 2048], BF16))
            cpair = [e3(nc.sbuf_tensor(f"cpair{i}", [128, 2048], BF16))
                     for i in range(2)]
            Rbf = e3(nc.sbuf_tensor("Rbf", [128, 2048], BF16))
            r_raw = e3(nc.sbuf_tensor("r_raw", [1, 2048], F32))
            r_bf = e3(nc.sbuf_tensor("r_bf", [1, 2048], BF16))
            psA = e3(nc.psum_tensor("psA", [128, 512], F32))
            psB = e3(nc.psum_tensor("psB", [128, 512], F32))
            psC = e3(nc.psum_tensor("psC", [128, 2048], F32))
            psAB = (psA, psB)

            s.wait_ge(DV.h, mh_ready)
            exp_on_bank = [0, 0]
            ctx_last_pe = [0, 0]
            rbf_evac_dv = 0
            cp_gd = [0, 0]
            for h in range(NH):
                cc, half = h // 2, h % 2
                rows = slice(64 * half, 64 * half + 64)
                p0.step()
                for sc in range(NT):
                    par = sc % 2
                    if sc in (5, 11):
                        p0.step()
                    for qq in range(4):
                        ps = psAB[qq % 2]
                        t.wait_ge(AC.h, exp_on_bank[qq % 2])
                        PE.inc(t.matmul(
                            ps[:, 0:512],
                            kh[rows, cc * S + sc * 128:
                               cc * S + sc * 128 + 128],
                            qh[rows, cc * S + qq * 512:
                               cc * S + (qq + 1) * 512],
                            start=True, stop=True))
                        s.wait_ge(PE.h, PE.n)
                        s.wait_ge(PE.h, ctx_last_pe[par])
                        AC.inc(s.activation(
                            out=pT[:, par * 2048 + qq * 512:
                                   par * 2048 + (qq + 1) * 512],
                            in_=ps[:, 0:512], func=AF.Exp, scale=0.125,
                            bias=Mh[:, h:h + 1]))
                        exp_on_bank[qq % 2] = AC.n
                    t.wait_ge(AC.h, act_flush())
                    if sc == 0:
                        t.wait_ge(DV.h, rbf_evac_dv)
                    for p in range(4):
                        PE.inc(t.matmul(
                            psC[0:65, p * 512:(p + 1) * 512],
                            vsb[:, sc * NH * 65 + h * 65:
                                sc * NH * 65 + h * 65 + 65],
                            pT[:, par * 2048 + p * 512:
                               par * 2048 + (p + 1) * 512],
                            start=(sc == 0), stop=(sc == NT - 1)))
                    ctx_last_pe[par] = PE.n
                v.wait_ge(PE.h, PE.n)
                DV.inc(v.reciprocal(out=r_raw[:], in_=psC[64:65, 0:2048]))
                v.wait_ge(DV.h, DV.n)
                DV.inc(v.tensor_copy(out=r_bf[:], in_=r_raw[:]))
                t.wait_ge(DV.h, DV.n)
                for p in range(4):
                    PE.inc(t.matmul(psC[64:128, p * 512:(p + 1) * 512],
                                    ones_b[0:1, 0:64],
                                    r_bf[0:1, p * 512:(p + 1) * 512],
                                    start=True, stop=True))
                v.wait_ge(PE.h, PE.n)
                DV.inc(v.tensor_copy(out=Rbf[0:64, :], in_=psC[64:128, :]))
                if half == 0:
                    v.wait_ge(GD.h, cp_gd[cc % 2])
                    DV.inc(v.tensor_mul(out=cpair[cc % 2][0:64, :],
                                        in0=psC[0:64, :], in1=Rbf[0:64, :]))
                else:
                    DV.inc(v.tensor_mul(out=cpair[cc % 2][64:128, :],
                                        in0=psC[0:64, :], in1=Rbf[0:64, :]))
                    g.wait_ge(DV.h, DV.n)
                    cp_gd[cc % 2] = dma(
                        g, GD, c["ctx_s"][cc * 128:(cc + 1) * 128, :],
                        cpair[cc % 2][:])
                rbf_evac_dv = DV.n
            barrier()
            if c.get("dbg"):
                dma(g, GD,
                    c["dbg_ctxn"][:].rearrange("p (cc t) -> p cc t", t=S),
                    c["ctx_s"][:].rearrange("(cc p) t -> p cc t", p=128))
                g.wait_ge(GD.h, GD.n)
            if stop_after == "P3":
                return
            p3.close()
            pqkv.close()

            p4 = p34.enter_context(ExitStack())
            e4 = p4.enter_context
            xst = [e4(nc.sbuf_tensor(f"xst{i}", [128, D], F32))
                   for i in range(2)]
            yb = e4(nc.sbuf_tensor("yb", [128, D], F32))
            zb = e4(nc.sbuf_tensor("zb", [128, D], F32))
            ub = e4(nc.sbuf_tensor("ub", [128, D], F32))
            x1t = [e4(nc.sbuf_tensor(f"x1t{i}", [128, D], F32))
                   for i in range(2)]
            x1c = e4(nc.sbuf_tensor("x1c", [128, D], F32))
            sqj = e4(nc.sbuf_tensor("sqj", [128, D], F32))
            st6p4 = e4(nc.sbuf_tensor("st6p4", [128, 12], F32))
            agp4 = e4(nc.sbuf_tensor("agp4", [128, 2], F32))
            st = [e4(nc.sbuf_tensor(f"st{i}", [128, 1], F32))
                  for i in range(7)]
            wo_b2 = e4(nc.sbuf_tensor("wo_b2", [128, NDC * 1024], BF16))
            ctile = [e4(nc.sbuf_tensor(f"ctile{i}", [128, NDC * 128], BF16))
                     for i in range(2)]
            po = [e4(nc.psum_tensor(f"po{i}", [128, 1024], F32))
                  for i in range(2)]
            for cc in range(NDC):
                dma(g, GD, wo_b2[:, cc * 1024:(cc + 1) * 1024],
                    c["wo_d"][cc * 128:(cc + 1) * 128, :])
            t.wait_ge(GD.h, GD.n)

            x_loaded = [0] * NT
            ct_ld = [0] * NT
            ct_pe = [0, 0]

            def load_ctile(tt):
                for cc in range(NDC):
                    dma(sy, LD, ctile[tt % 2][:, cc * 128:(cc + 1) * 128],
                        c["ctx_s"][cc * 128:(cc + 1) * 128,
                                   tt * 128:(tt + 1) * 128])
                ct_ld[tt] = LD.n

            for tt in range(2):
                load_ctile(tt)
                x_loaded[tt] = dma(sy, LD, xst[tt % 2][:],
                                   c["x_d"][tt * 128:(tt + 1) * 128, :])
            ln_bufs = {"st6": st6p4, "ag": agp4, "mu_neg": st[2],
                       "t0": st[3], "t1": st[4], "junk": st[5],
                       "fj": c["junkf"]}
            spill_gd = [0, 0]
            y_dv = [0, 0]
            for tt in range(NT):
                if tt >= 2:
                    t.wait_ge(DV.h, y_dv[tt % 2])
                pp4 = po[tt % 2]
                t.wait_ge(LD.h, ct_ld[tt])
                for cc in range(NDC):
                    for p in range(2):
                        PE.inc(t.matmul(
                            pp4[:, p * 512:(p + 1) * 512],
                            ctile[tt % 2][:, cc * 128:(cc + 1) * 128],
                            wo_b2[:, cc * 1024 + p * 512:
                                  cc * 1024 + (p + 1) * 512],
                            start=(cc == 0), stop=False))
                ct_pe[tt % 2] = PE.n
                for p in range(2):
                    PE.inc(t.matmul(pp4[:, p * 512:(p + 1) * 512], ones_b[:],
                                    c["bo_b"][0:1, p * 512:(p + 1) * 512],
                                    start=False, stop=True))
                v.wait_ge(PE.h, PE.n)
                v.wait_ge(LD.h, x_loaded[tt])
                DV.inc(v.tensor_add(out=yb[:], in0=pp4[:],
                                    in1=xst[tt % 2][:]))
                y_dv[tt % 2] = DV.n
                if c.get("dbg"):
                    g.wait_ge(DV.h, DV.n)
                    dma(g, GD, c["dbg_y"][tt * 128:(tt + 1) * 128, :], yb[:])
                emit_ln(nc, DV, AC, yb[:], zb[:], D, EPS_LN1, ln_bufs)
                v.wait_ge(AC.h, AC.n)
                if c.get("dbg"):
                    g.wait_ge(AC.h, AC.n)
                    dma(g, GD, c["dbg_z"][tt * 128:(tt + 1) * 128, :], zb[:])
                    v.wait_ge(GD.h, GD.n)
                v.scalar_tensor_tensor(out=sqj[:], in0=zb[:], scalar=1.0,
                                       in1=gb[:], op0=ALU.mult,
                                       op1=ALU.mult)
                v.tensor_add(out=zb[:], in0=sqj[:], in1=bb[:])
                DV.inc(v.tensor_add(out=ub[:], in0=zb[:], in1=xst[tt % 2][:]))
                if tt + 2 < NT:
                    sy.wait_ge(DV.h, DV.n)
                    sy.wait_ge(PE.h, ct_pe[tt % 2])
                    load_ctile(tt + 2)
                    x_loaded[tt + 2] = dma(sy, LD, xst[tt % 2][:],
                                           c["x_d"][(tt + 2) * 128:
                                                    (tt + 3) * 128, :])
                emit_ln(nc, DV, AC, ub[:], x1t[tt % 2][:], D, EPS_LN, ln_bufs)
                v.wait_ge(AC.h, AC.n)
                if tt >= 1:
                    v.wait_ge(GD.h, spill_gd[0])
                DV.inc(v.tensor_copy(out=x1c[:], in_=x1t[tt % 2][:]))
                g.wait_ge(DV.h, DV.n)
                spill_gd[0] = dma(g, GD,
                                  c["x1sp"][tt * 128:(tt + 1) * 128, :],
                                  x1c[:])
                if c.get("dbg"):
                    spill_gd[0] = dma(
                        g, GD, c["dbg_x1"][tt * 128:(tt + 1) * 128, :],
                        x1c[:])
            while not p0.done:
                p0.step()
            barrier()
            if stop_after == "P4":
                return

    p0stk.close()
    with ExitStack() as p5:
        e5 = p5.enter_context
        wgT = e5(nc.sbuf_tensor("wgT", [128, NDC * I], BF16))
        wuT = e5(nc.sbuf_tensor("wuT", [128, NDC * I], BF16))
        wdT = e5(nc.sbuf_tensor("wdT", [128, NIC * 1024], BF16))
        x1l = [e5(nc.sbuf_tensor(f"x1l{i}", [128, D], F32)) for i in range(2)]
        xq = [e5(nc.sbuf_tensor(f"xq{i}", [128, D], BF16)) for i in range(2)]
        xqT = [e5(nc.sbuf_tensor(f"xqT{i}", [128, D], BF16))
               for i in range(2)]
        silu_sb = e5(nc.sbuf_tensor("silu_sb", [128, IH], F32))
        h_sb = e5(nc.sbuf_tensor("h_sb", [128, IH], F32))
        hq = e5(nc.sbuf_tensor("hq", [128, IH], BF16))
        hqT = e5(nc.sbuf_tensor("hqT", [128, I], BF16))
        x2c = e5(nc.sbuf_tensor("x2c", [128, D], F32))
        st6p5 = e5(nc.sbuf_tensor("st6p5", [128, 12], F32))
        agp5 = e5(nc.sbuf_tensor("agp5", [128, 2], F32))
        st2 = [e5(nc.sbuf_tensor(f"st2{i}", [128, 1], F32)) for i in range(7)]
        ub2 = silu_sb
        x2b = h_sb
        qsc2 = {k: e5(nc.sbuf_tensor("p5q" + k,
                                     [128, 88 if k in "MSR" else IH],
                                     BF16 if k == "sg" else F32))
                for k in ("M", "S", "R", "absx", "a", "c1", "c2", "sg")}
        qsc2["fj"] = c["junkf"]
        dps = e5(nc.psum_tensor("dps", [128, 1024], F32))
        gps = e5(nc.psum_tensor("gps", [128, 1024], F32))
        gps2 = e5(nc.psum_tensor("gps2", [128, 1024], F32))
        tps2 = e5(nc.psum_tensor("tps2", [128, 2048], BF16))

        x1_ld = [0] * NT
        for tt in range(2):
            dma(sy, LD, x1l[tt % 2][:],
                c["x1sp"][tt * 128:(tt + 1) * 128, :])
        for cc in range(NDC):
            dma(sy, LD, wgT[:, cc * I:(cc + 1) * I],
                c["wgT_s"][cc * 128:(cc + 1) * 128, :])
            dma(sy, LD, wuT[:, cc * I:(cc + 1) * I],
                c["wuT_s"][cc * 128:(cc + 1) * 128, :])
        for ic in range(NIC):
            dma(sy, LD, wdT[:, ic * 1024:(ic + 1) * 1024],
                c["wdT_s"][ic * 128:(ic + 1) * 128, :])
        w_ready = LD.n
        x1_ld = [0] * NT
        x1_ld[0] = x1_ld[1] = LD.n
        t.wait_ge(LD.h, w_ready)
        ln_bufs2 = {"st6": st6p5, "ag": agp5, "mu_neg": st2[2],
                    "t0": st2[3], "t1": st2[4], "junk": st2[5],
                    "fj": c["junkf"]}
        out_gd = [0, 0]
        gu_ps = (gps, gps2)
        gu_dv = {0: 0, 1: 0}
        xqT_dv = [0, 0]

        def quant_x1(tt):
            par = tt % 2
            v.wait_ge(LD.h, x1_ld[tt])
            s.wait_ge(LD.h, x1_ld[tt])
            g.wait_ge(LD.h, x1_ld[tt])
            if c.get("dbg"):
                g.wait_ge(LD.h, x1_ld[tt])
                dma(g, GD, c["dbg_x1l"][tt * 128:(tt + 1) * 128, :],
                    x1l[par][:])
            emit_quant(nc, DV, AC, x1l[par][:], xq[par][:], D, qsc2,
                       war_dv=DV.n)
            if c.get("dbg"):
                g.wait_ge(DV.h, DV.n)
                dma(g, GD, c["dbg_xq"][tt * 128:(tt + 1) * 128, :], xq[par][:])
            dve_flush()
            return DV.n

        def transpose_xq(tt, dvq):
            par = tt % 2
            t.wait_ge(DV.h, dvq)
            for r in range(8):
                PE.inc(t.transpose(tps2[:, r * 128:(r + 1) * 128],
                                   xq[par][:, r * 128:(r + 1) * 128],
                                   ident[:]))
            v.wait_ge(PE.h, PE.n)
            DV.inc(v.tensor_copy(out=xqT[par][:], in_=tps2[:, 0:1024]))
            xqT_dv[par] = DV.n

        dvq0 = quant_x1(0)
        transpose_xq(0, dvq0)
        pieces6 = [(hf, poff, pw) for hf in range(2)
                   for (poff, pw) in _pieces(IH)]
        tr_pe = [0, 0]
        for tt in range(NT):
            par = tt % 2
            t.wait_ge(DV.h, xqT_dv[par])
            fl = [0] * 6

            def gu_piece(k):
                hf, poff, pw = pieces6[k]
                pp = gu_ps[k % 2]
                t.wait_ge(DV.h, gu_dv[k % 2])
                for wi, wTT in enumerate((wgT, wuT)):
                    for dcc in range(NDC):
                        PE.inc(t.matmul(
                            pp[:, wi * 512:wi * 512 + pw],
                            xqT[par][:, dcc * 128:(dcc + 1) * 128],
                            wTT[:, dcc * I + hf * IH + poff:
                                dcc * I + hf * IH + poff + pw],
                            start=(dcc == 0), stop=(dcc == NDC - 1)))
                s.wait_ge(PE.h, PE.n)
                AC.inc(s.activation(out=silu_sb[:, poff:poff + pw],
                                    in_=pp[:, 0:pw], func=AF.Silu))
                fl[k] = act_flush()

            def mq_piece(k):
                hf, poff, pw = pieces6[k]
                pp = gu_ps[k % 2]
                v.wait_ge(AC.h, fl[k])
                DV.inc(v.scalar_tensor_tensor(
                    out=h_sb[:, poff:poff + pw], in0=pp[:, 512:512 + pw],
                    scalar=1.0, in1=silu_sb[:, poff:poff + pw],
                    op0=ALU.mult, op1=ALU.mult))
                gu_dv[k % 2] = DV.n

            def quant_hf(hf):
                v.wait_ge(PE.h, tr_pe[1 - hf])
                emit_quant(nc, DV, AC, h_sb[:], hq[:], IH, qsc2, war_dv=DV.n)

            def hq_transposes(hf):
                t.wait_ge(DV.h, DV.n)
                for gi in range(2):
                    r0, r1 = gi * 8, min(gi * 8 + 8, 11)
                    bank = gi % 2
                    for r in range(r0, r1):
                        PE.inc(t.transpose(
                            tps2[:, bank * 1024 + (r - r0) * 128:
                                 bank * 1024 + (r - r0 + 1) * 128],
                            hq[:, r * 128:(r + 1) * 128], ident[:]))
                    v.wait_ge(PE.h, PE.n)
                    DV.inc(v.tensor_copy(
                        out=hqT[:, hf * IH + r0 * 128:hf * IH + r1 * 128],
                        in_=tps2[:, bank * 1024:
                                 bank * 1024 + (r1 - r0) * 128]))
                tr_pe[hf] = PE.n

            for k in range(6):
                if k >= 1:
                    mq_piece(k - 1)
                if k == 3:
                    quant_hf(0)
                gu_piece(k)
                if k == 3:
                    if c.get("dbg"):
                        g.wait_ge(DV.h, DV.n)
                        dma(g, GD, c["dbg_hq"][tt * 128:(tt + 1) * 128,
                                               0:IH], hq[:])
                    hq_transposes(0)
            mq_piece(5)
            quant_hf(1)
            if c.get("dbg"):
                g.wait_ge(DV.h, DV.n)
                dma(g, GD, c["dbg_hq"][tt * 128:(tt + 1) * 128,
                                       IH:2 * IH], hq[:])
            dve_flush()
            hq_transposes(1)
            hqT_dv = DV.n
            dvq_next = None
            if tt + 1 < NT:
                dvq_next = quant_x1(tt + 1)
            t.wait_ge(DV.h, hqT_dv)
            for ic in range(NIC):
                for p in range(2):
                    PE.inc(t.matmul(
                        dps[:, p * 512:(p + 1) * 512],
                        hqT[:, ic * 128:(ic + 1) * 128],
                        wdT[:, ic * 1024 + p * 512:ic * 1024 + (p + 1) * 512],
                        start=(ic == 0), stop=(ic == NIC - 1)))
            down_pe = PE.n
            if dvq_next is not None:
                transpose_xq(tt + 1, dvq_next)
            v.wait_ge(PE.h, down_pe)
            DV.inc(v.tensor_add(out=ub2[:, 0:D], in0=dps[:],
                                in1=x1l[par][:]))
            if c.get("dbg"):
                g.wait_ge(DV.h, DV.n)
                dma(g, GD, c["dbg_u"][tt * 128:(tt + 1) * 128, :],
                    ub2[:, 0:D])
                v.wait_ge(GD.h, GD.n)
            if tt + 2 < NT:
                sy.wait_ge(DV.h, DV.n)
                sy.wait_ge(AC.h, AC.n)
                x1_ld[tt + 2] = dma(sy, LD, x1l[par][:],
                                    c["x1sp"][(tt + 2) * 128:
                                              (tt + 3) * 128, :])
            emit_ln(nc, DV, AC, ub2[:, 0:D], x2b[:, 0:D], D, EPS_LN,
                    ln_bufs2)
            v.wait_ge(AC.h, AC.n)
            if tt >= 1:
                v.wait_ge(GD.h, out_gd[0])
            DV.inc(v.tensor_copy(out=x2c[:], in_=x2b[:, 0:D]))
            g.wait_ge(DV.h, DV.n)
            out_gd[0] = dma(g, GD, c["out_d"][tt * 128:(tt + 1) * 128, :],
                            x2c[:])
        g.wait_ge(GD.h, GD.n)
        sy.wait_ge(GD.h, GD.n)


def build_nc(stop_after=None, p0_quant_only=False, dbg=False):
    nc = bass.Bass("TRN2", target_bir_lowering=False)
    c = {"stop_after": stop_after, "p0_quant_only": p0_quant_only, "dbg": dbg}
    c["x_d"] = nc.dram_tensor("x", [S, D], F32, kind="ExternalInput").ap()
    c["xT_d"] = nc.dram_tensor("xT", [D, S], F32, kind="ExternalInput").ap()
    c["wq_d"] = nc.dram_tensor("wq", [D, HK], F32, kind="ExternalInput").ap()
    c["wk_d"] = nc.dram_tensor("wk", [D, HK], F32, kind="ExternalInput").ap()
    c["wv_d"] = nc.dram_tensor("wv", [D, HK], F32, kind="ExternalInput").ap()
    c["bq_d"] = nc.dram_tensor("bq", [128, 8], F32, kind="ExternalInput").ap()
    c["bk_d"] = nc.dram_tensor("bk", [128, 8], F32, kind="ExternalInput").ap()
    c["bv_d"] = nc.dram_tensor("bv", [1, HK], F32, kind="ExternalInput").ap()
    c["wo_d"] = nc.dram_tensor("wo", [HK, D], F32, kind="ExternalInput").ap()
    c["bo_d"] = nc.dram_tensor("bo", [1, D], F32, kind="ExternalInput").ap()
    c["gam_d"] = nc.dram_tensor("gam", [1, D], F32, kind="ExternalInput").ap()
    c["bet_d"] = nc.dram_tensor("bet", [1, D], F32, kind="ExternalInput").ap()
    c["wg_d"] = nc.dram_tensor("wg", [I, D], F32, kind="ExternalInput").ap()
    c["wu_d"] = nc.dram_tensor("wu", [I, D], F32, kind="ExternalInput").ap()
    c["wd_d"] = nc.dram_tensor("wd", [D, I], F32, kind="ExternalInput").ap()
    c["out_d"] = nc.dram_tensor("out", [S, D], F32, kind="ExternalOutput").ap()
    c["x1sp"] = nc.dram_tensor("x1sp", [S, D], F32, kind="Internal").ap()
    c["wgT_s"] = nc.dram_tensor("wgT_s", [D, I], BF16, kind="Internal").ap()
    c["wuT_s"] = nc.dram_tensor("wuT_s", [D, I], BF16, kind="Internal").ap()
    c["wdT_s"] = nc.dram_tensor("wdT_s", [I, D], BF16, kind="Internal").ap()
    c["t16"] = nc.dram_tensor("t16", [1, 16], F32, kind="Internal").ap()
    c["ctx_s"] = nc.dram_tensor("ctx_s", [HK, S], BF16, kind="Internal").ap()
    if c.get("dbg"):
        c["dbg_qh"] = nc.dram_tensor("dbg_qh", [128, NDC * S], BF16,
                                     kind="ExternalOutput").ap()
        c["dbg_kh"] = nc.dram_tensor("dbg_kh", [128, NDC * S], BF16,
                                     kind="ExternalOutput").ap()
        c["dbg_vsb"] = nc.dram_tensor("dbg_vsb", [128, NT * NH * 65], BF16,
                                      kind="ExternalOutput").ap()
        c["dbg_mh"] = nc.dram_tensor("dbg_mh", [128, 16], F32,
                                     kind="ExternalOutput").ap()
        c["dbg_m2"] = nc.dram_tensor("dbg_m2", [1, 16], BF16,
                                     kind="ExternalOutput").ap()
        c["dbg_qn"] = nc.dram_tensor("dbg_qn", [16, 1], F32,
                                     kind="ExternalOutput").ap()
        c["dbg_kn"] = nc.dram_tensor("dbg_kn", [16, 1], F32,
                                     kind="ExternalOutput").ap()
        c["dbg_pn"] = nc.dram_tensor("dbg_pn", [16, 2048], F32,
                                     kind="ExternalOutput").ap()
        c["dbg_m2a"] = nc.dram_tensor("dbg_m2a", [16, 1], F32,
                                      kind="ExternalOutput").ap()
        c["dbg_msq"] = nc.dram_tensor("dbg_msq", [16, 1], F32,
                                      kind="ExternalOutput").ap()
        c["dbg_m2b"] = nc.dram_tensor("dbg_m2b", [16, 1], F32,
                                      kind="ExternalOutput").ap()
        c["dbg_mrow"] = nc.dram_tensor("dbg_mrow", [1, 16], F32,
                                       kind="ExternalOutput").ap()
        c["dbg_ctxn"] = nc.dram_tensor("dbg_ctxn", [128, NDC * S], BF16,
                                       kind="ExternalOutput").ap()
        c["dbg_x1"] = nc.dram_tensor("dbg_x1", [S, D], F32,
                                     kind="ExternalOutput").ap()
        c["dbg_y"] = nc.dram_tensor("dbg_y", [S, D], F32,
                                    kind="ExternalOutput").ap()
        c["dbg_z"] = nc.dram_tensor("dbg_z", [S, D], F32,
                                    kind="ExternalOutput").ap()
        c["dbg_xq"] = nc.dram_tensor("dbg_xq", [S, D], BF16,
                                     kind="ExternalOutput").ap()
        c["dbg_hq"] = nc.dram_tensor("dbg_hq", [S, I], BF16,
                                     kind="ExternalOutput").ap()
        c["dbg_u"] = nc.dram_tensor("dbg_u", [S, D], F32,
                                    kind="ExternalOutput").ap()
        c["dbg_x1l"] = nc.dram_tensor("dbg_x1l", [S, D], F32,
                                      kind="ExternalOutput").ap()

    with ExitStack() as top:
        e = top.enter_context
        c["ident"] = e(nc.sbuf_tensor("ident", [128, 128], BF16))
        c["ones_f"] = e(nc.sbuf_tensor("ones_f", [1, 128], F32))
        c["ones_b"] = e(nc.sbuf_tensor("ones_b", [1, 128], BF16))
        c["Mh"] = e(nc.sbuf_tensor("Mh", [128, 16], F32))
        c["bqc"] = e(nc.sbuf_tensor("bqc", [128, 8], F32))
        c["bkc"] = e(nc.sbuf_tensor("bkc", [128, 8], F32))
        c["bv_b"] = e(nc.sbuf_tensor("bv_b", [1, HK], BF16))
        c["bo_b"] = e(nc.sbuf_tensor("bo_b", [1, D], BF16))
        c["rowtmp_bo"] = e(nc.sbuf_tensor("rowtmp_bo", [1, D], F32))
        c["junkf"] = e(nc.sbuf_tensor("junkf", [128, 32], F32))
        c["junkv"] = e(nc.sbuf_tensor("junkv", [128, 64], F32))
        c["LD"] = Sem(e(nc.semaphore("sLD")))
        c["GD"] = Sem(e(nc.semaphore("sGD")))
        c["PE"] = Sem(e(nc.semaphore("sPE")))
        c["AC"] = Sem(e(nc.semaphore("sAC")))
        c["DV"] = Sem(e(nc.semaphore("sDV")))
        c["GP"] = Sem(e(nc.semaphore("sGP")))
        block = e(nc.Block())

        @block.vector
        def _(_v):
            emit_all(nc, c)
    return nc


_NC_CACHE = {}


def kernel(L0_attn_x, L0_attn_Wq, L0_attn_bq, L0_attn_Wk, L0_attn_bk,
           L0_attn_Wv, L0_attn_bv, L0_attn_Wo, L0_attn_bo, L0_attn_gamma,
           L0_attn_beta, L0_ffn_w_gate, L0_ffn_w_up, L0_ffn_w_down):
    x = np.ascontiguousarray(np.asarray(L0_attn_x, dtype=np.float32))
    B = x.shape[0]
    wq = np.ascontiguousarray(np.asarray(L0_attn_Wq, np.float32).reshape(D, HK))
    wk = np.ascontiguousarray(np.asarray(L0_attn_Wk, np.float32).reshape(D, HK))
    wv = np.ascontiguousarray(np.asarray(L0_attn_Wv, np.float32).reshape(D, HK))
    bq = np.ascontiguousarray(np.asarray(L0_attn_bq, np.float32).reshape(8, 128).T)
    bk = np.ascontiguousarray(np.asarray(L0_attn_bk, np.float32).reshape(8, 128).T)
    bv = np.ascontiguousarray(np.asarray(L0_attn_bv, np.float32).reshape(1, HK))
    wo = np.ascontiguousarray(np.asarray(L0_attn_Wo, np.float32).reshape(HK, D))
    bo = np.ascontiguousarray(np.asarray(L0_attn_bo, np.float32).reshape(1, D))
    gam = np.ascontiguousarray(np.asarray(L0_attn_gamma, np.float32).reshape(1, D))
    bet = np.ascontiguousarray(np.asarray(L0_attn_beta, np.float32).reshape(1, D))
    wg = np.ascontiguousarray(np.asarray(L0_ffn_w_gate, np.float32))
    wu = np.ascontiguousarray(np.asarray(L0_ffn_w_up, np.float32))
    wd = np.ascontiguousarray(np.asarray(L0_ffn_w_down, np.float32))

    if "nc" not in _NC_CACHE:
        _NC_CACHE["nc"] = build_nc()
    nc = _NC_CACHE["nc"]
    in_maps = []
    for b in range(B):
        xb = np.ascontiguousarray(x[b])
        in_maps.append({
            "x": xb, "xT": np.ascontiguousarray(xb.T),
            "wq": wq, "wk": wk, "wv": wv, "bq": bq, "bk": bk, "bv": bv,
            "wo": wo, "bo": bo, "gam": gam, "bet": bet,
            "wg": wg, "wu": wu, "wd": wd,
        })
    res = run_bass_kernel_spmd(nc, in_maps, core_ids=list(range(B)))
    return np.stack([np.asarray(r["out"], dtype=np.float32)
                     for r in res.results], axis=0)


# version 15
# speedup vs baseline: 1.0076x; 1.0076x over previous
import numpy as np
from contextlib import ExitStack

import concourse.bass as bass
import concourse.mybir as mybir
from concourse.bass_utils import run_bass_kernel_spmd

F32 = mybir.dt.float32
BF16 = mybir.dt.bfloat16
AF = mybir.ActivationFunctionType
ALU = mybir.AluOpType
AX = mybir.AxisListType.X

S = 2048
D = 1024
HK = 1024
NH = 16
I = 2816
NT = 16
NDC = 8
NIC = 22
EPS_LN1 = 1e-3
EPS_LN = 1e-5
IH = 1408


class Sem:
    def __init__(self, h):
        self.h = h
        self.n = 0

    def inc(self, inst, k=1):
        inst.then_inc(self.h, k)
        self.n += k
        return self.n


def _pieces(width, step=512):
    out, off = [], 0
    while off < width:
        w = min(step, width - off)
        out.append((off, w))
        off += w
    return out


def emit_quant(nc, DV, AC, x_ap, out_ap, W, sc, war_dv=None):
    G = W // 16
    v = nc.vector
    M, Sb, R = sc["M"], sc["S"], sc["R"]
    absx, a, c1, c2, sg = sc["absx"], sc["a"], sc["c1"], sc["c2"], sc["sg"]
    nc.scalar.wait_ge(DV.h, DV.n if war_dv is None else war_dv)
    AC.inc(nc.scalar.sign(out=sg[:, 0:W], in_=x_ap))
    ac_sign = AC.n
    AC.inc(nc.scalar.activation(out=absx[:, 0:W], in_=x_ap, func=AF.Abs))
    ac_absx = AC.n
    xv = x_ap.rearrange("p (g k) -> p g k", k=16)
    DV.inc(v.tensor_reduce(out=M[:, 0:G], in_=xv, axis=AX, op=ALU.max,
                           apply_absolute_value=True))
    v.wait_ge(DV.h, DV.n)
    DV.inc(v.tensor_scalar(out=Sb[:, 0:G], in0=M[:, 0:G], scalar1=1.0 / 6.0,
                           scalar2=1e-12, op0=ALU.mult, op1=ALU.max))
    v.wait_ge(DV.h, DV.n)
    DV.inc(v.reciprocal(out=R[:, 0:G], in_=Sb[:, 0:G]))
    v.wait_ge(DV.h, DV.n)
    v.wait_ge(AC.h, ac_absx)
    Rb = R[:, 0:G].unsqueeze(2).broadcast_to([128, G, 16])
    Sbb = Sb[:, 0:G].unsqueeze(2).broadcast_to([128, G, 16])
    v.tensor_tensor(out=a[:, 0:W].rearrange("p (g k) -> p g k", k=16),
                    in0=absx[:, 0:W].rearrange("p (g k) -> p g k", k=16),
                    in1=Rb, op=ALU.mult)
    v.tensor_scalar(out=c1[:, 0:W], in0=a[:, 0:W], scalar1=0.25, scalar2=None,
                    op0=ALU.is_gt)
    v.scalar_tensor_tensor(out=c2[:, 0:W], in0=a[:, 0:W], scalar=0.75,
                           in1=c1[:, 0:W], op0=ALU.is_gt, op1=ALU.add)
    v.scalar_tensor_tensor(out=c1[:, 0:W], in0=a[:, 0:W], scalar=1.25,
                           in1=c2[:, 0:W], op0=ALU.is_gt, op1=ALU.add)
    v.scalar_tensor_tensor(out=c2[:, 0:W], in0=a[:, 0:W], scalar=1.75,
                           in1=c1[:, 0:W], op0=ALU.is_gt, op1=ALU.add)
    v.tensor_scalar(out=c1[:, 0:W], in0=a[:, 0:W], scalar1=5.0, scalar2=2.0,
                    op0=ALU.is_gt, op1=ALU.mult)
    v.scalar_tensor_tensor(out=absx[:, 0:W], in0=a[:, 0:W], scalar=3.5,
                           in1=c1[:, 0:W], op0=ALU.is_gt, op1=ALU.add)
    v.scalar_tensor_tensor(out=c1[:, 0:W], in0=a[:, 0:W], scalar=2.5,
                           in1=absx[:, 0:W], op0=ALU.is_gt, op1=ALU.add)
    v.scalar_tensor_tensor(out=absx[:, 0:W], in0=c2[:, 0:W], scalar=0.5,
                           in1=c1[:, 0:W], op0=ALU.mult, op1=ALU.add)
    v.scalar_tensor_tensor(out=a[:, 0:W].rearrange("p (g k) -> p g k", k=16),
                           in0=absx[:, 0:W].rearrange("p (g k) -> p g k", k=16),
                           scalar=1.0, in1=Sbb, op0=ALU.mult, op1=ALU.mult)
    v.wait_ge(AC.h, ac_sign)
    DV.inc(v.tensor_mul(out=out_ap, in0=a[:, 0:W], in1=sg[:, 0:W]))
    return DV.n, ac_sign


def emit_ln(nc, DV, AC, y_ap, z_out, W, eps, B):
    v, s = nc.vector, nc.scalar
    s.wait_ge(DV.h, DV.n)
    nst = W // 512
    for j in range(nst):
        DV.inc(v.bn_stats(out=B["st6"][:, j * 6:(j + 1) * 6],
                          in_=y_ap[:, j * 512:(j + 1) * 512]))
    v.wait_ge(DV.h, DV.n)
    DV.inc(v.bn_aggr(out=B["ag"][:, 0:2], in_=B["st6"][:, 0:6 * nst]))
    v.wait_ge(DV.h, DV.n)
    DV.inc(v.tensor_scalar(out=B["t0"][:], in0=B["ag"][:, 1:2], scalar1=eps,
                           scalar2=None, op0=ALU.add))
    v.wait_ge(DV.h, DV.n)
    DV.inc(v.reciprocal(out=B["t1"][:], in_=B["t0"][:]))
    v.wait_ge(DV.h, DV.n)
    DV.inc(v.tensor_scalar(out=B["mu_neg"][:], in0=B["ag"][:, 0:1],
                           scalar1=-1.0, scalar2=None, op0=ALU.mult))
    DV.inc(v.tensor_copy(out=B["junk"][:], in_=B["t1"][:]))
    DV.inc(v.tensor_copy(out=B["junk"][:], in_=B["mu_neg"][:]))
    s.wait_ge(DV.h, DV.n)
    AC.inc(s.activation(out=B["t0"][:], in_=B["t1"][:], func=AF.Sqrt))
    s.wait_ge(AC.h, AC.n)
    AC.inc(s.activation(out=B["t1"][:], in_=B["mu_neg"][:], func=AF.Identity,
                        scale=B["t0"][:, 0:1]))
    s.wait_ge(AC.h, AC.n)
    AC.inc(s.activation(out=z_out, in_=y_ap, func=AF.Identity,
                        scale=B["t0"][:, 0:1], bias=B["t1"][:, 0:1]))
    AC.inc(s.activation(out=B["fj"][:], in_=B["fj"][:], func=AF.Square))
    return AC.n


class P0Stream:

    def __init__(self, nc, c, stk, LD, GD, PE, AC, DV):
        e = stk.enter_context
        self.nc, self.c = nc, c
        self.LD, self.GD, self.PE, self.AC, self.DV = LD, GD, PE, AC, DV
        self.raw = [e(nc.sbuf_tensor(f"p0raw{i}", [128, 1024], F32))
                    for i in range(2)]
        self.qsc = {k: e(nc.sbuf_tensor(
            "p0q" + k, [128, 64 if k in ("M", "S", "R") else 1024],
            BF16 if k == "sg" else F32))
            for k in ("M", "S", "R", "absx", "a", "c1", "c2", "sg")}
        self.wbf = [e(nc.sbuf_tensor(f"p0wbf{i}", [128, 1024], BF16))
                    for i in range(2)]
        self.stg = e(nc.sbuf_tensor("p0stg", [128, 1024], BF16))
        self.tps = e(nc.psum_tensor("p0tps", [128, 2048], BF16))
        self.units = []
        for w_d, wT_s in ((c["wg_d"], c["wgT_s"]), (c["wu_d"], c["wuT_s"])):
            for r in range(I // 128):
                self.units.append((
                    w_d[r * 128:(r + 1) * 128, :], 8,
                    wT_s[0:1024, r * 128:(r + 1) * 128]
                    .rearrange("(j a) b -> a j b", j=8)))
        for r in range(8):
            for j0, j1 in ((0, 8), (8, 16), (16, 22)):
                self.units.append((
                    c["wd_d"][r * 128:(r + 1) * 128, j0 * 128:j1 * 128],
                    j1 - j0,
                    c["wdT_s"][j0 * 128:j1 * 128, r * 128:(r + 1) * 128]
                    .rearrange("(j a) b -> a j b", j=j1 - j0)))
        self.n_units = len(self.units)
        self.k = 0
        self.ld = [0] * self.n_units
        self.dvv = [0] * self.n_units
        self.acv = [0] * self.n_units
        self.pe_tr = [0, 0]
        self.evac_dv = [0, 0]
        self.gd = [0, 0]
        sy = nc.sync
        for i in range(min(2, self.n_units)):
            src, nblk, _ = self.units[i]
            sy.dma_start(out=self.raw[i % 2][:, 0:nblk * 128],
                         in_=src).then_inc(LD.h, 16)
            LD.n += 16
            self.ld[i] = LD.n

    @property
    def done(self):
        return self.k > self.n_units

    def step(self):
        if self.done:
            return
        nc = self.nc
        v, t, g, sy = nc.vector, nc.tensor, nc.gpsimd, nc.sync
        s = nc.scalar
        LD, GD, PE, AC, DV = self.LD, self.GD, self.PE, self.AC, self.DV
        k = self.k
        if k < self.n_units:
            src, nblk, _ = self.units[k]
            w = nblk * 128
            v.wait_ge(LD.h, self.ld[k])
            s.wait_ge(LD.h, self.ld[k])
            v.wait_ge(PE.h, self.pe_tr[k % 2])
            dvv, acv = emit_quant(nc, DV, AC, self.raw[k % 2][:, 0:w],
                                  self.wbf[k % 2][:, 0:w], w, self.qsc,
                                  war_dv=self.dvv[k - 1] if k else 0)
            self.dvv[k], self.acv[k] = dvv, acv
            if k + 2 < self.n_units:
                src2, nblk2, _ = self.units[k + 2]
                sy.wait_ge(DV.h, dvv)
                sy.wait_ge(AC.h, acv)
                sy.dma_start(out=self.raw[k % 2][:, 0:nblk2 * 128],
                             in_=src2).then_inc(LD.h, 16)
                LD.n += 16
                self.ld[k + 2] = LD.n
        if k >= 1:
            j = k - 1
            _, nblk, out_ap = self.units[j]
            w = nblk * 128
            par = j % 2
            t.wait_ge(DV.h, self.dvv[j])
            for b in range(nblk):
                PE.inc(t.transpose(
                    self.tps[:, par * 1024 + b * 128:par * 1024 + (b + 1) * 128],
                    self.wbf[par][:, b * 128:(b + 1) * 128],
                    self.c["ident"][:]))
            self.pe_tr[par] = PE.n
            v.wait_ge(PE.h, self.pe_tr[par])
            v.wait_ge(GD.h, self.gd[0])
            DV.inc(v.tensor_copy(out=self.stg[:, 0:w],
                                 in_=self.tps[:, par * 1024:par * 1024 + w]))
            self.evac_dv[par] = DV.n
            g.wait_ge(DV.h, self.evac_dv[par])
            g.dma_start(out=out_ap,
                        in_=self.stg[:, 0:w]
                        .rearrange("p (j b) -> p j b", j=nblk)
                        ).then_inc(GD.h, 16)
            GD.n += 16
            self.gd[0] = GD.n
        self.k += 1


def emit_all(nc, c):
    stop_after = c.get("stop_after")
    LD, GD, PE, AC, DV, GP = (c["LD"], c["GD"], c["PE"], c["AC"], c["DV"],
                              c["GP"])
    v, s, t, g, sy = nc.vector, nc.scalar, nc.tensor, nc.gpsimd, nc.sync
    ident, ones_f, ones_b, Mh = c["ident"], c["ones_f"], c["ones_b"], c["Mh"]

    def barrier():
        PE.inc(t.nop())
        AC.inc(s.nop())
        DV.inc(v.nop())
        GP.inc(g.nop())
        vals = [(PE, PE.n), (AC, AC.n), (DV, DV.n), (GP, GP.n),
                (LD, LD.n), (GD, GD.n)]
        own = {id(t): PE, id(s): AC, id(v): DV, id(g): GP}
        for eng in (t, s, v, g, sy):
            mine = own.get(id(eng))
            for sem, val in vals:
                if sem is mine:
                    continue
                eng.wait_ge(sem.h, val)

    def act_flush():
        AC.inc(s.activation(out=c["junkf"][:], in_=c["junkf"][:],
                            func=AF.Square))
        return AC.n

    def dve_flush():
        DV.inc(v.tensor_copy(out=c["junkv"][:], in_=c["junkv"][:]))
        return DV.n

    def dma(eng, sem, out, in_):
        eng.dma_start(out=out, in_=in_).then_inc(sem.h, 16)
        sem.n += 16
        return sem.n

    g.memset(ident[:], 0.0)
    g.affine_select(out=ident[:], in_=ident[:], compare_op=ALU.not_equal,
                    fill=1.0, base=0, pattern=[[-1, 128]], channel_multiplier=1)
    g.memset(ones_f[:], 1.0)
    g.memset(c["junkf"][:], 1.0)
    g.memset(c["junkv"][:], 1.0)
    GP.inc(g.memset(ones_b[:], 1.0))
    dma(sy, LD, c["bqc"][:], c["bq_d"][:])
    dma(sy, LD, c["bkc"][:], c["bk_d"][:])

    if stop_after == "INIT":
        barrier()
        return
    p0stk = ExitStack()
    p0 = P0Stream(nc, c, p0stk, LD, GD, PE, AC, DV)
    if stop_after in ("P0", "P0w1"):
        while not p0.done:
            p0.step()
        barrier()
        p0stk.close()
        return

    with ExitStack() as p2:
        e = p2.enter_context
        gb = e(nc.sbuf_tensor("gb", [128, D], F32))
        bb = e(nc.sbuf_tensor("bb", [128, D], F32))
        pqkv = e(ExitStack())
        eq = pqkv.enter_context
        qh = eq(nc.sbuf_tensor("qh", [128, NDC * S], BF16))
        kh = eq(nc.sbuf_tensor("kh", [128, NDC * S], BF16))
        vsb = eq(nc.sbuf_tensor("vsb", [128, NT * NH * 65], BF16))
        mh_ready = 0

        with ExitStack() as p2a:
            e2 = p2a.enter_context
            qn = e2(nc.sbuf_tensor("qn", [128, 1], F32))
            kn = e2(nc.sbuf_tensor("kn", [128, 1], F32))
            m2 = e2(nc.sbuf_tensor("m2", [128, 1], F32))
            msq = e2(nc.sbuf_tensor("msq", [128, 1], F32))
            mhe = e2(nc.sbuf_tensor("mhe", [128, 16], F32))
            xT_b = e2(nc.sbuf_tensor("xT_b", [128, NDC * S], BF16))
            wsl = e2(nc.sbuf_tensor("wsl", [128, 8 * 1024], BF16))
            xTf = e2(nc.sbuf_tensor("xTf", [128, 1024], F32))
            q2 = e2(nc.sbuf_tensor("q2", [128, 1024], BF16))
            masks = e2(nc.sbuf_tensor("masks", [128, 8 * 128], BF16))
            dma(g, GD, c["bv_b"][:], c["bv_d"][:])
            dma(g, GD, c["bo_b"][:], c["bo_d"][:])

            ldg = dma(sy, LD, xTf[0:1, :], c["gam_d"][:])
            with nc.psum_tensor("pgb", [128, 1024], F32) as pgb:
                t.wait_ge(LD.h, ldg)
                t.wait_ge(GP.h, GP.n)
                for p in range(2):
                    PE.inc(t.matmul(pgb[:, p * 512:(p + 1) * 512], ones_f[:],
                                    xTf[0:1, p * 512:(p + 1) * 512],
                                    start=True, stop=True))
                v.wait_ge(PE.h, PE.n)
                DV.inc(v.tensor_copy(out=gb[:], in_=pgb[:]))
                sy.wait_ge(PE.h, PE.n)
                dma(sy, LD, xTf[0:1, :], c["bet_d"][:])
                ldb = dma(sy, LD, c["rowtmp_bo"][:], c["bo_d"][:])
                t.wait_ge(DV.h, DV.n)
                t.wait_ge(LD.h, ldb)
                for p in range(2):
                    PE.inc(t.matmul(pgb[:, p * 512:(p + 1) * 512], ones_f[:],
                                    xTf[0:1, p * 512:(p + 1) * 512],
                                    start=True, stop=True))
                v.wait_ge(PE.h, PE.n)
                DV.inc(v.tensor_copy(out=bb[:], in_=pgb[:]))
                gbb_pe = PE.n

            g.memset(masks[:], 0.0)
            for cc in range(8):
                g.memset(masks[0:64, cc * 128 + 2 * cc:
                               cc * 128 + 2 * cc + 1], 1.0)
                g.memset(masks[64:128, cc * 128 + 2 * cc + 1:
                               cc * 128 + 2 * cc + 2], 1.0)
            vv = vsb[:].rearrange("p (sc h w) -> p sc h w", h=NH, w=65)
            GP.inc(g.memset(vv[:, :, :, 64:65], 1.0))

            sy.wait_ge(PE.h, gbb_pe)
            for dcc in range(NDC):
                for hf2 in range(2):
                    if dcc + hf2 >= 1:
                        sy.wait_ge(AC.h, AC.n)
                    ldv = dma(sy, LD, xTf[:],
                              c["xT_d"][dcc * 128:(dcc + 1) * 128,
                                        hf2 * 1024:(hf2 + 1) * 1024])
                    s.wait_ge(LD.h, ldv)
                    AC.inc(s.activation(
                        out=xT_b[:, dcc * S + hf2 * 1024:
                                 dcc * S + (hf2 + 1) * 1024],
                        in_=xTf[:], func=AF.Copy))
            for dcc in range(NDC):
                dma(g, GD, wsl[:, dcc * 1024:(dcc + 1) * 1024],
                    c["wq_d"][dcc * 128:(dcc + 1) * 128, :])

            t.wait_ge(AC.h, AC.n)
            t.wait_ge(GD.h, GD.n)
            v.wait_ge(LD.h, LD.n)
            qk_dv = 0
            with nc.psum_tensor("pq", [128, 2048], F32) as pq:
                for wi, (dst, bias) in enumerate(((qh, c["bqc"]),
                                                  (kh, c["bkc"]))):
                    if wi == 1:
                        g.wait_ge(PE.h, PE.n)
                        for dcc in range(NDC):
                            dma(g, GD, wsl[:, dcc * 1024:(dcc + 1) * 1024],
                                c["wk_d"][dcc * 128:(dcc + 1) * 128, :])
                        t.wait_ge(GD.h, GD.n)
                    for cc in range(NDC):
                        if wi > 0 or cc > 0:
                            t.wait_ge(DV.h, qk_dv)
                        for dcc in range(NDC):
                            for p in range(4):
                                PE.inc(t.matmul(
                                    pq[:, p * 512:(p + 1) * 512],
                                    wsl[:, dcc * 1024 + cc * 128:
                                        dcc * 1024 + cc * 128 + 128],
                                    xT_b[:, dcc * S + p * 512:
                                         dcc * S + (p + 1) * 512],
                                    start=(dcc == 0), stop=(dcc == NDC - 1)))
                        v.wait_ge(PE.h, PE.n)
                        for p in range(4):
                            DV.inc(v.tensor_scalar(
                                out=dst[:, cc * S + p * 512:
                                        cc * S + (p + 1) * 512],
                                in0=pq[:, p * 512:(p + 1) * 512],
                                scalar1=bias[:, cc:cc + 1], scalar2=None,
                                op0=ALU.add))
                        qk_dv = DV.n
                        p0.step()

            g.wait_ge(PE.h, PE.n)
            for dcc in range(NDC):
                dma(g, GD, wsl[:, dcc * 1024:(dcc + 1) * 1024],
                    c["wv_d"][dcc * 128:(dcc + 1) * 128, :])
            wv_gd = GD.n

            with ExitStack() as pnx:
                pn = pnx.enter_context(nc.psum_tensor("pn", [128, 2048], F32))
                pm = pnx.enter_context(nc.psum_tensor("pm", [128, 16], F32))
                pmt = pnx.enter_context(nc.psum_tensor("pmt", [1, 128], BF16))
                mb = pnx.enter_context(nc.sbuf_tensor("mb", [128, 1], BF16))
                mrow_b = pnx.enter_context(nc.sbuf_tensor("mrow_b", [1, 128], BF16))
                for src, dstn in ((qh, qn), (kh, kn)):
                    for cc in range(NDC):
                        for h2 in range(2):
                            if cc + h2 > 0:
                                v.wait_ge(PE.h, PE.n)
                            DV.inc(v.tensor_mul(
                                out=q2[:],
                                in0=src[:, cc * S + h2 * 1024:
                                        cc * S + (h2 + 1) * 1024],
                                in1=src[:, cc * S + h2 * 1024:
                                        cc * S + (h2 + 1) * 1024]))
                            t.wait_ge(DV.h, DV.n)
                            for p in range(2):
                                PE.inc(t.matmul(
                                    pn[:, h2 * 1024 + p * 512:
                                       h2 * 1024 + (p + 1) * 512],
                                    masks[:, cc * 128:(cc + 1) * 128],
                                    q2[:, p * 512:(p + 1) * 512],
                                    start=(cc == 0),
                                    stop=(cc == NDC - 1)))
                    v.wait_ge(PE.h, PE.n)
                    DV.inc(v.tensor_reduce(out=dstn[:], in_=pn[:],
                                           axis=AX, op=ALU.max))
                    v.wait_ge(DV.h, DV.n)
                if c.get("dbg"):
                    g.wait_ge(DV.h, DV.n)
                    dma(g, GD, c["dbg_qn"][:], qn[0:16, 0:1])
                    dma(g, GD, c["dbg_kn"][:], kn[0:16, 0:1])
                DV.inc(v.tensor_add(out=msq[:], in0=qn[:], in1=kn[:]))
                v.wait_ge(DV.h, DV.n)
                DV.inc(v.tensor_scalar(out=m2[:], in0=msq[:],
                                       scalar1=-0.0625, scalar2=None,
                                       op0=ALU.mult))
                v.wait_ge(DV.h, DV.n)
                v.tensor_copy(out=mb[:], in_=m2[:])
                DV.inc(v.tensor_copy(out=q2[:, 0:1024], in_=qh[:, 0:1024]))
                t.wait_ge(DV.h, DV.n)
                PE.inc(t.transpose(pmt[0:1, 0:128], mb[:, 0:1],
                                   ident[:]))
                v.wait_ge(PE.h, PE.n)
                v.tensor_copy(out=mrow_b[:], in_=pmt[0:1, 0:128])
                DV.inc(v.tensor_copy(out=q2[:, 0:1024], in_=qh[:, 0:1024]))
                t.wait_ge(DV.h, DV.n)
                PE.inc(t.matmul(pm[:, 0:16], ones_b[:], mrow_b[0:1, 0:16],
                                start=True, stop=True))
                if c.get("dbg"):
                    g.wait_ge(DV.h, DV.n)
                    dma(g, GD, c["dbg_m2"][:], mrow_b[0:1, 0:16])
                v.wait_ge(PE.h, PE.n)
                DV.inc(v.tensor_copy(out=Mh[:], in_=pm[:]))
                DV.inc(v.tensor_copy(out=mhe[:], in_=Mh[:]))
                mh_ready = DV.n

            t.wait_ge(GD.h, wv_gd)
            vh_dv = 0
            with nc.psum_tensor("pv", [128, 1024], F32) as pv:
                for sc in range(NT):
                    t.wait_ge(DV.h, vh_dv)
                    for dcc in range(NDC):
                        for p in range(2):
                            PE.inc(t.matmul(
                                pv[:, p * 512:(p + 1) * 512],
                                xT_b[:, dcc * S + sc * 128:
                                     dcc * S + sc * 128 + 128],
                                wsl[:, dcc * 1024 + p * 512:
                                    dcc * 1024 + (p + 1) * 512],
                                start=(dcc == 0), stop=False))
                    for p in range(2):
                        PE.inc(t.matmul(pv[:, p * 512:(p + 1) * 512],
                                        ones_b[:],
                                        c["bv_b"][0:1, p * 512:(p + 1) * 512],
                                        start=False, stop=True))
                    v.wait_ge(PE.h, PE.n)
                    v.wait_ge(GP.h, GP.n)
                    DV.inc(v.tensor_copy(
                        out=vsb[:, sc * NH * 65:(sc + 1) * NH * 65]
                        .rearrange("p (h w) -> p h w", w=65)[:, :, 0:64],
                        in_=pv[:].rearrange("p (h w) -> p h w", w=64)))
                    vh_dv = DV.n
                    if sc % 2 == 0:
                        p0.step()
            barrier()
            if stop_after == "P2":
                return

        if c.get("dbg"):
            dma(g, GD, c["dbg_qh"][:], qh[:])
            dma(g, GD, c["dbg_kh"][:], kh[:])
            dma(g, GD, c["dbg_vsb"][:], vsb[:])
            dma(g, GD, c["dbg_mh"][:], Mh[:])
            g.wait_ge(GD.h, GD.n)

        with ExitStack() as p34:
            p3 = p34.enter_context(ExitStack())
            e3 = p3.enter_context
            pT = e3(nc.sbuf_tensor("pT", [128, 2 * 2048], BF16))
            cpair = [e3(nc.sbuf_tensor(f"cpair{i}", [128, 2048], BF16))
                     for i in range(2)]
            Rbf = e3(nc.sbuf_tensor("Rbf", [128, 2048], BF16))
            cbf = e3(nc.sbuf_tensor("cbf", [128, 2048], BF16))
            r_bf = e3(nc.sbuf_tensor("r_bf", [1, 2048], BF16))
            psA = e3(nc.psum_tensor("psA", [128, 512], F32))
            psB = e3(nc.psum_tensor("psB", [128, 512], F32))
            psC = e3(nc.psum_tensor("psC", [128, 2048], F32))
            psAB = (psA, psB)

            s.wait_ge(DV.h, mh_ready)
            exp_on_bank = [0, 0]
            ctx_last_pe = [0, 0]
            rbf_evac_dv = 0
            cp_gd = [0, 0]
            for h in range(NH):
                cc, half = h // 2, h % 2
                rows = slice(64 * half, 64 * half + 64)
                p0.step()
                for sc in range(NT):
                    par = sc % 2
                    if sc in (5, 11):
                        p0.step()
                    for qq in range(4):
                        ps = psAB[qq % 2]
                        t.wait_ge(AC.h, exp_on_bank[qq % 2])
                        PE.inc(t.matmul(
                            ps[:, 0:512],
                            kh[rows, cc * S + sc * 128:
                               cc * S + sc * 128 + 128],
                            qh[rows, cc * S + qq * 512:
                               cc * S + (qq + 1) * 512],
                            start=True, stop=True))
                        s.wait_ge(PE.h, PE.n)
                        s.wait_ge(PE.h, ctx_last_pe[par])
                        AC.inc(s.activation(
                            out=pT[:, par * 2048 + qq * 512:
                                   par * 2048 + (qq + 1) * 512],
                            in_=ps[:, 0:512], func=AF.Exp, scale=0.125,
                            bias=Mh[:, h:h + 1]))
                        exp_on_bank[qq % 2] = AC.n
                    t.wait_ge(AC.h, act_flush())
                    if sc == 0:
                        t.wait_ge(DV.h, rbf_evac_dv)
                    for p in range(4):
                        PE.inc(t.matmul(
                            psC[0:65, p * 512:(p + 1) * 512],
                            vsb[:, sc * NH * 65 + h * 65:
                                sc * NH * 65 + h * 65 + 65],
                            pT[:, par * 2048 + p * 512:
                               par * 2048 + (p + 1) * 512],
                            start=(sc == 0), stop=(sc == NT - 1)))
                    ctx_last_pe[par] = PE.n
                v.wait_ge(PE.h, PE.n)
                with nc.allow_low_precision(reason="r is a bf16 mm operand"):
                    DV.inc(v.reciprocal(out=r_bf[:], in_=psC[64:65, 0:2048]))
                DV.inc(v.tensor_copy(out=cbf[0:64, :], in_=psC[0:64, :]))
                v.wait_ge(DV.h, DV.n)
                DV.inc(v.tensor_copy(out=c["junkv"][0:1, 0:64],
                                     in_=r_bf[0:1, 0:64]))
                t.wait_ge(DV.h, DV.n)
                for p in range(4):
                    PE.inc(t.matmul(psC[64:128, p * 512:(p + 1) * 512],
                                    ones_b[0:1, 0:64],
                                    r_bf[0:1, p * 512:(p + 1) * 512],
                                    start=True, stop=True))
                v.wait_ge(PE.h, PE.n)
                DV.inc(v.tensor_copy(out=Rbf[0:64, :], in_=psC[64:128, :]))
                rbf_evac_dv = DV.n
                if half == 0:
                    v.wait_ge(GD.h, cp_gd[cc % 2])
                    DV.inc(v.tensor_mul(out=cpair[cc % 2][0:64, :],
                                        in0=cbf[0:64, :], in1=Rbf[0:64, :]))
                else:
                    DV.inc(v.tensor_mul(out=cpair[cc % 2][64:128, :],
                                        in0=cbf[0:64, :], in1=Rbf[0:64, :]))
                    g.wait_ge(DV.h, DV.n)
                    cp_gd[cc % 2] = dma(
                        g, GD, c["ctx_s"][cc * 128:(cc + 1) * 128, :],
                        cpair[cc % 2][:])
            barrier()
            if c.get("dbg"):
                dma(g, GD,
                    c["dbg_ctxn"][:].rearrange("p (cc t) -> p cc t", t=S),
                    c["ctx_s"][:].rearrange("(cc p) t -> p cc t", p=128))
                g.wait_ge(GD.h, GD.n)
            if stop_after == "P3":
                return
            p3.close()
            pqkv.close()

            p4 = p34.enter_context(ExitStack())
            e4 = p4.enter_context
            xst = [e4(nc.sbuf_tensor(f"xst{i}", [128, D], F32))
                   for i in range(2)]
            yb = e4(nc.sbuf_tensor("yb", [128, D], F32))
            zb = e4(nc.sbuf_tensor("zb", [128, D], F32))
            ub = e4(nc.sbuf_tensor("ub", [128, D], F32))
            x1t = [e4(nc.sbuf_tensor(f"x1t{i}", [128, D], F32))
                   for i in range(2)]
            x1c = e4(nc.sbuf_tensor("x1c", [128, D], F32))
            sqj = e4(nc.sbuf_tensor("sqj", [128, D], F32))
            st6p4 = e4(nc.sbuf_tensor("st6p4", [128, 12], F32))
            agp4 = e4(nc.sbuf_tensor("agp4", [128, 2], F32))
            st = [e4(nc.sbuf_tensor(f"st{i}", [128, 1], F32))
                  for i in range(7)]
            wo_b2 = e4(nc.sbuf_tensor("wo_b2", [128, NDC * 1024], BF16))
            ctile = [e4(nc.sbuf_tensor(f"ctile{i}", [128, NDC * 128], BF16))
                     for i in range(2)]
            po = [e4(nc.psum_tensor(f"po{i}", [128, 1024], F32))
                  for i in range(2)]
            for cc in range(NDC):
                dma(g, GD, wo_b2[:, cc * 1024:(cc + 1) * 1024],
                    c["wo_d"][cc * 128:(cc + 1) * 128, :])
            t.wait_ge(GD.h, GD.n)

            x_loaded = [0] * NT
            ct_ld = [0] * NT
            ct_pe = [0, 0]

            def load_ctile(tt):
                for cc in range(NDC):
                    dma(sy, LD, ctile[tt % 2][:, cc * 128:(cc + 1) * 128],
                        c["ctx_s"][cc * 128:(cc + 1) * 128,
                                   tt * 128:(tt + 1) * 128])
                ct_ld[tt] = LD.n

            for tt in range(2):
                load_ctile(tt)
                x_loaded[tt] = dma(sy, LD, xst[tt % 2][:],
                                   c["x_d"][tt * 128:(tt + 1) * 128, :])
            ln_bufs = {"st6": st6p4, "ag": agp4, "mu_neg": st[2],
                       "t0": st[3], "t1": st[4], "junk": st[5],
                       "fj": c["junkf"]}
            spill_gd = [0, 0]
            y_dv = [0, 0]
            for tt in range(NT):
                if tt >= 2:
                    t.wait_ge(DV.h, y_dv[tt % 2])
                pp4 = po[tt % 2]
                t.wait_ge(LD.h, ct_ld[tt])
                for cc in range(NDC):
                    for p in range(2):
                        PE.inc(t.matmul(
                            pp4[:, p * 512:(p + 1) * 512],
                            ctile[tt % 2][:, cc * 128:(cc + 1) * 128],
                            wo_b2[:, cc * 1024 + p * 512:
                                  cc * 1024 + (p + 1) * 512],
                            start=(cc == 0), stop=False))
                ct_pe[tt % 2] = PE.n
                for p in range(2):
                    PE.inc(t.matmul(pp4[:, p * 512:(p + 1) * 512], ones_b[:],
                                    c["bo_b"][0:1, p * 512:(p + 1) * 512],
                                    start=False, stop=True))
                v.wait_ge(PE.h, PE.n)
                v.wait_ge(LD.h, x_loaded[tt])
                DV.inc(v.tensor_add(out=yb[:], in0=pp4[:],
                                    in1=xst[tt % 2][:]))
                y_dv[tt % 2] = DV.n
                if c.get("dbg"):
                    g.wait_ge(DV.h, DV.n)
                    dma(g, GD, c["dbg_y"][tt * 128:(tt + 1) * 128, :], yb[:])
                emit_ln(nc, DV, AC, yb[:], zb[:], D, EPS_LN1, ln_bufs)
                v.wait_ge(AC.h, AC.n)
                if c.get("dbg"):
                    g.wait_ge(AC.h, AC.n)
                    dma(g, GD, c["dbg_z"][tt * 128:(tt + 1) * 128, :], zb[:])
                    v.wait_ge(GD.h, GD.n)
                v.scalar_tensor_tensor(out=sqj[:], in0=zb[:], scalar=1.0,
                                       in1=gb[:], op0=ALU.mult,
                                       op1=ALU.mult)
                v.tensor_add(out=zb[:], in0=sqj[:], in1=bb[:])
                DV.inc(v.tensor_add(out=ub[:], in0=zb[:], in1=xst[tt % 2][:]))
                if tt + 2 < NT:
                    sy.wait_ge(DV.h, DV.n)
                    sy.wait_ge(PE.h, ct_pe[tt % 2])
                    load_ctile(tt + 2)
                    x_loaded[tt + 2] = dma(sy, LD, xst[tt % 2][:],
                                           c["x_d"][(tt + 2) * 128:
                                                    (tt + 3) * 128, :])
                emit_ln(nc, DV, AC, ub[:], x1t[tt % 2][:], D, EPS_LN, ln_bufs)
                v.wait_ge(AC.h, AC.n)
                if tt >= 1:
                    v.wait_ge(GD.h, spill_gd[0])
                DV.inc(v.tensor_copy(out=x1c[:], in_=x1t[tt % 2][:]))
                g.wait_ge(DV.h, DV.n)
                spill_gd[0] = dma(g, GD,
                                  c["x1sp"][tt * 128:(tt + 1) * 128, :],
                                  x1c[:])
                if c.get("dbg"):
                    spill_gd[0] = dma(
                        g, GD, c["dbg_x1"][tt * 128:(tt + 1) * 128, :],
                        x1c[:])
            while not p0.done:
                p0.step()
            barrier()
            if stop_after == "P4":
                return

    p0stk.close()
    with ExitStack() as p5:
        e5 = p5.enter_context
        wgT = e5(nc.sbuf_tensor("wgT", [128, NDC * I], BF16))
        wuT = e5(nc.sbuf_tensor("wuT", [128, NDC * I], BF16))
        wdT = e5(nc.sbuf_tensor("wdT", [128, NIC * 1024], BF16))
        x1l = [e5(nc.sbuf_tensor(f"x1l{i}", [128, D], F32)) for i in range(2)]
        xq = [e5(nc.sbuf_tensor(f"xq{i}", [128, D], BF16)) for i in range(2)]
        xqT = [e5(nc.sbuf_tensor(f"xqT{i}", [128, D], BF16))
               for i in range(2)]
        silu_sb = e5(nc.sbuf_tensor("silu_sb", [128, IH], F32))
        h_sb = e5(nc.sbuf_tensor("h_sb", [128, IH], F32))
        hq = e5(nc.sbuf_tensor("hq", [128, IH], BF16))
        hqT = e5(nc.sbuf_tensor("hqT", [128, I], BF16))
        ub2 = e5(nc.sbuf_tensor("ub2", [128, D], F32))
        x2b = e5(nc.sbuf_tensor("x2b", [128, D], F32))
        x2c = e5(nc.sbuf_tensor("x2c", [128, D], F32))
        st6p5 = e5(nc.sbuf_tensor("st6p5", [128, 12], F32))
        agp5 = e5(nc.sbuf_tensor("agp5", [128, 2], F32))
        st2 = [e5(nc.sbuf_tensor(f"st2{i}", [128, 1], F32)) for i in range(7)]
        qsc2 = {k: e5(nc.sbuf_tensor("p5q" + k,
                                     [128, 32 if k in "MSR" else 512],
                                     BF16 if k == "sg" else F32))
                for k in ("M", "S", "R", "absx", "a", "c1", "c2", "sg")}
        dps = e5(nc.psum_tensor("dps", [128, 1024], F32))
        gps = e5(nc.psum_tensor("gps", [128, 1024], F32))
        gps2 = e5(nc.psum_tensor("gps2", [128, 1024], F32))
        tps2 = e5(nc.psum_tensor("tps2", [128, 2048], BF16))

        x1_ld = [0] * NT
        for tt in range(2):
            dma(sy, LD, x1l[tt % 2][:],
                c["x1sp"][tt * 128:(tt + 1) * 128, :])
        for cc in range(NDC):
            dma(sy, LD, wgT[:, cc * I:(cc + 1) * I],
                c["wgT_s"][cc * 128:(cc + 1) * 128, :])
            dma(sy, LD, wuT[:, cc * I:(cc + 1) * I],
                c["wuT_s"][cc * 128:(cc + 1) * 128, :])
        for ic in range(NIC):
            dma(sy, LD, wdT[:, ic * 1024:(ic + 1) * 1024],
                c["wdT_s"][ic * 128:(ic + 1) * 128, :])
        w_ready = LD.n
        x1_ld = [0] * NT
        x1_ld[0] = x1_ld[1] = LD.n
        t.wait_ge(LD.h, w_ready)
        ln_bufs2 = {"st6": st6p5, "ag": agp5, "mu_neg": st2[2],
                    "t0": st2[3], "t1": st2[4], "junk": st2[5],
                    "fj": c["junkf"]}
        out_gd = [0, 0]
        gu_ps = (gps, gps2)
        gu_dv = {0: 0, 1: 0}
        xqT_dv = [0, 0]

        def quant_x1(tt):
            par = tt % 2
            v.wait_ge(LD.h, x1_ld[tt])
            s.wait_ge(LD.h, x1_ld[tt])
            g.wait_ge(LD.h, x1_ld[tt])
            if c.get("dbg"):
                g.wait_ge(LD.h, x1_ld[tt])
                dma(g, GD, c["dbg_x1l"][tt * 128:(tt + 1) * 128, :],
                    x1l[par][:])
            for qo in (0, 512):
                emit_quant(nc, DV, AC, x1l[par][:, qo:qo + 512],
                           xq[par][:, qo:qo + 512], 512, qsc2)
            if c.get("dbg"):
                g.wait_ge(DV.h, DV.n)
                dma(g, GD, c["dbg_xq"][tt * 128:(tt + 1) * 128, :], xq[par][:])
            dve_flush()
            return DV.n

        def transpose_xq(tt, dvq):
            par = tt % 2
            t.wait_ge(DV.h, dvq)
            for r in range(8):
                PE.inc(t.transpose(tps2[:, r * 128:(r + 1) * 128],
                                   xq[par][:, r * 128:(r + 1) * 128],
                                   ident[:]))
            v.wait_ge(PE.h, PE.n)
            DV.inc(v.tensor_copy(out=xqT[par][:], in_=tps2[:, 0:1024]))
            xqT_dv[par] = DV.n

        dvq0 = quant_x1(0)
        transpose_xq(0, dvq0)
        pieces6 = [(hf, poff, pw) for hf in range(2)
                   for (poff, pw) in _pieces(IH)]
        tr_pe = [0, 0]
        for tt in range(NT):
            par = tt % 2
            t.wait_ge(DV.h, xqT_dv[par])
            fl = [0] * 6

            def gu_piece(k):
                hf, poff, pw = pieces6[k]
                pp = gu_ps[k % 2]
                t.wait_ge(DV.h, gu_dv[k % 2])
                for wi, wTT in enumerate((wgT, wuT)):
                    for dcc in range(NDC):
                        PE.inc(t.matmul(
                            pp[:, wi * 512:wi * 512 + pw],
                            xqT[par][:, dcc * 128:(dcc + 1) * 128],
                            wTT[:, dcc * I + hf * IH + poff:
                                dcc * I + hf * IH + poff + pw],
                            start=(dcc == 0), stop=(dcc == NDC - 1)))
                s.wait_ge(PE.h, PE.n)
                AC.inc(s.activation(out=silu_sb[:, poff:poff + pw],
                                    in_=pp[:, 0:pw], func=AF.Silu))
                fl[k] = act_flush()

            def mq_piece(k):
                hf, poff, pw = pieces6[k]
                pp = gu_ps[k % 2]
                v.wait_ge(AC.h, fl[k])
                if k == 3:
                    v.wait_ge(PE.h, tr_pe[0])
                DV.inc(v.scalar_tensor_tensor(
                    out=h_sb[:, poff:poff + pw], in0=pp[:, 512:512 + pw],
                    scalar=1.0, in1=silu_sb[:, poff:poff + pw],
                    op0=ALU.mult, op1=ALU.mult))
                gu_dv[k % 2] = DV.n
                emit_quant(nc, DV, AC, h_sb[:, poff:poff + pw],
                           hq[:, poff:poff + pw], pw, qsc2, war_dv=DV.n)

            def hq_transposes(hf):
                t.wait_ge(DV.h, DV.n)
                for gi in range(2):
                    r0, r1 = gi * 8, min(gi * 8 + 8, 11)
                    bank = gi % 2
                    for r in range(r0, r1):
                        PE.inc(t.transpose(
                            tps2[:, bank * 1024 + (r - r0) * 128:
                                 bank * 1024 + (r - r0 + 1) * 128],
                            hq[:, r * 128:(r + 1) * 128], ident[:]))
                    v.wait_ge(PE.h, PE.n)
                    DV.inc(v.tensor_copy(
                        out=hqT[:, hf * IH + r0 * 128:hf * IH + r1 * 128],
                        in_=tps2[:, bank * 1024:
                                 bank * 1024 + (r1 - r0) * 128]))
                tr_pe[hf] = PE.n

            for k in range(6):
                if k >= 1:
                    mq_piece(k - 1)
                gu_piece(k)
                if k == 3:
                    if c.get("dbg"):
                        g.wait_ge(DV.h, DV.n)
                        dma(g, GD, c["dbg_hq"][tt * 128:(tt + 1) * 128,
                                               0:IH], hq[:])
                    hq_transposes(0)
            mq_piece(5)
            if c.get("dbg"):
                g.wait_ge(DV.h, DV.n)
                dma(g, GD, c["dbg_hq"][tt * 128:(tt + 1) * 128,
                                       IH:2 * IH], hq[:])
            dve_flush()
            hq_transposes(1)
            hqT_dv = DV.n
            dvq_next = None
            if tt + 1 < NT:
                dvq_next = quant_x1(tt + 1)
            t.wait_ge(DV.h, hqT_dv)
            for ic in range(NIC):
                for p in range(2):
                    PE.inc(t.matmul(
                        dps[:, p * 512:(p + 1) * 512],
                        hqT[:, ic * 128:(ic + 1) * 128],
                        wdT[:, ic * 1024 + p * 512:ic * 1024 + (p + 1) * 512],
                        start=(ic == 0), stop=(ic == NIC - 1)))
            down_pe = PE.n
            if dvq_next is not None:
                transpose_xq(tt + 1, dvq_next)
            v.wait_ge(PE.h, down_pe)
            DV.inc(v.tensor_add(out=ub2[:], in0=dps[:],
                                in1=x1l[par][:]))
            if c.get("dbg"):
                g.wait_ge(DV.h, DV.n)
                dma(g, GD, c["dbg_u"][tt * 128:(tt + 1) * 128, :], ub2[:])
                v.wait_ge(GD.h, GD.n)
            if tt + 2 < NT:
                sy.wait_ge(DV.h, DV.n)
                sy.wait_ge(AC.h, AC.n)
                x1_ld[tt + 2] = dma(sy, LD, x1l[par][:],
                                    c["x1sp"][(tt + 2) * 128:
                                              (tt + 3) * 128, :])
            emit_ln(nc, DV, AC, ub2[:], x2b[:], D, EPS_LN, ln_bufs2)
            v.wait_ge(AC.h, AC.n)
            if tt >= 1:
                v.wait_ge(GD.h, out_gd[0])
            DV.inc(v.tensor_copy(out=x2c[:], in_=x2b[:]))
            g.wait_ge(DV.h, DV.n)
            out_gd[0] = dma(g, GD, c["out_d"][tt * 128:(tt + 1) * 128, :],
                            x2c[:])
        g.wait_ge(GD.h, GD.n)
        sy.wait_ge(GD.h, GD.n)


def build_nc(stop_after=None, p0_quant_only=False, dbg=False):
    nc = bass.Bass("TRN2", target_bir_lowering=False)
    c = {"stop_after": stop_after, "p0_quant_only": p0_quant_only, "dbg": dbg}
    c["x_d"] = nc.dram_tensor("x", [S, D], F32, kind="ExternalInput").ap()
    c["xT_d"] = nc.dram_tensor("xT", [D, S], F32, kind="ExternalInput").ap()
    c["wq_d"] = nc.dram_tensor("wq", [D, HK], F32, kind="ExternalInput").ap()
    c["wk_d"] = nc.dram_tensor("wk", [D, HK], F32, kind="ExternalInput").ap()
    c["wv_d"] = nc.dram_tensor("wv", [D, HK], F32, kind="ExternalInput").ap()
    c["bq_d"] = nc.dram_tensor("bq", [128, 8], F32, kind="ExternalInput").ap()
    c["bk_d"] = nc.dram_tensor("bk", [128, 8], F32, kind="ExternalInput").ap()
    c["bv_d"] = nc.dram_tensor("bv", [1, HK], F32, kind="ExternalInput").ap()
    c["wo_d"] = nc.dram_tensor("wo", [HK, D], F32, kind="ExternalInput").ap()
    c["bo_d"] = nc.dram_tensor("bo", [1, D], F32, kind="ExternalInput").ap()
    c["gam_d"] = nc.dram_tensor("gam", [1, D], F32, kind="ExternalInput").ap()
    c["bet_d"] = nc.dram_tensor("bet", [1, D], F32, kind="ExternalInput").ap()
    c["wg_d"] = nc.dram_tensor("wg", [I, D], F32, kind="ExternalInput").ap()
    c["wu_d"] = nc.dram_tensor("wu", [I, D], F32, kind="ExternalInput").ap()
    c["wd_d"] = nc.dram_tensor("wd", [D, I], F32, kind="ExternalInput").ap()
    c["out_d"] = nc.dram_tensor("out", [S, D], F32, kind="ExternalOutput").ap()
    c["x1sp"] = nc.dram_tensor("x1sp", [S, D], F32, kind="Internal").ap()
    c["wgT_s"] = nc.dram_tensor("wgT_s", [D, I], BF16, kind="Internal").ap()
    c["wuT_s"] = nc.dram_tensor("wuT_s", [D, I], BF16, kind="Internal").ap()
    c["wdT_s"] = nc.dram_tensor("wdT_s", [I, D], BF16, kind="Internal").ap()
    c["t16"] = nc.dram_tensor("t16", [1, 16], F32, kind="Internal").ap()
    c["ctx_s"] = nc.dram_tensor("ctx_s", [HK, S], BF16, kind="Internal").ap()
    if c.get("dbg"):
        c["dbg_qh"] = nc.dram_tensor("dbg_qh", [128, NDC * S], BF16,
                                     kind="ExternalOutput").ap()
        c["dbg_kh"] = nc.dram_tensor("dbg_kh", [128, NDC * S], BF16,
                                     kind="ExternalOutput").ap()
        c["dbg_vsb"] = nc.dram_tensor("dbg_vsb", [128, NT * NH * 65], BF16,
                                      kind="ExternalOutput").ap()
        c["dbg_mh"] = nc.dram_tensor("dbg_mh", [128, 16], F32,
                                     kind="ExternalOutput").ap()
        c["dbg_m2"] = nc.dram_tensor("dbg_m2", [1, 16], BF16,
                                     kind="ExternalOutput").ap()
        c["dbg_qn"] = nc.dram_tensor("dbg_qn", [16, 1], F32,
                                     kind="ExternalOutput").ap()
        c["dbg_kn"] = nc.dram_tensor("dbg_kn", [16, 1], F32,
                                     kind="ExternalOutput").ap()
        c["dbg_pn"] = nc.dram_tensor("dbg_pn", [16, 2048], F32,
                                     kind="ExternalOutput").ap()
        c["dbg_m2a"] = nc.dram_tensor("dbg_m2a", [16, 1], F32,
                                      kind="ExternalOutput").ap()
        c["dbg_msq"] = nc.dram_tensor("dbg_msq", [16, 1], F32,
                                      kind="ExternalOutput").ap()
        c["dbg_m2b"] = nc.dram_tensor("dbg_m2b", [16, 1], F32,
                                      kind="ExternalOutput").ap()
        c["dbg_mrow"] = nc.dram_tensor("dbg_mrow", [1, 16], F32,
                                       kind="ExternalOutput").ap()
        c["dbg_ctxn"] = nc.dram_tensor("dbg_ctxn", [128, NDC * S], BF16,
                                       kind="ExternalOutput").ap()
        c["dbg_x1"] = nc.dram_tensor("dbg_x1", [S, D], F32,
                                     kind="ExternalOutput").ap()
        c["dbg_y"] = nc.dram_tensor("dbg_y", [S, D], F32,
                                    kind="ExternalOutput").ap()
        c["dbg_z"] = nc.dram_tensor("dbg_z", [S, D], F32,
                                    kind="ExternalOutput").ap()
        c["dbg_xq"] = nc.dram_tensor("dbg_xq", [S, D], BF16,
                                     kind="ExternalOutput").ap()
        c["dbg_hq"] = nc.dram_tensor("dbg_hq", [S, I], BF16,
                                     kind="ExternalOutput").ap()
        c["dbg_u"] = nc.dram_tensor("dbg_u", [S, D], F32,
                                    kind="ExternalOutput").ap()
        c["dbg_x1l"] = nc.dram_tensor("dbg_x1l", [S, D], F32,
                                      kind="ExternalOutput").ap()

    with ExitStack() as top:
        e = top.enter_context
        c["ident"] = e(nc.sbuf_tensor("ident", [128, 128], BF16))
        c["ones_f"] = e(nc.sbuf_tensor("ones_f", [1, 128], F32))
        c["ones_b"] = e(nc.sbuf_tensor("ones_b", [1, 128], BF16))
        c["Mh"] = e(nc.sbuf_tensor("Mh", [128, 16], F32))
        c["bqc"] = e(nc.sbuf_tensor("bqc", [128, 8], F32))
        c["bkc"] = e(nc.sbuf_tensor("bkc", [128, 8], F32))
        c["bv_b"] = e(nc.sbuf_tensor("bv_b", [1, HK], BF16))
        c["bo_b"] = e(nc.sbuf_tensor("bo_b", [1, D], BF16))
        c["rowtmp_bo"] = e(nc.sbuf_tensor("rowtmp_bo", [1, D], F32))
        c["junkf"] = e(nc.sbuf_tensor("junkf", [128, 32], F32))
        c["junkv"] = e(nc.sbuf_tensor("junkv", [128, 64], F32))
        c["LD"] = Sem(e(nc.semaphore("sLD")))
        c["GD"] = Sem(e(nc.semaphore("sGD")))
        c["PE"] = Sem(e(nc.semaphore("sPE")))
        c["AC"] = Sem(e(nc.semaphore("sAC")))
        c["DV"] = Sem(e(nc.semaphore("sDV")))
        c["GP"] = Sem(e(nc.semaphore("sGP")))
        block = e(nc.Block())

        @block.vector
        def _(_v):
            emit_all(nc, c)
    return nc


_NC_CACHE = {}


def kernel(L0_attn_x, L0_attn_Wq, L0_attn_bq, L0_attn_Wk, L0_attn_bk,
           L0_attn_Wv, L0_attn_bv, L0_attn_Wo, L0_attn_bo, L0_attn_gamma,
           L0_attn_beta, L0_ffn_w_gate, L0_ffn_w_up, L0_ffn_w_down):
    x = np.ascontiguousarray(np.asarray(L0_attn_x, dtype=np.float32))
    B = x.shape[0]
    wq = np.ascontiguousarray(np.asarray(L0_attn_Wq, np.float32).reshape(D, HK))
    wk = np.ascontiguousarray(np.asarray(L0_attn_Wk, np.float32).reshape(D, HK))
    wv = np.ascontiguousarray(np.asarray(L0_attn_Wv, np.float32).reshape(D, HK))
    bq = np.ascontiguousarray(np.asarray(L0_attn_bq, np.float32).reshape(8, 128).T)
    bk = np.ascontiguousarray(np.asarray(L0_attn_bk, np.float32).reshape(8, 128).T)
    bv = np.ascontiguousarray(np.asarray(L0_attn_bv, np.float32).reshape(1, HK))
    wo = np.ascontiguousarray(np.asarray(L0_attn_Wo, np.float32).reshape(HK, D))
    bo = np.ascontiguousarray(np.asarray(L0_attn_bo, np.float32).reshape(1, D))
    gam = np.ascontiguousarray(np.asarray(L0_attn_gamma, np.float32).reshape(1, D))
    bet = np.ascontiguousarray(np.asarray(L0_attn_beta, np.float32).reshape(1, D))
    wg = np.ascontiguousarray(np.asarray(L0_ffn_w_gate, np.float32))
    wu = np.ascontiguousarray(np.asarray(L0_ffn_w_up, np.float32))
    wd = np.ascontiguousarray(np.asarray(L0_ffn_w_down, np.float32))

    if "nc" not in _NC_CACHE:
        _NC_CACHE["nc"] = build_nc()
    nc = _NC_CACHE["nc"]
    in_maps = []
    for b in range(B):
        xb = np.ascontiguousarray(x[b])
        in_maps.append({
            "x": xb, "xT": np.ascontiguousarray(xb.T),
            "wq": wq, "wk": wk, "wv": wv, "bq": bq, "bk": bk, "bv": bv,
            "wo": wo, "bo": bo, "gam": gam, "bet": bet,
            "wg": wg, "wu": wu, "wd": wd,
        })
    res = run_bass_kernel_spmd(nc, in_maps, core_ids=list(range(B)))
    return np.stack([np.asarray(r["out"], dtype=np.float32)
                     for r in res.results], axis=0)


# version 16
# speedup vs baseline: 1.0114x; 1.0038x over previous
import numpy as np
from contextlib import ExitStack

import concourse.bass as bass
import concourse.mybir as mybir
from concourse.bass_utils import run_bass_kernel_spmd

F32 = mybir.dt.float32
BF16 = mybir.dt.bfloat16
AF = mybir.ActivationFunctionType
ALU = mybir.AluOpType
AX = mybir.AxisListType.X

S = 2048
D = 1024
HK = 1024
NH = 16
I = 2816
NT = 16
NDC = 8
NIC = 22
EPS_LN1 = 1e-3
EPS_LN = 1e-5
IH = 1408


class Sem:
    def __init__(self, h):
        self.h = h
        self.n = 0

    def inc(self, inst, k=1):
        inst.then_inc(self.h, k)
        self.n += k
        return self.n


def _pieces(width, step=512):
    out, off = [], 0
    while off < width:
        w = min(step, width - off)
        out.append((off, w))
        off += w
    return out


def emit_quant(nc, DV, AC, x_ap, out_ap, W, sc, war_dv=None):
    G = W // 16
    v = nc.vector
    M, Sb, R = sc["M"], sc["S"], sc["R"]
    absx, a, c1, c2, sg = sc["absx"], sc["a"], sc["c1"], sc["c2"], sc["sg"]
    nc.scalar.wait_ge(DV.h, DV.n if war_dv is None else war_dv)
    AC.inc(nc.scalar.sign(out=sg[:, 0:W], in_=x_ap))
    ac_sign = AC.n
    AC.inc(nc.scalar.activation(out=absx[:, 0:W], in_=x_ap, func=AF.Abs))
    ac_absx = AC.n
    xv = x_ap.rearrange("p (g k) -> p g k", k=16)
    DV.inc(v.tensor_reduce(out=M[:, 0:G], in_=xv, axis=AX, op=ALU.max,
                           apply_absolute_value=True))
    v.wait_ge(DV.h, DV.n)
    DV.inc(v.tensor_scalar(out=Sb[:, 0:G], in0=M[:, 0:G], scalar1=1.0 / 6.0,
                           scalar2=1e-12, op0=ALU.mult, op1=ALU.max))
    v.wait_ge(DV.h, DV.n)
    DV.inc(v.reciprocal(out=R[:, 0:G], in_=Sb[:, 0:G]))
    v.wait_ge(DV.h, DV.n)
    v.wait_ge(AC.h, ac_absx)
    Rb = R[:, 0:G].unsqueeze(2).broadcast_to([128, G, 16])
    Sbb = Sb[:, 0:G].unsqueeze(2).broadcast_to([128, G, 16])
    v.tensor_tensor(out=a[:, 0:W].rearrange("p (g k) -> p g k", k=16),
                    in0=absx[:, 0:W].rearrange("p (g k) -> p g k", k=16),
                    in1=Rb, op=ALU.mult)
    v.tensor_scalar(out=c1[:, 0:W], in0=a[:, 0:W], scalar1=0.25, scalar2=None,
                    op0=ALU.is_gt)
    v.scalar_tensor_tensor(out=c2[:, 0:W], in0=a[:, 0:W], scalar=0.75,
                           in1=c1[:, 0:W], op0=ALU.is_gt, op1=ALU.add)
    v.scalar_tensor_tensor(out=c1[:, 0:W], in0=a[:, 0:W], scalar=1.25,
                           in1=c2[:, 0:W], op0=ALU.is_gt, op1=ALU.add)
    v.scalar_tensor_tensor(out=c2[:, 0:W], in0=a[:, 0:W], scalar=1.75,
                           in1=c1[:, 0:W], op0=ALU.is_gt, op1=ALU.add)
    v.tensor_scalar(out=c1[:, 0:W], in0=a[:, 0:W], scalar1=5.0, scalar2=2.0,
                    op0=ALU.is_gt, op1=ALU.mult)
    v.scalar_tensor_tensor(out=absx[:, 0:W], in0=a[:, 0:W], scalar=3.5,
                           in1=c1[:, 0:W], op0=ALU.is_gt, op1=ALU.add)
    v.scalar_tensor_tensor(out=c1[:, 0:W], in0=a[:, 0:W], scalar=2.5,
                           in1=absx[:, 0:W], op0=ALU.is_gt, op1=ALU.add)
    v.scalar_tensor_tensor(out=absx[:, 0:W], in0=c2[:, 0:W], scalar=0.5,
                           in1=c1[:, 0:W], op0=ALU.mult, op1=ALU.add)
    v.scalar_tensor_tensor(out=a[:, 0:W].rearrange("p (g k) -> p g k", k=16),
                           in0=absx[:, 0:W].rearrange("p (g k) -> p g k", k=16),
                           scalar=1.0, in1=Sbb, op0=ALU.mult, op1=ALU.mult)
    v.wait_ge(AC.h, ac_sign)
    DV.inc(v.tensor_mul(out=out_ap, in0=a[:, 0:W], in1=sg[:, 0:W]))
    return DV.n, ac_sign


def emit_ln(nc, DV, AC, y_ap, z_out, W, eps, B):
    v, s = nc.vector, nc.scalar
    s.wait_ge(DV.h, DV.n)
    nst = W // 512
    for j in range(nst):
        DV.inc(v.bn_stats(out=B["st6"][:, j * 6:(j + 1) * 6],
                          in_=y_ap[:, j * 512:(j + 1) * 512]))
    v.wait_ge(DV.h, DV.n)
    DV.inc(v.bn_aggr(out=B["ag"][:, 0:2], in_=B["st6"][:, 0:6 * nst]))
    v.wait_ge(DV.h, DV.n)
    DV.inc(v.tensor_scalar(out=B["t0"][:], in0=B["ag"][:, 1:2], scalar1=eps,
                           scalar2=None, op0=ALU.add))
    v.wait_ge(DV.h, DV.n)
    DV.inc(v.reciprocal(out=B["t1"][:], in_=B["t0"][:]))
    v.wait_ge(DV.h, DV.n)
    DV.inc(v.tensor_scalar(out=B["mu_neg"][:], in0=B["ag"][:, 0:1],
                           scalar1=-1.0, scalar2=None, op0=ALU.mult))
    DV.inc(v.tensor_copy(out=B["junk"][:], in_=B["t1"][:]))
    DV.inc(v.tensor_copy(out=B["junk"][:], in_=B["mu_neg"][:]))
    s.wait_ge(DV.h, DV.n)
    AC.inc(s.activation(out=B["t0"][:], in_=B["t1"][:], func=AF.Sqrt))
    s.wait_ge(AC.h, AC.n)
    AC.inc(s.activation(out=B["t1"][:], in_=B["mu_neg"][:], func=AF.Identity,
                        scale=B["t0"][:, 0:1]))
    s.wait_ge(AC.h, AC.n)
    AC.inc(s.activation(out=z_out, in_=y_ap, func=AF.Identity,
                        scale=B["t0"][:, 0:1], bias=B["t1"][:, 0:1]))
    AC.inc(s.activation(out=B["fj"][:], in_=B["fj"][:], func=AF.Square))
    return AC.n


class P0Stream:

    def __init__(self, nc, c, stk, LD, GD, PE, AC, DV):
        e = stk.enter_context
        self.nc, self.c = nc, c
        self.LD, self.GD, self.PE, self.AC, self.DV = LD, GD, PE, AC, DV
        self.raw = [e(nc.sbuf_tensor(f"p0raw{i}", [128, 1024], F32))
                    for i in range(2)]
        self.qsc = {k: e(nc.sbuf_tensor(
            "p0q" + k, [128, 64 if k in ("M", "S", "R") else 1024],
            BF16 if k == "sg" else F32))
            for k in ("M", "S", "R", "absx", "a", "c1", "c2", "sg")}
        self.wbf = [e(nc.sbuf_tensor(f"p0wbf{i}", [128, 1024], BF16))
                    for i in range(2)]
        self.stg = e(nc.sbuf_tensor("p0stg", [128, 1024], BF16))
        self.tps = e(nc.psum_tensor("p0tps", [128, 2048], BF16))
        self.units = []
        for w_d, wT_s in ((c["wg_d"], c["wgT_s"]), (c["wu_d"], c["wuT_s"])):
            for r in range(I // 128):
                self.units.append((
                    w_d[r * 128:(r + 1) * 128, :], 8,
                    wT_s[0:1024, r * 128:(r + 1) * 128]
                    .rearrange("(j a) b -> a j b", j=8)))
        for r in range(8):
            for j0, j1 in ((0, 8), (8, 16), (16, 22)):
                self.units.append((
                    c["wd_d"][r * 128:(r + 1) * 128, j0 * 128:j1 * 128],
                    j1 - j0,
                    c["wdT_s"][j0 * 128:j1 * 128, r * 128:(r + 1) * 128]
                    .rearrange("(j a) b -> a j b", j=j1 - j0)))
        self.n_units = len(self.units)
        self.k = 0
        self.ld = [0] * self.n_units
        self.dvv = [0] * self.n_units
        self.acv = [0] * self.n_units
        self.pe_tr = [0, 0]
        self.evac_dv = [0, 0]
        self.gd = [0, 0]
        sy = nc.sync
        for i in range(min(2, self.n_units)):
            src, nblk, _ = self.units[i]
            sy.dma_start(out=self.raw[i % 2][:, 0:nblk * 128],
                         in_=src).then_inc(LD.h, 16)
            LD.n += 16
            self.ld[i] = LD.n

    @property
    def done(self):
        return self.k > self.n_units

    def step(self):
        if self.done:
            return
        nc = self.nc
        v, t, g, sy = nc.vector, nc.tensor, nc.gpsimd, nc.sync
        s = nc.scalar
        LD, GD, PE, AC, DV = self.LD, self.GD, self.PE, self.AC, self.DV
        k = self.k
        if k < self.n_units:
            src, nblk, _ = self.units[k]
            w = nblk * 128
            v.wait_ge(LD.h, self.ld[k])
            s.wait_ge(LD.h, self.ld[k])
            v.wait_ge(PE.h, self.pe_tr[k % 2])
            dvv, acv = emit_quant(nc, DV, AC, self.raw[k % 2][:, 0:w],
                                  self.wbf[k % 2][:, 0:w], w, self.qsc,
                                  war_dv=self.dvv[k - 1] if k else 0)
            self.dvv[k], self.acv[k] = dvv, acv
            if k + 2 < self.n_units:
                src2, nblk2, _ = self.units[k + 2]
                sy.wait_ge(DV.h, dvv)
                sy.wait_ge(AC.h, acv)
                sy.dma_start(out=self.raw[k % 2][:, 0:nblk2 * 128],
                             in_=src2).then_inc(LD.h, 16)
                LD.n += 16
                self.ld[k + 2] = LD.n
        if k >= 1:
            j = k - 1
            _, nblk, out_ap = self.units[j]
            w = nblk * 128
            par = j % 2
            t.wait_ge(DV.h, self.dvv[j])
            for b in range(nblk):
                PE.inc(t.transpose(
                    self.tps[:, par * 1024 + b * 128:par * 1024 + (b + 1) * 128],
                    self.wbf[par][:, b * 128:(b + 1) * 128],
                    self.c["ident"][:]))
            self.pe_tr[par] = PE.n
            v.wait_ge(PE.h, self.pe_tr[par])
            v.wait_ge(GD.h, self.gd[0])
            DV.inc(v.tensor_copy(out=self.stg[:, 0:w],
                                 in_=self.tps[:, par * 1024:par * 1024 + w]))
            self.evac_dv[par] = DV.n
            g.wait_ge(DV.h, self.evac_dv[par])
            g.dma_start(out=out_ap,
                        in_=self.stg[:, 0:w]
                        .rearrange("p (j b) -> p j b", j=nblk)
                        ).then_inc(GD.h, 16)
            GD.n += 16
            self.gd[0] = GD.n
        self.k += 1


def emit_all(nc, c):
    stop_after = c.get("stop_after")
    LD, GD, PE, AC, DV, GP = (c["LD"], c["GD"], c["PE"], c["AC"], c["DV"],
                              c["GP"])
    v, s, t, g, sy = nc.vector, nc.scalar, nc.tensor, nc.gpsimd, nc.sync
    ident, ones_f, ones_b, Mh = c["ident"], c["ones_f"], c["ones_b"], c["Mh"]

    def barrier():
        PE.inc(t.nop())
        AC.inc(s.nop())
        DV.inc(v.nop())
        GP.inc(g.nop())
        vals = [(PE, PE.n), (AC, AC.n), (DV, DV.n), (GP, GP.n),
                (LD, LD.n), (GD, GD.n)]
        own = {id(t): PE, id(s): AC, id(v): DV, id(g): GP}
        for eng in (t, s, v, g, sy):
            mine = own.get(id(eng))
            for sem, val in vals:
                if sem is mine:
                    continue
                eng.wait_ge(sem.h, val)

    def act_flush():
        AC.inc(s.activation(out=c["junkf"][:], in_=c["junkf"][:],
                            func=AF.Square))
        return AC.n

    def dve_flush():
        DV.inc(v.tensor_copy(out=c["junkv"][:], in_=c["junkv"][:]))
        return DV.n

    def dma(eng, sem, out, in_):
        eng.dma_start(out=out, in_=in_).then_inc(sem.h, 16)
        sem.n += 16
        return sem.n

    g.memset(ident[:], 0.0)
    g.affine_select(out=ident[:], in_=ident[:], compare_op=ALU.not_equal,
                    fill=1.0, base=0, pattern=[[-1, 128]], channel_multiplier=1)
    g.memset(ones_f[:], 1.0)
    g.memset(c["junkf"][:], 1.0)
    g.memset(c["junkv"][:], 1.0)
    GP.inc(g.memset(ones_b[:], 1.0))
    dma(sy, LD, c["bqc"][:], c["bq_d"][:])
    dma(sy, LD, c["bkc"][:], c["bk_d"][:])

    if stop_after == "INIT":
        barrier()
        return
    p0stk = ExitStack()
    p0 = P0Stream(nc, c, p0stk, LD, GD, PE, AC, DV)
    if stop_after in ("P0", "P0w1"):
        while not p0.done:
            p0.step()
        barrier()
        p0stk.close()
        return

    with ExitStack() as p2:
        e = p2.enter_context
        gb = e(nc.sbuf_tensor("gb", [128, D], F32))
        bb = e(nc.sbuf_tensor("bb", [128, D], F32))
        pqkv = e(ExitStack())
        eq = pqkv.enter_context
        qh = eq(nc.sbuf_tensor("qh", [128, NDC * S], BF16))
        kh = eq(nc.sbuf_tensor("kh", [128, NDC * S], BF16))
        vsb = eq(nc.sbuf_tensor("vsb", [128, NT * NH * 65], BF16))
        mh_ready = 0

        with ExitStack() as p2a:
            e2 = p2a.enter_context
            qn = e2(nc.sbuf_tensor("qn", [128, 1], F32))
            kn = e2(nc.sbuf_tensor("kn", [128, 1], F32))
            m2 = e2(nc.sbuf_tensor("m2", [128, 1], F32))
            msq = e2(nc.sbuf_tensor("msq", [128, 1], F32))
            mhe = e2(nc.sbuf_tensor("mhe", [128, 16], F32))
            xT_b = e2(nc.sbuf_tensor("xT_b", [128, NDC * S], BF16))
            wsl = e2(nc.sbuf_tensor("wsl", [128, 8 * 1024], BF16))
            xTf = e2(nc.sbuf_tensor("xTf", [128, 1024], F32))
            q2 = e2(nc.sbuf_tensor("q2", [128, 1024], BF16))
            masks = e2(nc.sbuf_tensor("masks", [128, 8 * 128], BF16))
            dma(g, GD, c["bv_b"][:], c["bv_d"][:])
            dma(g, GD, c["bo_b"][:], c["bo_d"][:])

            ldg = dma(sy, LD, xTf[0:1, :], c["gam_d"][:])
            with nc.psum_tensor("pgb", [128, 1024], F32) as pgb:
                t.wait_ge(LD.h, ldg)
                t.wait_ge(GP.h, GP.n)
                for p in range(2):
                    PE.inc(t.matmul(pgb[:, p * 512:(p + 1) * 512], ones_f[:],
                                    xTf[0:1, p * 512:(p + 1) * 512],
                                    start=True, stop=True))
                v.wait_ge(PE.h, PE.n)
                DV.inc(v.tensor_copy(out=gb[:], in_=pgb[:]))
                sy.wait_ge(PE.h, PE.n)
                dma(sy, LD, xTf[0:1, :], c["bet_d"][:])
                ldb = dma(sy, LD, c["rowtmp_bo"][:], c["bo_d"][:])
                t.wait_ge(DV.h, DV.n)
                t.wait_ge(LD.h, ldb)
                for p in range(2):
                    PE.inc(t.matmul(pgb[:, p * 512:(p + 1) * 512], ones_f[:],
                                    xTf[0:1, p * 512:(p + 1) * 512],
                                    start=True, stop=True))
                v.wait_ge(PE.h, PE.n)
                DV.inc(v.tensor_copy(out=bb[:], in_=pgb[:]))
                gbb_pe = PE.n

            g.memset(masks[:], 0.0)
            for cc in range(8):
                g.memset(masks[0:64, cc * 128 + 2 * cc:
                               cc * 128 + 2 * cc + 1], 1.0)
                g.memset(masks[64:128, cc * 128 + 2 * cc + 1:
                               cc * 128 + 2 * cc + 2], 1.0)
            vv = vsb[:].rearrange("p (sc h w) -> p sc h w", h=NH, w=65)
            GP.inc(g.memset(vv[:, :, :, 64:65], 1.0))

            sy.wait_ge(PE.h, gbb_pe)
            for dcc in range(NDC):
                for hf2 in range(2):
                    if dcc + hf2 >= 1:
                        sy.wait_ge(AC.h, AC.n)
                    ldv = dma(sy, LD, xTf[:],
                              c["xT_d"][dcc * 128:(dcc + 1) * 128,
                                        hf2 * 1024:(hf2 + 1) * 1024])
                    s.wait_ge(LD.h, ldv)
                    AC.inc(s.activation(
                        out=xT_b[:, dcc * S + hf2 * 1024:
                                 dcc * S + (hf2 + 1) * 1024],
                        in_=xTf[:], func=AF.Copy))
            for dcc in range(NDC):
                dma(g, GD, wsl[:, dcc * 1024:(dcc + 1) * 1024],
                    c["wq_d"][dcc * 128:(dcc + 1) * 128, :])

            t.wait_ge(AC.h, AC.n)
            t.wait_ge(GD.h, GD.n)
            v.wait_ge(LD.h, LD.n)
            qk_dv = 0
            with nc.psum_tensor("pq", [128, 2048], F32) as pq:
                for wi, (dst, bias) in enumerate(((qh, c["bqc"]),
                                                  (kh, c["bkc"]))):
                    if wi == 1:
                        g.wait_ge(PE.h, PE.n)
                        for dcc in range(NDC):
                            dma(g, GD, wsl[:, dcc * 1024:(dcc + 1) * 1024],
                                c["wk_d"][dcc * 128:(dcc + 1) * 128, :])
                        t.wait_ge(GD.h, GD.n)
                    for cc in range(NDC):
                        if wi > 0 or cc > 0:
                            t.wait_ge(DV.h, qk_dv)
                        for dcc in range(NDC):
                            for p in range(4):
                                PE.inc(t.matmul(
                                    pq[:, p * 512:(p + 1) * 512],
                                    wsl[:, dcc * 1024 + cc * 128:
                                        dcc * 1024 + cc * 128 + 128],
                                    xT_b[:, dcc * S + p * 512:
                                         dcc * S + (p + 1) * 512],
                                    start=(dcc == 0), stop=(dcc == NDC - 1)))
                        v.wait_ge(PE.h, PE.n)
                        for p in range(4):
                            DV.inc(v.tensor_scalar(
                                out=dst[:, cc * S + p * 512:
                                        cc * S + (p + 1) * 512],
                                in0=pq[:, p * 512:(p + 1) * 512],
                                scalar1=bias[:, cc:cc + 1], scalar2=None,
                                op0=ALU.add))
                        qk_dv = DV.n
                        p0.step()

            g.wait_ge(PE.h, PE.n)
            for dcc in range(NDC):
                dma(g, GD, wsl[:, dcc * 1024:(dcc + 1) * 1024],
                    c["wv_d"][dcc * 128:(dcc + 1) * 128, :])
            wv_gd = GD.n

            with ExitStack() as pnx:
                pn = pnx.enter_context(nc.psum_tensor("pn", [128, 2048], F32))
                pm = pnx.enter_context(nc.psum_tensor("pm", [128, 16], F32))
                pmt = pnx.enter_context(nc.psum_tensor("pmt", [1, 128], BF16))
                mb = pnx.enter_context(nc.sbuf_tensor("mb", [128, 1], BF16))
                mrow_b = pnx.enter_context(nc.sbuf_tensor("mrow_b", [1, 128], BF16))
                for src, dstn in ((qh, qn), (kh, kn)):
                    for cc in range(NDC):
                        for h2 in range(2):
                            if cc + h2 > 0:
                                v.wait_ge(PE.h, PE.n)
                            DV.inc(v.tensor_mul(
                                out=q2[:],
                                in0=src[:, cc * S + h2 * 1024:
                                        cc * S + (h2 + 1) * 1024],
                                in1=src[:, cc * S + h2 * 1024:
                                        cc * S + (h2 + 1) * 1024]))
                            t.wait_ge(DV.h, DV.n)
                            for p in range(2):
                                PE.inc(t.matmul(
                                    pn[:, h2 * 1024 + p * 512:
                                       h2 * 1024 + (p + 1) * 512],
                                    masks[:, cc * 128:(cc + 1) * 128],
                                    q2[:, p * 512:(p + 1) * 512],
                                    start=(cc == 0),
                                    stop=(cc == NDC - 1)))
                    v.wait_ge(PE.h, PE.n)
                    DV.inc(v.tensor_reduce(out=dstn[:], in_=pn[:],
                                           axis=AX, op=ALU.max))
                    v.wait_ge(DV.h, DV.n)
                if c.get("dbg"):
                    g.wait_ge(DV.h, DV.n)
                    dma(g, GD, c["dbg_qn"][:], qn[0:16, 0:1])
                    dma(g, GD, c["dbg_kn"][:], kn[0:16, 0:1])
                DV.inc(v.tensor_add(out=msq[:], in0=qn[:], in1=kn[:]))
                v.wait_ge(DV.h, DV.n)
                DV.inc(v.tensor_scalar(out=m2[:], in0=msq[:],
                                       scalar1=-0.0625, scalar2=None,
                                       op0=ALU.mult))
                v.wait_ge(DV.h, DV.n)
                v.tensor_copy(out=mb[:], in_=m2[:])
                DV.inc(v.tensor_copy(out=q2[:, 0:1024], in_=qh[:, 0:1024]))
                t.wait_ge(DV.h, DV.n)
                PE.inc(t.transpose(pmt[0:1, 0:128], mb[:, 0:1],
                                   ident[:]))
                v.wait_ge(PE.h, PE.n)
                v.tensor_copy(out=mrow_b[:], in_=pmt[0:1, 0:128])
                DV.inc(v.tensor_copy(out=q2[:, 0:1024], in_=qh[:, 0:1024]))
                t.wait_ge(DV.h, DV.n)
                PE.inc(t.matmul(pm[:, 0:16], ones_b[:], mrow_b[0:1, 0:16],
                                start=True, stop=True))
                if c.get("dbg"):
                    g.wait_ge(DV.h, DV.n)
                    dma(g, GD, c["dbg_m2"][:], mrow_b[0:1, 0:16])
                v.wait_ge(PE.h, PE.n)
                DV.inc(v.tensor_copy(out=Mh[:], in_=pm[:]))
                DV.inc(v.tensor_copy(out=mhe[:], in_=Mh[:]))
                mh_ready = DV.n

            t.wait_ge(GD.h, wv_gd)
            vh_dv = 0
            with nc.psum_tensor("pv", [128, 1024], F32) as pv:
                for sc in range(NT):
                    t.wait_ge(DV.h, vh_dv)
                    for dcc in range(NDC):
                        for p in range(2):
                            PE.inc(t.matmul(
                                pv[:, p * 512:(p + 1) * 512],
                                xT_b[:, dcc * S + sc * 128:
                                     dcc * S + sc * 128 + 128],
                                wsl[:, dcc * 1024 + p * 512:
                                    dcc * 1024 + (p + 1) * 512],
                                start=(dcc == 0), stop=False))
                    for p in range(2):
                        PE.inc(t.matmul(pv[:, p * 512:(p + 1) * 512],
                                        ones_b[:],
                                        c["bv_b"][0:1, p * 512:(p + 1) * 512],
                                        start=False, stop=True))
                    v.wait_ge(PE.h, PE.n)
                    v.wait_ge(GP.h, GP.n)
                    DV.inc(v.tensor_copy(
                        out=vsb[:, sc * NH * 65:(sc + 1) * NH * 65]
                        .rearrange("p (h w) -> p h w", w=65)[:, :, 0:64],
                        in_=pv[:].rearrange("p (h w) -> p h w", w=64)))
                    vh_dv = DV.n
                    if sc % 2 == 0:
                        p0.step()
            barrier()
            if stop_after == "P2":
                return

        if c.get("dbg"):
            dma(g, GD, c["dbg_qh"][:], qh[:])
            dma(g, GD, c["dbg_kh"][:], kh[:])
            dma(g, GD, c["dbg_vsb"][:], vsb[:])
            dma(g, GD, c["dbg_mh"][:], Mh[:])
            g.wait_ge(GD.h, GD.n)

        with ExitStack() as p34:
            p3 = p34.enter_context(ExitStack())
            e3 = p3.enter_context
            pT = e3(nc.sbuf_tensor("pT", [128, 2 * 2048], BF16))
            cpair = [e3(nc.sbuf_tensor(f"cpair{i}", [128, 2048], BF16))
                     for i in range(2)]
            Rbf = e3(nc.sbuf_tensor("Rbf", [128, 2048], BF16))
            cbf = e3(nc.sbuf_tensor("cbf", [128, 2048], BF16))
            r_bf = e3(nc.sbuf_tensor("r_bf", [1, 2048], BF16))
            psA = e3(nc.psum_tensor("psA", [128, 512], F32))
            psB = e3(nc.psum_tensor("psB", [128, 512], F32))
            psC = e3(nc.psum_tensor("psC", [128, 2048], F32))
            psAB = (psA, psB)

            s.wait_ge(DV.h, mh_ready)
            exp_on_bank = [0, 0]
            ctx_last_pe = [0, 0]
            rbf_evac_dv = 0
            cp_gd = [0, 0]
            for h in range(NH):
                cc, half = h // 2, h % 2
                rows = slice(64 * half, 64 * half + 64)
                p0.step()
                for sc in range(NT):
                    par = sc % 2
                    if sc in (5, 11):
                        p0.step()
                    for qq in range(4):
                        ps = psAB[qq % 2]
                        t.wait_ge(AC.h, exp_on_bank[qq % 2])
                        PE.inc(t.matmul(
                            ps[:, 0:512],
                            kh[rows, cc * S + sc * 128:
                               cc * S + sc * 128 + 128],
                            qh[rows, cc * S + qq * 512:
                               cc * S + (qq + 1) * 512],
                            start=True, stop=True))
                        s.wait_ge(PE.h, PE.n)
                        s.wait_ge(PE.h, ctx_last_pe[par])
                        AC.inc(s.activation(
                            out=pT[:, par * 2048 + qq * 512:
                                   par * 2048 + (qq + 1) * 512],
                            in_=ps[:, 0:512], func=AF.Exp, scale=0.125,
                            bias=Mh[:, h:h + 1]))
                        exp_on_bank[qq % 2] = AC.n
                    t.wait_ge(AC.h, act_flush())
                    if sc == 0:
                        t.wait_ge(DV.h, rbf_evac_dv)
                    for p in range(4):
                        PE.inc(t.matmul(
                            psC[0:65, p * 512:(p + 1) * 512],
                            vsb[:, sc * NH * 65 + h * 65:
                                sc * NH * 65 + h * 65 + 65],
                            pT[:, par * 2048 + p * 512:
                               par * 2048 + (p + 1) * 512],
                            start=(sc == 0), stop=(sc == NT - 1)))
                    ctx_last_pe[par] = PE.n
                v.wait_ge(PE.h, PE.n)
                with nc.allow_low_precision(reason="r is a bf16 mm operand"):
                    DV.inc(v.reciprocal(out=r_bf[:], in_=psC[64:65, 0:2048]))
                DV.inc(v.tensor_copy(out=cbf[0:64, :], in_=psC[0:64, :]))
                v.wait_ge(DV.h, DV.n)
                DV.inc(v.tensor_copy(out=c["junkv"][0:1, 0:64],
                                     in_=r_bf[0:1, 0:64]))
                t.wait_ge(DV.h, DV.n)
                for p in range(4):
                    PE.inc(t.matmul(psC[64:128, p * 512:(p + 1) * 512],
                                    ones_b[0:1, 0:64],
                                    r_bf[0:1, p * 512:(p + 1) * 512],
                                    start=True, stop=True))
                v.wait_ge(PE.h, PE.n)
                DV.inc(v.tensor_copy(out=Rbf[0:64, :], in_=psC[64:128, :]))
                rbf_evac_dv = DV.n
                if half == 0:
                    v.wait_ge(GD.h, cp_gd[cc % 2])
                    DV.inc(v.tensor_mul(out=cpair[cc % 2][0:64, :],
                                        in0=cbf[0:64, :], in1=Rbf[0:64, :]))
                else:
                    DV.inc(v.tensor_mul(out=cpair[cc % 2][64:128, :],
                                        in0=cbf[0:64, :], in1=Rbf[0:64, :]))
                    g.wait_ge(DV.h, DV.n)
                    cp_gd[cc % 2] = dma(
                        g, GD, c["ctx_s"][cc * 128:(cc + 1) * 128, :],
                        cpair[cc % 2][:])
            barrier()
            if c.get("dbg"):
                dma(g, GD,
                    c["dbg_ctxn"][:].rearrange("p (cc t) -> p cc t", t=S),
                    c["ctx_s"][:].rearrange("(cc p) t -> p cc t", p=128))
                g.wait_ge(GD.h, GD.n)
            if stop_after == "P3":
                return
            p3.close()
            pqkv.close()

            p4 = p34.enter_context(ExitStack())
            e4 = p4.enter_context
            xst = [e4(nc.sbuf_tensor(f"xst{i}", [128, D], F32))
                   for i in range(2)]
            yb = e4(nc.sbuf_tensor("yb", [128, D], F32))
            zb = e4(nc.sbuf_tensor("zb", [128, D], F32))
            ub = e4(nc.sbuf_tensor("ub", [128, D], F32))
            x1t = [e4(nc.sbuf_tensor(f"x1t{i}", [128, D], F32))
                   for i in range(2)]
            x1c = e4(nc.sbuf_tensor("x1c", [128, D], F32))
            sqj = e4(nc.sbuf_tensor("sqj", [128, D], F32))
            st6p4 = e4(nc.sbuf_tensor("st6p4", [128, 12], F32))
            agp4 = e4(nc.sbuf_tensor("agp4", [128, 2], F32))
            st = [e4(nc.sbuf_tensor(f"st{i}", [128, 1], F32))
                  for i in range(7)]
            wo_b2 = e4(nc.sbuf_tensor("wo_b2", [128, NDC * 1024], BF16))
            ctile = [e4(nc.sbuf_tensor(f"ctile{i}", [128, NDC * 128], BF16))
                     for i in range(2)]
            po = [e4(nc.psum_tensor(f"po{i}", [128, 1024], F32))
                  for i in range(2)]
            for cc in range(NDC):
                dma(g, GD, wo_b2[:, cc * 1024:(cc + 1) * 1024],
                    c["wo_d"][cc * 128:(cc + 1) * 128, :])
            t.wait_ge(GD.h, GD.n)

            x_loaded = [0] * NT
            ct_ld = [0] * NT
            ct_pe = [0, 0]

            def load_ctile(tt):
                for cc in range(NDC):
                    dma(sy, LD, ctile[tt % 2][:, cc * 128:(cc + 1) * 128],
                        c["ctx_s"][cc * 128:(cc + 1) * 128,
                                   tt * 128:(tt + 1) * 128])
                ct_ld[tt] = LD.n

            for tt in range(2):
                load_ctile(tt)
                x_loaded[tt] = dma(sy, LD, xst[tt % 2][:],
                                   c["x_d"][tt * 128:(tt + 1) * 128, :])
            ln_bufs = {"st6": st6p4, "ag": agp4, "mu_neg": st[2],
                       "t0": st[3], "t1": st[4], "junk": st[5],
                       "fj": c["junkf"]}
            spill_gd = [0, 0]
            y_dv = [0, 0]
            for tt in range(NT):
                if tt >= 2:
                    t.wait_ge(DV.h, y_dv[tt % 2])
                pp4 = po[tt % 2]
                t.wait_ge(LD.h, ct_ld[tt])
                for cc in range(NDC):
                    for p in range(2):
                        PE.inc(t.matmul(
                            pp4[:, p * 512:(p + 1) * 512],
                            ctile[tt % 2][:, cc * 128:(cc + 1) * 128],
                            wo_b2[:, cc * 1024 + p * 512:
                                  cc * 1024 + (p + 1) * 512],
                            start=(cc == 0), stop=False))
                ct_pe[tt % 2] = PE.n
                for p in range(2):
                    PE.inc(t.matmul(pp4[:, p * 512:(p + 1) * 512], ones_b[:],
                                    c["bo_b"][0:1, p * 512:(p + 1) * 512],
                                    start=False, stop=True))
                v.wait_ge(PE.h, PE.n)
                v.wait_ge(LD.h, x_loaded[tt])
                DV.inc(v.tensor_add(out=yb[:], in0=pp4[:],
                                    in1=xst[tt % 2][:]))
                y_dv[tt % 2] = DV.n
                if c.get("dbg"):
                    g.wait_ge(DV.h, DV.n)
                    dma(g, GD, c["dbg_y"][tt * 128:(tt + 1) * 128, :], yb[:])
                emit_ln(nc, DV, AC, yb[:], zb[:], D, EPS_LN1, ln_bufs)
                v.wait_ge(AC.h, AC.n)
                if c.get("dbg"):
                    g.wait_ge(AC.h, AC.n)
                    dma(g, GD, c["dbg_z"][tt * 128:(tt + 1) * 128, :], zb[:])
                    v.wait_ge(GD.h, GD.n)
                v.scalar_tensor_tensor(out=sqj[:], in0=zb[:], scalar=1.0,
                                       in1=gb[:], op0=ALU.mult,
                                       op1=ALU.mult)
                v.tensor_add(out=zb[:], in0=sqj[:], in1=bb[:])
                DV.inc(v.tensor_add(out=ub[:], in0=zb[:], in1=xst[tt % 2][:]))
                if tt + 2 < NT:
                    sy.wait_ge(DV.h, DV.n)
                    sy.wait_ge(PE.h, ct_pe[tt % 2])
                    load_ctile(tt + 2)
                    x_loaded[tt + 2] = dma(sy, LD, xst[tt % 2][:],
                                           c["x_d"][(tt + 2) * 128:
                                                    (tt + 3) * 128, :])
                emit_ln(nc, DV, AC, ub[:], x1t[tt % 2][:], D, EPS_LN, ln_bufs)
                v.wait_ge(AC.h, AC.n)
                if tt >= 1:
                    v.wait_ge(GD.h, spill_gd[0])
                DV.inc(v.tensor_copy(out=x1c[:], in_=x1t[tt % 2][:]))
                g.wait_ge(DV.h, DV.n)
                spill_gd[0] = dma(g, GD,
                                  c["x1sp"][tt * 128:(tt + 1) * 128, :],
                                  x1c[:])
                if c.get("dbg"):
                    spill_gd[0] = dma(
                        g, GD, c["dbg_x1"][tt * 128:(tt + 1) * 128, :],
                        x1c[:])
            while not p0.done:
                p0.step()
            barrier()
            if stop_after == "P4":
                return

    p0stk.close()
    with ExitStack() as p5:
        e5 = p5.enter_context
        wgT = e5(nc.sbuf_tensor("wgT", [128, NDC * I], BF16))
        wuT = e5(nc.sbuf_tensor("wuT", [128, NDC * I], BF16))
        wdT = e5(nc.sbuf_tensor("wdT", [128, NIC * 1024], BF16))
        x1l = [e5(nc.sbuf_tensor(f"x1l{i}", [128, D], F32)) for i in range(2)]
        xq = [e5(nc.sbuf_tensor(f"xq{i}", [128, D], BF16)) for i in range(2)]
        xqT = [e5(nc.sbuf_tensor(f"xqT{i}", [128, D], BF16))
               for i in range(2)]
        silu_sb = e5(nc.sbuf_tensor("silu_sb", [128, IH], F32))
        h_sb = e5(nc.sbuf_tensor("h_sb", [128, IH], F32))
        hq = e5(nc.sbuf_tensor("hq", [128, IH], BF16))
        hqT = e5(nc.sbuf_tensor("hqT", [128, I], BF16))
        ub2 = e5(nc.sbuf_tensor("ub2", [128, D], F32))
        x2b = e5(nc.sbuf_tensor("x2b", [128, D], F32))
        x2c = e5(nc.sbuf_tensor("x2c", [128, D], F32))
        st6p5 = e5(nc.sbuf_tensor("st6p5", [128, 12], F32))
        agp5 = e5(nc.sbuf_tensor("agp5", [128, 2], F32))
        st2 = [e5(nc.sbuf_tensor(f"st2{i}", [128, 1], F32)) for i in range(7)]
        qsc2 = {k: e5(nc.sbuf_tensor("p5q" + k,
                                     [128, 32 if k in "MSR" else 512],
                                     BF16 if k == "sg" else F32))
                for k in ("M", "S", "R", "absx", "a", "c1", "c2", "sg")}
        dps = e5(nc.psum_tensor("dps", [128, 1024], F32))
        gps = e5(nc.psum_tensor("gps", [128, 1024], F32))
        gps2 = e5(nc.psum_tensor("gps2", [128, 1024], F32))
        tps2 = e5(nc.psum_tensor("tps2", [128, 2048], BF16))

        x1_ld = [0] * NT
        for tt in range(2):
            dma(sy, LD, x1l[tt % 2][:],
                c["x1sp"][tt * 128:(tt + 1) * 128, :])
        for cc in range(NDC):
            dma(sy, LD, wgT[:, cc * I:(cc + 1) * I],
                c["wgT_s"][cc * 128:(cc + 1) * 128, :])
            dma(sy, LD, wuT[:, cc * I:(cc + 1) * I],
                c["wuT_s"][cc * 128:(cc + 1) * 128, :])
        gu_ready = LD.n
        for ic in range(NIC):
            dma(sy, LD, wdT[:, ic * 1024:(ic + 1) * 1024],
                c["wdT_s"][ic * 128:(ic + 1) * 128, :])
        wd_ready = LD.n
        x1_ld = [0] * NT
        x1_ld[0] = x1_ld[1] = gu_ready
        t.wait_ge(LD.h, gu_ready)
        ln_bufs2 = {"st6": st6p5, "ag": agp5, "mu_neg": st2[2],
                    "t0": st2[3], "t1": st2[4], "junk": st2[5],
                    "fj": c["junkf"]}
        out_gd = [0, 0]
        gu_ps = (gps, gps2)
        gu_dv = {0: 0, 1: 0}
        xqT_dv = [0, 0]

        def quant_x1(tt):
            par = tt % 2
            v.wait_ge(LD.h, x1_ld[tt])
            s.wait_ge(LD.h, x1_ld[tt])
            g.wait_ge(LD.h, x1_ld[tt])
            if c.get("dbg"):
                g.wait_ge(LD.h, x1_ld[tt])
                dma(g, GD, c["dbg_x1l"][tt * 128:(tt + 1) * 128, :],
                    x1l[par][:])
            for qo in (0, 512):
                emit_quant(nc, DV, AC, x1l[par][:, qo:qo + 512],
                           xq[par][:, qo:qo + 512], 512, qsc2)
            if c.get("dbg"):
                g.wait_ge(DV.h, DV.n)
                dma(g, GD, c["dbg_xq"][tt * 128:(tt + 1) * 128, :], xq[par][:])
            dve_flush()
            return DV.n

        def transpose_xq(tt, dvq):
            par = tt % 2
            t.wait_ge(DV.h, dvq)
            for r in range(8):
                PE.inc(t.transpose(tps2[:, r * 128:(r + 1) * 128],
                                   xq[par][:, r * 128:(r + 1) * 128],
                                   ident[:]))
            v.wait_ge(PE.h, PE.n)
            DV.inc(v.tensor_copy(out=xqT[par][:], in_=tps2[:, 0:1024]))
            xqT_dv[par] = DV.n

        dvq0 = quant_x1(0)
        transpose_xq(0, dvq0)
        pieces6 = [(hf, poff, pw) for hf in range(2)
                   for (poff, pw) in _pieces(IH)]
        tr_pe = [0, 0]
        for tt in range(NT):
            par = tt % 2
            t.wait_ge(DV.h, xqT_dv[par])
            fl = [0] * 6

            def gu_piece(k):
                hf, poff, pw = pieces6[k]
                pp = gu_ps[k % 2]
                t.wait_ge(DV.h, gu_dv[k % 2])
                for wi, wTT in enumerate((wgT, wuT)):
                    for dcc in range(NDC):
                        PE.inc(t.matmul(
                            pp[:, wi * 512:wi * 512 + pw],
                            xqT[par][:, dcc * 128:(dcc + 1) * 128],
                            wTT[:, dcc * I + hf * IH + poff:
                                dcc * I + hf * IH + poff + pw],
                            start=(dcc == 0), stop=(dcc == NDC - 1)))
                s.wait_ge(PE.h, PE.n)
                AC.inc(s.activation(out=silu_sb[:, poff:poff + pw],
                                    in_=pp[:, 0:pw], func=AF.Silu))
                fl[k] = act_flush()

            def mq_piece(k):
                hf, poff, pw = pieces6[k]
                pp = gu_ps[k % 2]
                v.wait_ge(AC.h, fl[k])
                if k == 3:
                    v.wait_ge(PE.h, tr_pe[0])
                DV.inc(v.scalar_tensor_tensor(
                    out=h_sb[:, poff:poff + pw], in0=pp[:, 512:512 + pw],
                    scalar=1.0, in1=silu_sb[:, poff:poff + pw],
                    op0=ALU.mult, op1=ALU.mult))
                gu_dv[k % 2] = DV.n
                emit_quant(nc, DV, AC, h_sb[:, poff:poff + pw],
                           hq[:, poff:poff + pw], pw, qsc2, war_dv=DV.n)

            def hq_transposes(hf):
                t.wait_ge(DV.h, DV.n)
                for gi in range(2):
                    r0, r1 = gi * 8, min(gi * 8 + 8, 11)
                    bank = gi % 2
                    for r in range(r0, r1):
                        PE.inc(t.transpose(
                            tps2[:, bank * 1024 + (r - r0) * 128:
                                 bank * 1024 + (r - r0 + 1) * 128],
                            hq[:, r * 128:(r + 1) * 128], ident[:]))
                    v.wait_ge(PE.h, PE.n)
                    DV.inc(v.tensor_copy(
                        out=hqT[:, hf * IH + r0 * 128:hf * IH + r1 * 128],
                        in_=tps2[:, bank * 1024:
                                 bank * 1024 + (r1 - r0) * 128]))
                tr_pe[hf] = PE.n

            for k in range(6):
                if k >= 1:
                    mq_piece(k - 1)
                gu_piece(k)
                if k == 3:
                    if c.get("dbg"):
                        g.wait_ge(DV.h, DV.n)
                        dma(g, GD, c["dbg_hq"][tt * 128:(tt + 1) * 128,
                                               0:IH], hq[:])
                    hq_transposes(0)
            mq_piece(5)
            if c.get("dbg"):
                g.wait_ge(DV.h, DV.n)
                dma(g, GD, c["dbg_hq"][tt * 128:(tt + 1) * 128,
                                       IH:2 * IH], hq[:])
            dve_flush()
            hq_transposes(1)
            hqT_dv = DV.n
            dvq_next = None
            if tt + 1 < NT:
                dvq_next = quant_x1(tt + 1)
            t.wait_ge(DV.h, hqT_dv)
            if tt == 0:
                t.wait_ge(LD.h, wd_ready)
            for ic in range(NIC):
                for p in range(2):
                    PE.inc(t.matmul(
                        dps[:, p * 512:(p + 1) * 512],
                        hqT[:, ic * 128:(ic + 1) * 128],
                        wdT[:, ic * 1024 + p * 512:ic * 1024 + (p + 1) * 512],
                        start=(ic == 0), stop=(ic == NIC - 1)))
            down_pe = PE.n
            if dvq_next is not None:
                transpose_xq(tt + 1, dvq_next)
            v.wait_ge(PE.h, down_pe)
            DV.inc(v.tensor_add(out=ub2[:], in0=dps[:],
                                in1=x1l[par][:]))
            if c.get("dbg"):
                g.wait_ge(DV.h, DV.n)
                dma(g, GD, c["dbg_u"][tt * 128:(tt + 1) * 128, :], ub2[:])
                v.wait_ge(GD.h, GD.n)
            if tt + 2 < NT:
                sy.wait_ge(DV.h, DV.n)
                sy.wait_ge(AC.h, AC.n)
                x1_ld[tt + 2] = dma(sy, LD, x1l[par][:],
                                    c["x1sp"][(tt + 2) * 128:
                                              (tt + 3) * 128, :])
            emit_ln(nc, DV, AC, ub2[:], x2b[:], D, EPS_LN, ln_bufs2)
            v.wait_ge(AC.h, AC.n)
            if tt >= 1:
                v.wait_ge(GD.h, out_gd[0])
            DV.inc(v.tensor_copy(out=x2c[:], in_=x2b[:]))
            g.wait_ge(DV.h, DV.n)
            out_gd[0] = dma(g, GD, c["out_d"][tt * 128:(tt + 1) * 128, :],
                            x2c[:])
        g.wait_ge(GD.h, GD.n)
        sy.wait_ge(GD.h, GD.n)


def build_nc(stop_after=None, p0_quant_only=False, dbg=False):
    nc = bass.Bass("TRN2", target_bir_lowering=False)
    c = {"stop_after": stop_after, "p0_quant_only": p0_quant_only, "dbg": dbg}
    c["x_d"] = nc.dram_tensor("x", [S, D], F32, kind="ExternalInput").ap()
    c["xT_d"] = nc.dram_tensor("xT", [D, S], F32, kind="ExternalInput").ap()
    c["wq_d"] = nc.dram_tensor("wq", [D, HK], F32, kind="ExternalInput").ap()
    c["wk_d"] = nc.dram_tensor("wk", [D, HK], F32, kind="ExternalInput").ap()
    c["wv_d"] = nc.dram_tensor("wv", [D, HK], F32, kind="ExternalInput").ap()
    c["bq_d"] = nc.dram_tensor("bq", [128, 8], F32, kind="ExternalInput").ap()
    c["bk_d"] = nc.dram_tensor("bk", [128, 8], F32, kind="ExternalInput").ap()
    c["bv_d"] = nc.dram_tensor("bv", [1, HK], F32, kind="ExternalInput").ap()
    c["wo_d"] = nc.dram_tensor("wo", [HK, D], F32, kind="ExternalInput").ap()
    c["bo_d"] = nc.dram_tensor("bo", [1, D], F32, kind="ExternalInput").ap()
    c["gam_d"] = nc.dram_tensor("gam", [1, D], F32, kind="ExternalInput").ap()
    c["bet_d"] = nc.dram_tensor("bet", [1, D], F32, kind="ExternalInput").ap()
    c["wg_d"] = nc.dram_tensor("wg", [I, D], F32, kind="ExternalInput").ap()
    c["wu_d"] = nc.dram_tensor("wu", [I, D], F32, kind="ExternalInput").ap()
    c["wd_d"] = nc.dram_tensor("wd", [D, I], F32, kind="ExternalInput").ap()
    c["out_d"] = nc.dram_tensor("out", [S, D], F32, kind="ExternalOutput").ap()
    c["x1sp"] = nc.dram_tensor("x1sp", [S, D], F32, kind="Internal").ap()
    c["wgT_s"] = nc.dram_tensor("wgT_s", [D, I], BF16, kind="Internal").ap()
    c["wuT_s"] = nc.dram_tensor("wuT_s", [D, I], BF16, kind="Internal").ap()
    c["wdT_s"] = nc.dram_tensor("wdT_s", [I, D], BF16, kind="Internal").ap()
    c["t16"] = nc.dram_tensor("t16", [1, 16], F32, kind="Internal").ap()
    c["ctx_s"] = nc.dram_tensor("ctx_s", [HK, S], BF16, kind="Internal").ap()
    if c.get("dbg"):
        c["dbg_qh"] = nc.dram_tensor("dbg_qh", [128, NDC * S], BF16,
                                     kind="ExternalOutput").ap()
        c["dbg_kh"] = nc.dram_tensor("dbg_kh", [128, NDC * S], BF16,
                                     kind="ExternalOutput").ap()
        c["dbg_vsb"] = nc.dram_tensor("dbg_vsb", [128, NT * NH * 65], BF16,
                                      kind="ExternalOutput").ap()
        c["dbg_mh"] = nc.dram_tensor("dbg_mh", [128, 16], F32,
                                     kind="ExternalOutput").ap()
        c["dbg_m2"] = nc.dram_tensor("dbg_m2", [1, 16], BF16,
                                     kind="ExternalOutput").ap()
        c["dbg_qn"] = nc.dram_tensor("dbg_qn", [16, 1], F32,
                                     kind="ExternalOutput").ap()
        c["dbg_kn"] = nc.dram_tensor("dbg_kn", [16, 1], F32,
                                     kind="ExternalOutput").ap()
        c["dbg_pn"] = nc.dram_tensor("dbg_pn", [16, 2048], F32,
                                     kind="ExternalOutput").ap()
        c["dbg_m2a"] = nc.dram_tensor("dbg_m2a", [16, 1], F32,
                                      kind="ExternalOutput").ap()
        c["dbg_msq"] = nc.dram_tensor("dbg_msq", [16, 1], F32,
                                      kind="ExternalOutput").ap()
        c["dbg_m2b"] = nc.dram_tensor("dbg_m2b", [16, 1], F32,
                                      kind="ExternalOutput").ap()
        c["dbg_mrow"] = nc.dram_tensor("dbg_mrow", [1, 16], F32,
                                       kind="ExternalOutput").ap()
        c["dbg_ctxn"] = nc.dram_tensor("dbg_ctxn", [128, NDC * S], BF16,
                                       kind="ExternalOutput").ap()
        c["dbg_x1"] = nc.dram_tensor("dbg_x1", [S, D], F32,
                                     kind="ExternalOutput").ap()
        c["dbg_y"] = nc.dram_tensor("dbg_y", [S, D], F32,
                                    kind="ExternalOutput").ap()
        c["dbg_z"] = nc.dram_tensor("dbg_z", [S, D], F32,
                                    kind="ExternalOutput").ap()
        c["dbg_xq"] = nc.dram_tensor("dbg_xq", [S, D], BF16,
                                     kind="ExternalOutput").ap()
        c["dbg_hq"] = nc.dram_tensor("dbg_hq", [S, I], BF16,
                                     kind="ExternalOutput").ap()
        c["dbg_u"] = nc.dram_tensor("dbg_u", [S, D], F32,
                                    kind="ExternalOutput").ap()
        c["dbg_x1l"] = nc.dram_tensor("dbg_x1l", [S, D], F32,
                                      kind="ExternalOutput").ap()

    with ExitStack() as top:
        e = top.enter_context
        c["ident"] = e(nc.sbuf_tensor("ident", [128, 128], BF16))
        c["ones_f"] = e(nc.sbuf_tensor("ones_f", [1, 128], F32))
        c["ones_b"] = e(nc.sbuf_tensor("ones_b", [1, 128], BF16))
        c["Mh"] = e(nc.sbuf_tensor("Mh", [128, 16], F32))
        c["bqc"] = e(nc.sbuf_tensor("bqc", [128, 8], F32))
        c["bkc"] = e(nc.sbuf_tensor("bkc", [128, 8], F32))
        c["bv_b"] = e(nc.sbuf_tensor("bv_b", [1, HK], BF16))
        c["bo_b"] = e(nc.sbuf_tensor("bo_b", [1, D], BF16))
        c["rowtmp_bo"] = e(nc.sbuf_tensor("rowtmp_bo", [1, D], F32))
        c["junkf"] = e(nc.sbuf_tensor("junkf", [128, 32], F32))
        c["junkv"] = e(nc.sbuf_tensor("junkv", [128, 64], F32))
        c["LD"] = Sem(e(nc.semaphore("sLD")))
        c["GD"] = Sem(e(nc.semaphore("sGD")))
        c["PE"] = Sem(e(nc.semaphore("sPE")))
        c["AC"] = Sem(e(nc.semaphore("sAC")))
        c["DV"] = Sem(e(nc.semaphore("sDV")))
        c["GP"] = Sem(e(nc.semaphore("sGP")))
        block = e(nc.Block())

        @block.vector
        def _(_v):
            emit_all(nc, c)
    return nc


_NC_CACHE = {}


def kernel(L0_attn_x, L0_attn_Wq, L0_attn_bq, L0_attn_Wk, L0_attn_bk,
           L0_attn_Wv, L0_attn_bv, L0_attn_Wo, L0_attn_bo, L0_attn_gamma,
           L0_attn_beta, L0_ffn_w_gate, L0_ffn_w_up, L0_ffn_w_down):
    x = np.ascontiguousarray(np.asarray(L0_attn_x, dtype=np.float32))
    B = x.shape[0]
    wq = np.ascontiguousarray(np.asarray(L0_attn_Wq, np.float32).reshape(D, HK))
    wk = np.ascontiguousarray(np.asarray(L0_attn_Wk, np.float32).reshape(D, HK))
    wv = np.ascontiguousarray(np.asarray(L0_attn_Wv, np.float32).reshape(D, HK))
    bq = np.ascontiguousarray(np.asarray(L0_attn_bq, np.float32).reshape(8, 128).T)
    bk = np.ascontiguousarray(np.asarray(L0_attn_bk, np.float32).reshape(8, 128).T)
    bv = np.ascontiguousarray(np.asarray(L0_attn_bv, np.float32).reshape(1, HK))
    wo = np.ascontiguousarray(np.asarray(L0_attn_Wo, np.float32).reshape(HK, D))
    bo = np.ascontiguousarray(np.asarray(L0_attn_bo, np.float32).reshape(1, D))
    gam = np.ascontiguousarray(np.asarray(L0_attn_gamma, np.float32).reshape(1, D))
    bet = np.ascontiguousarray(np.asarray(L0_attn_beta, np.float32).reshape(1, D))
    wg = np.ascontiguousarray(np.asarray(L0_ffn_w_gate, np.float32))
    wu = np.ascontiguousarray(np.asarray(L0_ffn_w_up, np.float32))
    wd = np.ascontiguousarray(np.asarray(L0_ffn_w_down, np.float32))

    if "nc" not in _NC_CACHE:
        _NC_CACHE["nc"] = build_nc()
    nc = _NC_CACHE["nc"]
    in_maps = []
    for b in range(B):
        xb = np.ascontiguousarray(x[b])
        in_maps.append({
            "x": xb, "xT": np.ascontiguousarray(xb.T),
            "wq": wq, "wk": wk, "wv": wv, "bq": bq, "bk": bk, "bv": bv,
            "wo": wo, "bo": bo, "gam": gam, "bet": bet,
            "wg": wg, "wu": wu, "wd": wd,
        })
    res = run_bass_kernel_spmd(nc, in_maps, core_ids=list(range(B)))
    return np.stack([np.asarray(r["out"], dtype=np.float32)
                     for r in res.results], axis=0)


# version 17 (final)
# speedup vs baseline: 1.0563x; 1.0444x over previous
import numpy as np
from contextlib import ExitStack

import concourse.bass as bass
import concourse.mybir as mybir
from concourse.bass_utils import run_bass_kernel_spmd

F32 = mybir.dt.float32
BF16 = mybir.dt.bfloat16
AF = mybir.ActivationFunctionType
ALU = mybir.AluOpType
AX = mybir.AxisListType.X

S = 2048
D = 1024
HK = 1024
NH = 16
I = 2816
NT = 16
NDC = 8
NIC = 22
EPS_LN1 = 1e-3
EPS_LN = 1e-5
IH = 1408


class Sem:
    def __init__(self, h):
        self.h = h
        self.n = 0

    def inc(self, inst, k=1):
        inst.then_inc(self.h, k)
        self.n += k
        return self.n


def _pieces(width, step=512):
    out, off = [], 0
    while off < width:
        w = min(step, width - off)
        out.append((off, w))
        off += w
    return out


def emit_quant(nc, DV, AC, x_ap, out_ap, W, sc, war_dv=None):
    G = W // 16
    v = nc.vector
    M, Sb, R = sc["M"], sc["S"], sc["R"]
    absx, a, c1, c2, sg = sc["absx"], sc["a"], sc["c1"], sc["c2"], sc["sg"]
    nc.scalar.wait_ge(DV.h, DV.n if war_dv is None else war_dv)
    AC.inc(nc.scalar.sign(out=sg[:, 0:W], in_=x_ap))
    ac_sign = AC.n
    AC.inc(nc.scalar.activation(out=absx[:, 0:W], in_=x_ap, func=AF.Abs))
    ac_absx = AC.n
    xv = x_ap.rearrange("p (g k) -> p g k", k=16)
    DV.inc(v.tensor_reduce(out=M[:, 0:G], in_=xv, axis=AX, op=ALU.max,
                           apply_absolute_value=True))
    v.wait_ge(DV.h, DV.n)
    DV.inc(v.tensor_scalar(out=Sb[:, 0:G], in0=M[:, 0:G], scalar1=1.0 / 6.0,
                           scalar2=1e-12, op0=ALU.mult, op1=ALU.max))
    v.wait_ge(DV.h, DV.n)
    DV.inc(v.reciprocal(out=R[:, 0:G], in_=Sb[:, 0:G]))
    v.wait_ge(DV.h, DV.n)
    v.wait_ge(AC.h, ac_absx)
    Rb = R[:, 0:G].unsqueeze(2).broadcast_to([128, G, 16])
    Sbb = Sb[:, 0:G].unsqueeze(2).broadcast_to([128, G, 16])
    v.tensor_tensor(out=a[:, 0:W].rearrange("p (g k) -> p g k", k=16),
                    in0=absx[:, 0:W].rearrange("p (g k) -> p g k", k=16),
                    in1=Rb, op=ALU.mult)
    v.tensor_scalar(out=c1[:, 0:W], in0=a[:, 0:W], scalar1=0.25, scalar2=None,
                    op0=ALU.is_gt)
    v.scalar_tensor_tensor(out=c2[:, 0:W], in0=a[:, 0:W], scalar=0.75,
                           in1=c1[:, 0:W], op0=ALU.is_gt, op1=ALU.add)
    v.scalar_tensor_tensor(out=c1[:, 0:W], in0=a[:, 0:W], scalar=1.25,
                           in1=c2[:, 0:W], op0=ALU.is_gt, op1=ALU.add)
    v.scalar_tensor_tensor(out=c2[:, 0:W], in0=a[:, 0:W], scalar=1.75,
                           in1=c1[:, 0:W], op0=ALU.is_gt, op1=ALU.add)
    v.tensor_scalar(out=c1[:, 0:W], in0=a[:, 0:W], scalar1=5.0, scalar2=2.0,
                    op0=ALU.is_gt, op1=ALU.mult)
    v.scalar_tensor_tensor(out=absx[:, 0:W], in0=a[:, 0:W], scalar=3.5,
                           in1=c1[:, 0:W], op0=ALU.is_gt, op1=ALU.add)
    v.scalar_tensor_tensor(out=c1[:, 0:W], in0=a[:, 0:W], scalar=2.5,
                           in1=absx[:, 0:W], op0=ALU.is_gt, op1=ALU.add)
    v.scalar_tensor_tensor(out=absx[:, 0:W], in0=c2[:, 0:W], scalar=0.5,
                           in1=c1[:, 0:W], op0=ALU.mult, op1=ALU.add)
    v.scalar_tensor_tensor(out=a[:, 0:W].rearrange("p (g k) -> p g k", k=16),
                           in0=absx[:, 0:W].rearrange("p (g k) -> p g k", k=16),
                           scalar=1.0, in1=Sbb, op0=ALU.mult, op1=ALU.mult)
    v.wait_ge(AC.h, ac_sign)
    DV.inc(v.tensor_mul(out=out_ap, in0=a[:, 0:W], in1=sg[:, 0:W]))
    return DV.n, ac_sign


def emit_ln(nc, DV, AC, y_ap, z_out, W, eps, B):
    v, s = nc.vector, nc.scalar
    s.wait_ge(DV.h, DV.n)
    nst = W // 512
    for j in range(nst):
        DV.inc(v.bn_stats(out=B["st6"][:, j * 6:(j + 1) * 6],
                          in_=y_ap[:, j * 512:(j + 1) * 512]))
    v.wait_ge(DV.h, DV.n)
    DV.inc(v.bn_aggr(out=B["ag"][:, 0:2], in_=B["st6"][:, 0:6 * nst]))
    v.wait_ge(DV.h, DV.n)
    DV.inc(v.tensor_scalar(out=B["t0"][:], in0=B["ag"][:, 1:2], scalar1=eps,
                           scalar2=None, op0=ALU.add))
    v.wait_ge(DV.h, DV.n)
    DV.inc(v.reciprocal(out=B["t1"][:], in_=B["t0"][:]))
    v.wait_ge(DV.h, DV.n)
    DV.inc(v.tensor_scalar(out=B["mu_neg"][:], in0=B["ag"][:, 0:1],
                           scalar1=-1.0, scalar2=None, op0=ALU.mult))
    DV.inc(v.tensor_copy(out=B["junk"][:], in_=B["t1"][:]))
    DV.inc(v.tensor_copy(out=B["junk"][:], in_=B["mu_neg"][:]))
    s.wait_ge(DV.h, DV.n)
    AC.inc(s.activation(out=B["t0"][:], in_=B["t1"][:], func=AF.Sqrt))
    s.wait_ge(AC.h, AC.n)
    AC.inc(s.activation(out=B["t1"][:], in_=B["mu_neg"][:], func=AF.Identity,
                        scale=B["t0"][:, 0:1]))
    s.wait_ge(AC.h, AC.n)
    AC.inc(s.activation(out=z_out, in_=y_ap, func=AF.Identity,
                        scale=B["t0"][:, 0:1], bias=B["t1"][:, 0:1]))
    AC.inc(s.activation(out=B["fj"][:], in_=B["fj"][:], func=AF.Square))
    return AC.n


class P0Stream:

    def __init__(self, nc, c, stk, LD, GD, PE, AC, DV):
        e = stk.enter_context
        self.nc, self.c = nc, c
        self.LD, self.GD, self.PE, self.AC, self.DV = LD, GD, PE, AC, DV
        self.raw = [e(nc.sbuf_tensor(f"p0raw{i}", [128, 1024], F32))
                    for i in range(2)]
        self.qsc = {k: e(nc.sbuf_tensor(
            "p0q" + k, [128, 64 if k in ("M", "S", "R") else 1024],
            BF16 if k == "sg" else F32))
            for k in ("M", "S", "R", "absx", "a", "c1", "c2", "sg")}
        self.wbf = [e(nc.sbuf_tensor(f"p0wbf{i}", [128, 1024], BF16))
                    for i in range(2)]
        self.stg = e(nc.sbuf_tensor("p0stg", [128, 1024], BF16))
        self.tps = e(nc.psum_tensor("p0tps", [128, 2048], BF16))
        self.units = []
        for w_d, wT_s in ((c["wg_d"], c["wgT_s"]), (c["wu_d"], c["wuT_s"])):
            for r in range(I // 128):
                self.units.append((
                    w_d[r * 128:(r + 1) * 128, :], 8,
                    wT_s[0:1024, r * 128:(r + 1) * 128]
                    .rearrange("(j a) b -> a j b", j=8)))
        for r in range(8):
            for j0, j1 in ((0, 8), (8, 16), (16, 22)):
                self.units.append((
                    c["wd_d"][r * 128:(r + 1) * 128, j0 * 128:j1 * 128],
                    j1 - j0,
                    c["wdT_s"][j0 * 128:j1 * 128, r * 128:(r + 1) * 128]
                    .rearrange("(j a) b -> a j b", j=j1 - j0)))
        self.n_units = len(self.units)
        self.k = 0
        self.ld = [0] * self.n_units
        self.dvv = [0] * self.n_units
        self.acv = [0] * self.n_units
        self.pe_tr = [0, 0]
        self.evac_dv = [0, 0]
        self.gd = [0, 0]
        sy = nc.sync
        for i in range(min(2, self.n_units)):
            src, nblk, _ = self.units[i]
            sy.dma_start(out=self.raw[i % 2][:, 0:nblk * 128],
                         in_=src).then_inc(LD.h, 16)
            LD.n += 16
            self.ld[i] = LD.n

    @property
    def done(self):
        return self.k > self.n_units

    def step(self):
        if self.done:
            return
        nc = self.nc
        v, t, g, sy = nc.vector, nc.tensor, nc.gpsimd, nc.sync
        s = nc.scalar
        LD, GD, PE, AC, DV = self.LD, self.GD, self.PE, self.AC, self.DV
        k = self.k
        if k < self.n_units:
            src, nblk, _ = self.units[k]
            w = nblk * 128
            v.wait_ge(LD.h, self.ld[k])
            s.wait_ge(LD.h, self.ld[k])
            v.wait_ge(PE.h, self.pe_tr[k % 2])
            dvv, acv = emit_quant(nc, DV, AC, self.raw[k % 2][:, 0:w],
                                  self.wbf[k % 2][:, 0:w], w, self.qsc,
                                  war_dv=self.dvv[k - 1] if k else 0)
            self.dvv[k], self.acv[k] = dvv, acv
            if k + 2 < self.n_units:
                src2, nblk2, _ = self.units[k + 2]
                sy.wait_ge(DV.h, dvv)
                sy.wait_ge(AC.h, acv)
                sy.dma_start(out=self.raw[k % 2][:, 0:nblk2 * 128],
                             in_=src2).then_inc(LD.h, 16)
                LD.n += 16
                self.ld[k + 2] = LD.n
        if k >= 1:
            j = k - 1
            _, nblk, out_ap = self.units[j]
            w = nblk * 128
            par = j % 2
            t.wait_ge(DV.h, self.dvv[j])
            for b in range(nblk):
                PE.inc(t.transpose(
                    self.tps[:, par * 1024 + b * 128:par * 1024 + (b + 1) * 128],
                    self.wbf[par][:, b * 128:(b + 1) * 128],
                    self.c["ident"][:]))
            self.pe_tr[par] = PE.n
            v.wait_ge(PE.h, self.pe_tr[par])
            v.wait_ge(GD.h, self.gd[0])
            DV.inc(v.tensor_copy(out=self.stg[:, 0:w],
                                 in_=self.tps[:, par * 1024:par * 1024 + w]))
            self.evac_dv[par] = DV.n
            g.wait_ge(DV.h, self.evac_dv[par])
            g.dma_start(out=out_ap,
                        in_=self.stg[:, 0:w]
                        .rearrange("p (j b) -> p j b", j=nblk)
                        ).then_inc(GD.h, 16)
            GD.n += 16
            self.gd[0] = GD.n
        self.k += 1


def emit_all(nc, c):
    stop_after = c.get("stop_after")
    LD, GD, PE, AC, DV, GP = (c["LD"], c["GD"], c["PE"], c["AC"], c["DV"],
                              c["GP"])
    v, s, t, g, sy = nc.vector, nc.scalar, nc.tensor, nc.gpsimd, nc.sync
    ident, ones_f, ones_b, Mh = c["ident"], c["ones_f"], c["ones_b"], c["Mh"]

    def barrier():
        PE.inc(t.nop())
        AC.inc(s.nop())
        DV.inc(v.nop())
        GP.inc(g.nop())
        vals = [(PE, PE.n), (AC, AC.n), (DV, DV.n), (GP, GP.n),
                (LD, LD.n), (GD, GD.n)]
        own = {id(t): PE, id(s): AC, id(v): DV, id(g): GP}
        for eng in (t, s, v, g, sy):
            mine = own.get(id(eng))
            for sem, val in vals:
                if sem is mine:
                    continue
                eng.wait_ge(sem.h, val)

    def act_flush():
        AC.inc(s.activation(out=c["junkf"][:], in_=c["junkf"][:],
                            func=AF.Square))
        return AC.n

    def dve_flush():
        DV.inc(v.tensor_copy(out=c["junkv"][:], in_=c["junkv"][:]))
        return DV.n

    def dma(eng, sem, out, in_):
        eng.dma_start(out=out, in_=in_).then_inc(sem.h, 16)
        sem.n += 16
        return sem.n

    g.memset(ident[:], 0.0)
    g.affine_select(out=ident[:], in_=ident[:], compare_op=ALU.not_equal,
                    fill=1.0, base=0, pattern=[[-1, 128]], channel_multiplier=1)
    g.memset(ones_f[:], 1.0)
    g.memset(c["junkf"][:], 1.0)
    g.memset(c["junkv"][:], 1.0)
    GP.inc(g.memset(ones_b[:], 1.0))
    dma(sy, LD, c["bqc"][:], c["bq_d"][:])
    dma(sy, LD, c["bkc"][:], c["bk_d"][:])

    if stop_after == "INIT":
        barrier()
        return
    p0stk = ExitStack()
    p0 = P0Stream(nc, c, p0stk, LD, GD, PE, AC, DV)
    if stop_after in ("P0", "P0w1"):
        while not p0.done:
            p0.step()
        barrier()
        p0stk.close()
        return

    with ExitStack() as p2:
        e = p2.enter_context
        gb = e(nc.sbuf_tensor("gb", [128, D], F32))
        bb = e(nc.sbuf_tensor("bb", [128, D], F32))
        pqkv = e(ExitStack())
        eq = pqkv.enter_context
        qh = eq(nc.sbuf_tensor("qh", [128, NDC * S], BF16))
        kh = eq(nc.sbuf_tensor("kh", [128, NDC * S], BF16))
        vsb = eq(nc.sbuf_tensor("vsb", [128, NT * NH * 65], BF16))
        mh_ready = 0

        with ExitStack() as p2a:
            e2 = p2a.enter_context
            qn = e2(nc.sbuf_tensor("qn", [128, 1], F32))
            kn = e2(nc.sbuf_tensor("kn", [128, 1], F32))
            m2 = e2(nc.sbuf_tensor("m2", [128, 1], F32))
            msq = e2(nc.sbuf_tensor("msq", [128, 1], F32))
            mhe = e2(nc.sbuf_tensor("mhe", [128, 16], F32))
            xT_b = e2(nc.sbuf_tensor("xT_b", [128, NDC * S], BF16))
            wsl = e2(nc.sbuf_tensor("wsl", [128, 8 * 1024], BF16))
            xTf = e2(nc.sbuf_tensor("xTf", [128, 1024], F32))
            q2 = e2(nc.sbuf_tensor("q2", [128, 1024], BF16))
            masks = e2(nc.sbuf_tensor("masks", [128, 8 * 128], BF16))
            dma(g, GD, c["bv_b"][:], c["bv_d"][:])
            dma(g, GD, c["bo_b"][:], c["bo_d"][:])

            ldg = dma(sy, LD, xTf[0:1, :], c["gam_d"][:])
            with nc.psum_tensor("pgb", [128, 1024], F32) as pgb:
                t.wait_ge(LD.h, ldg)
                t.wait_ge(GP.h, GP.n)
                for p in range(2):
                    PE.inc(t.matmul(pgb[:, p * 512:(p + 1) * 512], ones_f[:],
                                    xTf[0:1, p * 512:(p + 1) * 512],
                                    start=True, stop=True))
                v.wait_ge(PE.h, PE.n)
                DV.inc(v.tensor_copy(out=gb[:], in_=pgb[:]))
                sy.wait_ge(PE.h, PE.n)
                dma(sy, LD, xTf[0:1, :], c["bet_d"][:])
                ldb = dma(sy, LD, c["rowtmp_bo"][:], c["bo_d"][:])
                t.wait_ge(DV.h, DV.n)
                t.wait_ge(LD.h, ldb)
                for p in range(2):
                    PE.inc(t.matmul(pgb[:, p * 512:(p + 1) * 512], ones_f[:],
                                    xTf[0:1, p * 512:(p + 1) * 512],
                                    start=True, stop=True))
                v.wait_ge(PE.h, PE.n)
                DV.inc(v.tensor_copy(out=bb[:], in_=pgb[:]))
                gbb_pe = PE.n

            g.memset(masks[:], 0.0)
            for cc in range(8):
                g.memset(masks[0:64, cc * 128 + 2 * cc:
                               cc * 128 + 2 * cc + 1], 1.0)
                g.memset(masks[64:128, cc * 128 + 2 * cc + 1:
                               cc * 128 + 2 * cc + 2], 1.0)
            vv = vsb[:].rearrange("p (sc h w) -> p sc h w", h=NH, w=65)
            GP.inc(g.memset(vv[:, :, :, 64:65], 1.0))

            sy.wait_ge(PE.h, gbb_pe)
            for dcc in range(NDC):
                for hf2 in range(2):
                    if dcc + hf2 >= 1:
                        sy.wait_ge(AC.h, AC.n)
                    ldv = dma(sy, LD, xTf[:],
                              c["xT_d"][dcc * 128:(dcc + 1) * 128,
                                        hf2 * 1024:(hf2 + 1) * 1024])
                    s.wait_ge(LD.h, ldv)
                    AC.inc(s.activation(
                        out=xT_b[:, dcc * S + hf2 * 1024:
                                 dcc * S + (hf2 + 1) * 1024],
                        in_=xTf[:], func=AF.Copy))
            for dcc in range(NDC):
                dma(g, GD, wsl[:, dcc * 1024:(dcc + 1) * 1024],
                    c["wq_d"][dcc * 128:(dcc + 1) * 128, :])

            t.wait_ge(AC.h, AC.n)
            t.wait_ge(GD.h, GD.n)
            v.wait_ge(LD.h, LD.n)
            qk_dv = 0
            with nc.psum_tensor("pq", [128, 2048], F32) as pq:
                for wi, (dst, bias) in enumerate(((qh, c["bqc"]),
                                                  (kh, c["bkc"]))):
                    if wi == 1:
                        g.wait_ge(PE.h, PE.n)
                        for dcc in range(NDC):
                            dma(g, GD, wsl[:, dcc * 1024:(dcc + 1) * 1024],
                                c["wk_d"][dcc * 128:(dcc + 1) * 128, :])
                        t.wait_ge(GD.h, GD.n)
                    for cc in range(NDC):
                        if wi > 0 or cc > 0:
                            t.wait_ge(DV.h, qk_dv)
                        for dcc in range(NDC):
                            for p in range(4):
                                PE.inc(t.matmul(
                                    pq[:, p * 512:(p + 1) * 512],
                                    wsl[:, dcc * 1024 + cc * 128:
                                        dcc * 1024 + cc * 128 + 128],
                                    xT_b[:, dcc * S + p * 512:
                                         dcc * S + (p + 1) * 512],
                                    start=(dcc == 0), stop=(dcc == NDC - 1)))
                        v.wait_ge(PE.h, PE.n)
                        for p in range(4):
                            DV.inc(v.tensor_scalar(
                                out=dst[:, cc * S + p * 512:
                                        cc * S + (p + 1) * 512],
                                in0=pq[:, p * 512:(p + 1) * 512],
                                scalar1=bias[:, cc:cc + 1], scalar2=None,
                                op0=ALU.add))
                        qk_dv = DV.n
                        p0.step()

            g.wait_ge(PE.h, PE.n)
            for dcc in range(NDC):
                dma(g, GD, wsl[:, dcc * 1024:(dcc + 1) * 1024],
                    c["wv_d"][dcc * 128:(dcc + 1) * 128, :])
            wv_gd = GD.n

            with ExitStack() as pnx:
                pn = pnx.enter_context(nc.psum_tensor("pn", [128, 2048], F32))
                pm = pnx.enter_context(nc.psum_tensor("pm", [128, 16], F32))
                pmt = pnx.enter_context(nc.psum_tensor("pmt", [1, 128], BF16))
                mb = pnx.enter_context(nc.sbuf_tensor("mb", [128, 1], BF16))
                mrow_b = pnx.enter_context(nc.sbuf_tensor("mrow_b", [1, 128], BF16))
                for src, dstn in ((qh, qn), (kh, kn)):
                    for cc in range(NDC):
                        for h2 in range(2):
                            if cc + h2 > 0:
                                v.wait_ge(PE.h, PE.n)
                            DV.inc(v.tensor_mul(
                                out=q2[:],
                                in0=src[:, cc * S + h2 * 1024:
                                        cc * S + (h2 + 1) * 1024],
                                in1=src[:, cc * S + h2 * 1024:
                                        cc * S + (h2 + 1) * 1024]))
                            t.wait_ge(DV.h, DV.n)
                            for p in range(2):
                                PE.inc(t.matmul(
                                    pn[:, h2 * 1024 + p * 512:
                                       h2 * 1024 + (p + 1) * 512],
                                    masks[:, cc * 128:(cc + 1) * 128],
                                    q2[:, p * 512:(p + 1) * 512],
                                    start=(cc == 0),
                                    stop=(cc == NDC - 1)))
                    v.wait_ge(PE.h, PE.n)
                    DV.inc(v.tensor_reduce(out=dstn[:], in_=pn[:],
                                           axis=AX, op=ALU.max))
                    v.wait_ge(DV.h, DV.n)
                if c.get("dbg"):
                    g.wait_ge(DV.h, DV.n)
                    dma(g, GD, c["dbg_qn"][:], qn[0:16, 0:1])
                    dma(g, GD, c["dbg_kn"][:], kn[0:16, 0:1])
                DV.inc(v.tensor_add(out=msq[:], in0=qn[:], in1=kn[:]))
                v.wait_ge(DV.h, DV.n)
                DV.inc(v.tensor_scalar(out=m2[:], in0=msq[:],
                                       scalar1=-0.0625, scalar2=None,
                                       op0=ALU.mult))
                v.wait_ge(DV.h, DV.n)
                v.tensor_copy(out=mb[:], in_=m2[:])
                DV.inc(v.tensor_copy(out=q2[:, 0:1024], in_=qh[:, 0:1024]))
                t.wait_ge(DV.h, DV.n)
                PE.inc(t.transpose(pmt[0:1, 0:128], mb[:, 0:1],
                                   ident[:]))
                v.wait_ge(PE.h, PE.n)
                v.tensor_copy(out=mrow_b[:], in_=pmt[0:1, 0:128])
                DV.inc(v.tensor_copy(out=q2[:, 0:1024], in_=qh[:, 0:1024]))
                t.wait_ge(DV.h, DV.n)
                PE.inc(t.matmul(pm[:, 0:16], ones_b[:], mrow_b[0:1, 0:16],
                                start=True, stop=True))
                if c.get("dbg"):
                    g.wait_ge(DV.h, DV.n)
                    dma(g, GD, c["dbg_m2"][:], mrow_b[0:1, 0:16])
                v.wait_ge(PE.h, PE.n)
                DV.inc(v.tensor_copy(out=Mh[:], in_=pm[:]))
                DV.inc(v.tensor_copy(out=mhe[:], in_=Mh[:]))
                mh_ready = DV.n

            t.wait_ge(GD.h, wv_gd)
            vh_dv = 0
            with nc.psum_tensor("pv", [128, 1024], F32) as pv:
                for sc in range(NT):
                    t.wait_ge(DV.h, vh_dv)
                    for dcc in range(NDC):
                        for p in range(2):
                            PE.inc(t.matmul(
                                pv[:, p * 512:(p + 1) * 512],
                                xT_b[:, dcc * S + sc * 128:
                                     dcc * S + sc * 128 + 128],
                                wsl[:, dcc * 1024 + p * 512:
                                    dcc * 1024 + (p + 1) * 512],
                                start=(dcc == 0), stop=False))
                    for p in range(2):
                        PE.inc(t.matmul(pv[:, p * 512:(p + 1) * 512],
                                        ones_b[:],
                                        c["bv_b"][0:1, p * 512:(p + 1) * 512],
                                        start=False, stop=True))
                    v.wait_ge(PE.h, PE.n)
                    v.wait_ge(GP.h, GP.n)
                    DV.inc(v.tensor_copy(
                        out=vsb[:, sc * NH * 65:(sc + 1) * NH * 65]
                        .rearrange("p (h w) -> p h w", w=65)[:, :, 0:64],
                        in_=pv[:].rearrange("p (h w) -> p h w", w=64)))
                    vh_dv = DV.n
                    if sc % 2 == 0:
                        p0.step()
            barrier()
            if stop_after == "P2":
                return

        if c.get("dbg"):
            dma(g, GD, c["dbg_qh"][:], qh[:])
            dma(g, GD, c["dbg_kh"][:], kh[:])
            dma(g, GD, c["dbg_vsb"][:], vsb[:])
            dma(g, GD, c["dbg_mh"][:], Mh[:])
            g.wait_ge(GD.h, GD.n)

        with ExitStack() as p34:
            p3 = p34.enter_context(ExitStack())
            e3 = p3.enter_context
            pT = e3(nc.sbuf_tensor("pT", [128, 2 * 2048], BF16))
            cpair = [e3(nc.sbuf_tensor(f"cpair{i}", [128, 2048], BF16))
                     for i in range(2)]
            Rbf = e3(nc.sbuf_tensor("Rbf", [128, 2048], BF16))
            cbf = e3(nc.sbuf_tensor("cbf", [128, 2048], BF16))
            r_bf = e3(nc.sbuf_tensor("r_bf", [1, 2048], BF16))
            psA = e3(nc.psum_tensor("psA", [128, 512], F32))
            psB = e3(nc.psum_tensor("psB", [128, 512], F32))
            psC = e3(nc.psum_tensor("psC", [128, 2048], F32))
            psAB = (psA, psB)

            s.wait_ge(DV.h, mh_ready)
            exp_on_bank = [0, 0]
            ctx_last_pe = [0, 0]
            rbf_evac_dv = 0
            cp_gd = [0, 0]
            for h in range(NH):
                cc, half = h // 2, h % 2
                rows = slice(64 * half, 64 * half + 64)
                p0.step()
                for scp in range(NT // 2):
                    if scp in (2, 5):
                        p0.step()
                    for half2 in range(2):
                        sc = 2 * scp + half2
                        for qq in range(4):
                            ps = psAB[qq % 2]
                            t.wait_ge(AC.h, exp_on_bank[qq % 2])
                            PE.inc(t.matmul(
                                ps[:, 0:512],
                                kh[rows, cc * S + sc * 128:
                                   cc * S + sc * 128 + 128],
                                qh[rows, cc * S + qq * 512:
                                   cc * S + (qq + 1) * 512],
                                start=True, stop=True))
                            s.wait_ge(PE.h, PE.n)
                            s.wait_ge(PE.h, ctx_last_pe[half2])
                            AC.inc(s.activation(
                                out=pT[:, half2 * 2048 + qq * 512:
                                       half2 * 2048 + (qq + 1) * 512],
                                in_=ps[:, 0:512], func=AF.Exp, scale=0.125,
                                bias=Mh[:, h:h + 1]))
                            exp_on_bank[qq % 2] = AC.n
                    t.wait_ge(AC.h, act_flush())
                    if scp == 0:
                        t.wait_ge(DV.h, rbf_evac_dv)
                    for half2 in range(2):
                        sc = 2 * scp + half2
                        for p in range(4):
                            PE.inc(t.matmul(
                                psC[0:65, p * 512:(p + 1) * 512],
                                vsb[:, sc * NH * 65 + h * 65:
                                    sc * NH * 65 + h * 65 + 65],
                                pT[:, half2 * 2048 + p * 512:
                                   half2 * 2048 + (p + 1) * 512],
                                start=(sc == 0), stop=(sc == NT - 1)))
                        ctx_last_pe[half2] = PE.n
                v.wait_ge(PE.h, PE.n)
                with nc.allow_low_precision(reason="r is a bf16 mm operand"):
                    DV.inc(v.reciprocal(out=r_bf[:], in_=psC[64:65, 0:2048]))
                DV.inc(v.tensor_copy(out=cbf[0:64, :], in_=psC[0:64, :]))
                v.wait_ge(DV.h, DV.n)
                DV.inc(v.tensor_copy(out=c["junkv"][0:1, 0:64],
                                     in_=r_bf[0:1, 0:64]))
                t.wait_ge(DV.h, DV.n)
                for p in range(4):
                    PE.inc(t.matmul(psC[64:128, p * 512:(p + 1) * 512],
                                    ones_b[0:1, 0:64],
                                    r_bf[0:1, p * 512:(p + 1) * 512],
                                    start=True, stop=True))
                v.wait_ge(PE.h, PE.n)
                DV.inc(v.tensor_copy(out=Rbf[0:64, :], in_=psC[64:128, :]))
                rbf_evac_dv = DV.n
                if half == 0:
                    v.wait_ge(GD.h, cp_gd[cc % 2])
                    DV.inc(v.tensor_mul(out=cpair[cc % 2][0:64, :],
                                        in0=cbf[0:64, :], in1=Rbf[0:64, :]))
                else:
                    DV.inc(v.tensor_mul(out=cpair[cc % 2][64:128, :],
                                        in0=cbf[0:64, :], in1=Rbf[0:64, :]))
                    g.wait_ge(DV.h, DV.n)
                    cp_gd[cc % 2] = dma(
                        g, GD, c["ctx_s"][cc * 128:(cc + 1) * 128, :],
                        cpair[cc % 2][:])
            barrier()
            if c.get("dbg"):
                dma(g, GD,
                    c["dbg_ctxn"][:].rearrange("p (cc t) -> p cc t", t=S),
                    c["ctx_s"][:].rearrange("(cc p) t -> p cc t", p=128))
                g.wait_ge(GD.h, GD.n)
            if stop_after == "P3":
                return
            p3.close()
            pqkv.close()

            p4 = p34.enter_context(ExitStack())
            e4 = p4.enter_context
            xst = [e4(nc.sbuf_tensor(f"xst{i}", [128, D], F32))
                   for i in range(2)]
            yb = e4(nc.sbuf_tensor("yb", [128, D], F32))
            zb = e4(nc.sbuf_tensor("zb", [128, D], F32))
            ub = e4(nc.sbuf_tensor("ub", [128, D], F32))
            x1t = [e4(nc.sbuf_tensor(f"x1t{i}", [128, D], F32))
                   for i in range(2)]
            x1c = e4(nc.sbuf_tensor("x1c", [128, D], F32))
            sqj = e4(nc.sbuf_tensor("sqj", [128, D], F32))
            st6p4 = e4(nc.sbuf_tensor("st6p4", [128, 12], F32))
            agp4 = e4(nc.sbuf_tensor("agp4", [128, 2], F32))
            st = [e4(nc.sbuf_tensor(f"st{i}", [128, 1], F32))
                  for i in range(7)]
            wo_b2 = e4(nc.sbuf_tensor("wo_b2", [128, NDC * 1024], BF16))
            ctile = [e4(nc.sbuf_tensor(f"ctile{i}", [128, NDC * 128], BF16))
                     for i in range(2)]
            po = [e4(nc.psum_tensor(f"po{i}", [128, 1024], F32))
                  for i in range(2)]
            for cc in range(NDC):
                dma(g, GD, wo_b2[:, cc * 1024:(cc + 1) * 1024],
                    c["wo_d"][cc * 128:(cc + 1) * 128, :])
            t.wait_ge(GD.h, GD.n)

            x_loaded = [0] * NT
            ct_ld = [0] * NT
            ct_pe = [0, 0]

            def load_ctile(tt):
                for cc in range(NDC):
                    dma(sy, LD, ctile[tt % 2][:, cc * 128:(cc + 1) * 128],
                        c["ctx_s"][cc * 128:(cc + 1) * 128,
                                   tt * 128:(tt + 1) * 128])
                ct_ld[tt] = LD.n

            for tt in range(2):
                load_ctile(tt)
                x_loaded[tt] = dma(sy, LD, xst[tt % 2][:],
                                   c["x_d"][tt * 128:(tt + 1) * 128, :])
            ln_bufs = {"st6": st6p4, "ag": agp4, "mu_neg": st[2],
                       "t0": st[3], "t1": st[4], "junk": st[5],
                       "fj": c["junkf"]}
            spill_gd = [0, 0]
            y_dv = [0, 0]
            for tt in range(NT):
                if tt >= 2:
                    t.wait_ge(DV.h, y_dv[tt % 2])
                pp4 = po[tt % 2]
                t.wait_ge(LD.h, ct_ld[tt])
                for cc in range(NDC):
                    for p in range(2):
                        PE.inc(t.matmul(
                            pp4[:, p * 512:(p + 1) * 512],
                            ctile[tt % 2][:, cc * 128:(cc + 1) * 128],
                            wo_b2[:, cc * 1024 + p * 512:
                                  cc * 1024 + (p + 1) * 512],
                            start=(cc == 0), stop=False))
                ct_pe[tt % 2] = PE.n
                for p in range(2):
                    PE.inc(t.matmul(pp4[:, p * 512:(p + 1) * 512], ones_b[:],
                                    c["bo_b"][0:1, p * 512:(p + 1) * 512],
                                    start=False, stop=True))
                v.wait_ge(PE.h, PE.n)
                v.wait_ge(LD.h, x_loaded[tt])
                DV.inc(v.tensor_add(out=yb[:], in0=pp4[:],
                                    in1=xst[tt % 2][:]))
                y_dv[tt % 2] = DV.n
                if c.get("dbg"):
                    g.wait_ge(DV.h, DV.n)
                    dma(g, GD, c["dbg_y"][tt * 128:(tt + 1) * 128, :], yb[:])
                emit_ln(nc, DV, AC, yb[:], zb[:], D, EPS_LN1, ln_bufs)
                v.wait_ge(AC.h, AC.n)
                if c.get("dbg"):
                    g.wait_ge(AC.h, AC.n)
                    dma(g, GD, c["dbg_z"][tt * 128:(tt + 1) * 128, :], zb[:])
                    v.wait_ge(GD.h, GD.n)
                v.scalar_tensor_tensor(out=sqj[:], in0=zb[:], scalar=1.0,
                                       in1=gb[:], op0=ALU.mult,
                                       op1=ALU.mult)
                v.tensor_add(out=zb[:], in0=sqj[:], in1=bb[:])
                DV.inc(v.tensor_add(out=ub[:], in0=zb[:], in1=xst[tt % 2][:]))
                if tt + 2 < NT:
                    sy.wait_ge(DV.h, DV.n)
                    sy.wait_ge(PE.h, ct_pe[tt % 2])
                    load_ctile(tt + 2)
                    x_loaded[tt + 2] = dma(sy, LD, xst[tt % 2][:],
                                           c["x_d"][(tt + 2) * 128:
                                                    (tt + 3) * 128, :])
                emit_ln(nc, DV, AC, ub[:], x1t[tt % 2][:], D, EPS_LN, ln_bufs)
                v.wait_ge(AC.h, AC.n)
                if tt >= 1:
                    v.wait_ge(GD.h, spill_gd[0])
                DV.inc(v.tensor_copy(out=x1c[:], in_=x1t[tt % 2][:]))
                g.wait_ge(DV.h, DV.n)
                spill_gd[0] = dma(g, GD,
                                  c["x1sp"][tt * 128:(tt + 1) * 128, :],
                                  x1c[:])
                if c.get("dbg"):
                    spill_gd[0] = dma(
                        g, GD, c["dbg_x1"][tt * 128:(tt + 1) * 128, :],
                        x1c[:])
            while not p0.done:
                p0.step()
            barrier()
            if stop_after == "P4":
                return

    p0stk.close()
    with ExitStack() as p5:
        e5 = p5.enter_context
        wgT = e5(nc.sbuf_tensor("wgT", [128, NDC * I], BF16))
        wuT = e5(nc.sbuf_tensor("wuT", [128, NDC * I], BF16))
        wdT = e5(nc.sbuf_tensor("wdT", [128, NIC * 1024], BF16))
        x1l = [e5(nc.sbuf_tensor(f"x1l{i}", [128, D], F32)) for i in range(2)]
        xq = [e5(nc.sbuf_tensor(f"xq{i}", [128, D], BF16)) for i in range(2)]
        xqT = [e5(nc.sbuf_tensor(f"xqT{i}", [128, D], BF16))
               for i in range(2)]
        silu_sb = e5(nc.sbuf_tensor("silu_sb", [128, IH], F32))
        h_sb = e5(nc.sbuf_tensor("h_sb", [128, IH], F32))
        hq = e5(nc.sbuf_tensor("hq", [128, IH], BF16))
        hqT = e5(nc.sbuf_tensor("hqT", [128, I], BF16))
        ub2 = e5(nc.sbuf_tensor("ub2", [128, D], F32))
        x2b = e5(nc.sbuf_tensor("x2b", [128, D], F32))
        x2c = e5(nc.sbuf_tensor("x2c", [128, D], F32))
        st6p5 = e5(nc.sbuf_tensor("st6p5", [128, 12], F32))
        agp5 = e5(nc.sbuf_tensor("agp5", [128, 2], F32))
        st2 = [e5(nc.sbuf_tensor(f"st2{i}", [128, 1], F32)) for i in range(7)]
        qsc2 = {k: e5(nc.sbuf_tensor("p5q" + k,
                                     [128, 32 if k in "MSR" else 512],
                                     BF16 if k == "sg" else F32))
                for k in ("M", "S", "R", "absx", "a", "c1", "c2", "sg")}
        dps = e5(nc.psum_tensor("dps", [128, 1024], F32))
        gps = e5(nc.psum_tensor("gps", [128, 1024], F32))
        gps2 = e5(nc.psum_tensor("gps2", [128, 1024], F32))
        tps2 = e5(nc.psum_tensor("tps2", [128, 2048], BF16))

        x1_ld = [0] * NT
        for tt in range(2):
            dma(sy, LD, x1l[tt % 2][:],
                c["x1sp"][tt * 128:(tt + 1) * 128, :])
        for cc in range(NDC):
            dma(sy, LD, wgT[:, cc * I:(cc + 1) * I],
                c["wgT_s"][cc * 128:(cc + 1) * 128, :])
            dma(sy, LD, wuT[:, cc * I:(cc + 1) * I],
                c["wuT_s"][cc * 128:(cc + 1) * 128, :])
        gu_ready = LD.n
        for ic in range(NIC):
            dma(sy, LD, wdT[:, ic * 1024:(ic + 1) * 1024],
                c["wdT_s"][ic * 128:(ic + 1) * 128, :])
        wd_ready = LD.n
        x1_ld = [0] * NT
        x1_ld[0] = x1_ld[1] = gu_ready
        t.wait_ge(LD.h, gu_ready)
        ln_bufs2 = {"st6": st6p5, "ag": agp5, "mu_neg": st2[2],
                    "t0": st2[3], "t1": st2[4], "junk": st2[5],
                    "fj": c["junkf"]}
        out_gd = [0, 0]
        gu_ps = (gps, gps2)
        gu_dv = {0: 0, 1: 0}
        xqT_dv = [0, 0]

        def quant_x1(tt):
            par = tt % 2
            v.wait_ge(LD.h, x1_ld[tt])
            s.wait_ge(LD.h, x1_ld[tt])
            g.wait_ge(LD.h, x1_ld[tt])
            if c.get("dbg"):
                g.wait_ge(LD.h, x1_ld[tt])
                dma(g, GD, c["dbg_x1l"][tt * 128:(tt + 1) * 128, :],
                    x1l[par][:])
            for qo in (0, 512):
                emit_quant(nc, DV, AC, x1l[par][:, qo:qo + 512],
                           xq[par][:, qo:qo + 512], 512, qsc2)
            if c.get("dbg"):
                g.wait_ge(DV.h, DV.n)
                dma(g, GD, c["dbg_xq"][tt * 128:(tt + 1) * 128, :], xq[par][:])
            dve_flush()
            return DV.n

        def transpose_xq(tt, dvq):
            par = tt % 2
            t.wait_ge(DV.h, dvq)
            for r in range(8):
                PE.inc(t.transpose(tps2[:, r * 128:(r + 1) * 128],
                                   xq[par][:, r * 128:(r + 1) * 128],
                                   ident[:]))
            v.wait_ge(PE.h, PE.n)
            DV.inc(v.tensor_copy(out=xqT[par][:], in_=tps2[:, 0:1024]))
            xqT_dv[par] = DV.n

        dvq0 = quant_x1(0)
        transpose_xq(0, dvq0)
        pieces6 = [(hf, poff, pw) for hf in range(2)
                   for (poff, pw) in _pieces(IH)]
        tr_pe = [0, 0]
        for tt in range(NT):
            par = tt % 2
            t.wait_ge(DV.h, xqT_dv[par])
            fl = [0] * 6

            def gu_piece(k):
                hf, poff, pw = pieces6[k]
                pp = gu_ps[k % 2]
                t.wait_ge(DV.h, gu_dv[k % 2])
                for wi, wTT in enumerate((wgT, wuT)):
                    for dcc in range(NDC):
                        PE.inc(t.matmul(
                            pp[:, wi * 512:wi * 512 + pw],
                            xqT[par][:, dcc * 128:(dcc + 1) * 128],
                            wTT[:, dcc * I + hf * IH + poff:
                                dcc * I + hf * IH + poff + pw],
                            start=(dcc == 0), stop=(dcc == NDC - 1)))
                s.wait_ge(PE.h, PE.n)
                AC.inc(s.activation(out=silu_sb[:, poff:poff + pw],
                                    in_=pp[:, 0:pw], func=AF.Silu))
                fl[k] = act_flush()

            def mq_piece(k):
                hf, poff, pw = pieces6[k]
                pp = gu_ps[k % 2]
                v.wait_ge(AC.h, fl[k])
                if k == 3:
                    v.wait_ge(PE.h, tr_pe[0])
                DV.inc(v.scalar_tensor_tensor(
                    out=h_sb[:, poff:poff + pw], in0=pp[:, 512:512 + pw],
                    scalar=1.0, in1=silu_sb[:, poff:poff + pw],
                    op0=ALU.mult, op1=ALU.mult))
                gu_dv[k % 2] = DV.n
                emit_quant(nc, DV, AC, h_sb[:, poff:poff + pw],
                           hq[:, poff:poff + pw], pw, qsc2, war_dv=DV.n)

            def hq_transposes(hf):
                t.wait_ge(DV.h, DV.n)
                for gi in range(2):
                    r0, r1 = gi * 8, min(gi * 8 + 8, 11)
                    bank = gi % 2
                    for r in range(r0, r1):
                        PE.inc(t.transpose(
                            tps2[:, bank * 1024 + (r - r0) * 128:
                                 bank * 1024 + (r - r0 + 1) * 128],
                            hq[:, r * 128:(r + 1) * 128], ident[:]))
                    v.wait_ge(PE.h, PE.n)
                    DV.inc(v.tensor_copy(
                        out=hqT[:, hf * IH + r0 * 128:hf * IH + r1 * 128],
                        in_=tps2[:, bank * 1024:
                                 bank * 1024 + (r1 - r0) * 128]))
                tr_pe[hf] = PE.n

            for k in range(6):
                if k >= 1:
                    mq_piece(k - 1)
                gu_piece(k)
                if k == 3:
                    if c.get("dbg"):
                        g.wait_ge(DV.h, DV.n)
                        dma(g, GD, c["dbg_hq"][tt * 128:(tt + 1) * 128,
                                               0:IH], hq[:])
                    hq_transposes(0)
            mq_piece(5)
            if c.get("dbg"):
                g.wait_ge(DV.h, DV.n)
                dma(g, GD, c["dbg_hq"][tt * 128:(tt + 1) * 128,
                                       IH:2 * IH], hq[:])
            dve_flush()
            hq_transposes(1)
            hqT_dv = DV.n
            dvq_next = None
            if tt + 1 < NT:
                dvq_next = quant_x1(tt + 1)
            t.wait_ge(DV.h, hqT_dv)
            if tt == 0:
                t.wait_ge(LD.h, wd_ready)
            for ic in range(NIC):
                for p in range(2):
                    PE.inc(t.matmul(
                        dps[:, p * 512:(p + 1) * 512],
                        hqT[:, ic * 128:(ic + 1) * 128],
                        wdT[:, ic * 1024 + p * 512:ic * 1024 + (p + 1) * 512],
                        start=(ic == 0), stop=(ic == NIC - 1)))
            down_pe = PE.n
            if dvq_next is not None:
                transpose_xq(tt + 1, dvq_next)
            v.wait_ge(PE.h, down_pe)
            DV.inc(v.tensor_add(out=ub2[:], in0=dps[:],
                                in1=x1l[par][:]))
            if c.get("dbg"):
                g.wait_ge(DV.h, DV.n)
                dma(g, GD, c["dbg_u"][tt * 128:(tt + 1) * 128, :], ub2[:])
                v.wait_ge(GD.h, GD.n)
            if tt + 2 < NT:
                sy.wait_ge(DV.h, DV.n)
                sy.wait_ge(AC.h, AC.n)
                x1_ld[tt + 2] = dma(sy, LD, x1l[par][:],
                                    c["x1sp"][(tt + 2) * 128:
                                              (tt + 3) * 128, :])
            emit_ln(nc, DV, AC, ub2[:], x2b[:], D, EPS_LN, ln_bufs2)
            v.wait_ge(AC.h, AC.n)
            if tt >= 1:
                v.wait_ge(GD.h, out_gd[0])
            DV.inc(v.tensor_copy(out=x2c[:], in_=x2b[:]))
            g.wait_ge(DV.h, DV.n)
            out_gd[0] = dma(g, GD, c["out_d"][tt * 128:(tt + 1) * 128, :],
                            x2c[:])
        g.wait_ge(GD.h, GD.n)
        sy.wait_ge(GD.h, GD.n)


def build_nc(stop_after=None, p0_quant_only=False, dbg=False):
    nc = bass.Bass("TRN2", target_bir_lowering=False)
    c = {"stop_after": stop_after, "p0_quant_only": p0_quant_only, "dbg": dbg}
    c["x_d"] = nc.dram_tensor("x", [S, D], F32, kind="ExternalInput").ap()
    c["xT_d"] = nc.dram_tensor("xT", [D, S], F32, kind="ExternalInput").ap()
    c["wq_d"] = nc.dram_tensor("wq", [D, HK], F32, kind="ExternalInput").ap()
    c["wk_d"] = nc.dram_tensor("wk", [D, HK], F32, kind="ExternalInput").ap()
    c["wv_d"] = nc.dram_tensor("wv", [D, HK], F32, kind="ExternalInput").ap()
    c["bq_d"] = nc.dram_tensor("bq", [128, 8], F32, kind="ExternalInput").ap()
    c["bk_d"] = nc.dram_tensor("bk", [128, 8], F32, kind="ExternalInput").ap()
    c["bv_d"] = nc.dram_tensor("bv", [1, HK], F32, kind="ExternalInput").ap()
    c["wo_d"] = nc.dram_tensor("wo", [HK, D], F32, kind="ExternalInput").ap()
    c["bo_d"] = nc.dram_tensor("bo", [1, D], F32, kind="ExternalInput").ap()
    c["gam_d"] = nc.dram_tensor("gam", [1, D], F32, kind="ExternalInput").ap()
    c["bet_d"] = nc.dram_tensor("bet", [1, D], F32, kind="ExternalInput").ap()
    c["wg_d"] = nc.dram_tensor("wg", [I, D], F32, kind="ExternalInput").ap()
    c["wu_d"] = nc.dram_tensor("wu", [I, D], F32, kind="ExternalInput").ap()
    c["wd_d"] = nc.dram_tensor("wd", [D, I], F32, kind="ExternalInput").ap()
    c["out_d"] = nc.dram_tensor("out", [S, D], F32, kind="ExternalOutput").ap()
    c["x1sp"] = nc.dram_tensor("x1sp", [S, D], F32, kind="Internal").ap()
    c["wgT_s"] = nc.dram_tensor("wgT_s", [D, I], BF16, kind="Internal").ap()
    c["wuT_s"] = nc.dram_tensor("wuT_s", [D, I], BF16, kind="Internal").ap()
    c["wdT_s"] = nc.dram_tensor("wdT_s", [I, D], BF16, kind="Internal").ap()
    c["t16"] = nc.dram_tensor("t16", [1, 16], F32, kind="Internal").ap()
    c["ctx_s"] = nc.dram_tensor("ctx_s", [HK, S], BF16, kind="Internal").ap()
    if c.get("dbg"):
        c["dbg_qh"] = nc.dram_tensor("dbg_qh", [128, NDC * S], BF16,
                                     kind="ExternalOutput").ap()
        c["dbg_kh"] = nc.dram_tensor("dbg_kh", [128, NDC * S], BF16,
                                     kind="ExternalOutput").ap()
        c["dbg_vsb"] = nc.dram_tensor("dbg_vsb", [128, NT * NH * 65], BF16,
                                      kind="ExternalOutput").ap()
        c["dbg_mh"] = nc.dram_tensor("dbg_mh", [128, 16], F32,
                                     kind="ExternalOutput").ap()
        c["dbg_m2"] = nc.dram_tensor("dbg_m2", [1, 16], BF16,
                                     kind="ExternalOutput").ap()
        c["dbg_qn"] = nc.dram_tensor("dbg_qn", [16, 1], F32,
                                     kind="ExternalOutput").ap()
        c["dbg_kn"] = nc.dram_tensor("dbg_kn", [16, 1], F32,
                                     kind="ExternalOutput").ap()
        c["dbg_pn"] = nc.dram_tensor("dbg_pn", [16, 2048], F32,
                                     kind="ExternalOutput").ap()
        c["dbg_m2a"] = nc.dram_tensor("dbg_m2a", [16, 1], F32,
                                      kind="ExternalOutput").ap()
        c["dbg_msq"] = nc.dram_tensor("dbg_msq", [16, 1], F32,
                                      kind="ExternalOutput").ap()
        c["dbg_m2b"] = nc.dram_tensor("dbg_m2b", [16, 1], F32,
                                      kind="ExternalOutput").ap()
        c["dbg_mrow"] = nc.dram_tensor("dbg_mrow", [1, 16], F32,
                                       kind="ExternalOutput").ap()
        c["dbg_ctxn"] = nc.dram_tensor("dbg_ctxn", [128, NDC * S], BF16,
                                       kind="ExternalOutput").ap()
        c["dbg_x1"] = nc.dram_tensor("dbg_x1", [S, D], F32,
                                     kind="ExternalOutput").ap()
        c["dbg_y"] = nc.dram_tensor("dbg_y", [S, D], F32,
                                    kind="ExternalOutput").ap()
        c["dbg_z"] = nc.dram_tensor("dbg_z", [S, D], F32,
                                    kind="ExternalOutput").ap()
        c["dbg_xq"] = nc.dram_tensor("dbg_xq", [S, D], BF16,
                                     kind="ExternalOutput").ap()
        c["dbg_hq"] = nc.dram_tensor("dbg_hq", [S, I], BF16,
                                     kind="ExternalOutput").ap()
        c["dbg_u"] = nc.dram_tensor("dbg_u", [S, D], F32,
                                    kind="ExternalOutput").ap()
        c["dbg_x1l"] = nc.dram_tensor("dbg_x1l", [S, D], F32,
                                      kind="ExternalOutput").ap()

    with ExitStack() as top:
        e = top.enter_context
        c["ident"] = e(nc.sbuf_tensor("ident", [128, 128], BF16))
        c["ones_f"] = e(nc.sbuf_tensor("ones_f", [1, 128], F32))
        c["ones_b"] = e(nc.sbuf_tensor("ones_b", [1, 128], BF16))
        c["Mh"] = e(nc.sbuf_tensor("Mh", [128, 16], F32))
        c["bqc"] = e(nc.sbuf_tensor("bqc", [128, 8], F32))
        c["bkc"] = e(nc.sbuf_tensor("bkc", [128, 8], F32))
        c["bv_b"] = e(nc.sbuf_tensor("bv_b", [1, HK], BF16))
        c["bo_b"] = e(nc.sbuf_tensor("bo_b", [1, D], BF16))
        c["rowtmp_bo"] = e(nc.sbuf_tensor("rowtmp_bo", [1, D], F32))
        c["junkf"] = e(nc.sbuf_tensor("junkf", [128, 32], F32))
        c["junkv"] = e(nc.sbuf_tensor("junkv", [128, 64], F32))
        c["LD"] = Sem(e(nc.semaphore("sLD")))
        c["GD"] = Sem(e(nc.semaphore("sGD")))
        c["PE"] = Sem(e(nc.semaphore("sPE")))
        c["AC"] = Sem(e(nc.semaphore("sAC")))
        c["DV"] = Sem(e(nc.semaphore("sDV")))
        c["GP"] = Sem(e(nc.semaphore("sGP")))
        block = e(nc.Block())

        @block.vector
        def _(_v):
            emit_all(nc, c)
    return nc


_NC_CACHE = {}


def kernel(L0_attn_x, L0_attn_Wq, L0_attn_bq, L0_attn_Wk, L0_attn_bk,
           L0_attn_Wv, L0_attn_bv, L0_attn_Wo, L0_attn_bo, L0_attn_gamma,
           L0_attn_beta, L0_ffn_w_gate, L0_ffn_w_up, L0_ffn_w_down):
    x = np.ascontiguousarray(np.asarray(L0_attn_x, dtype=np.float32))
    B = x.shape[0]
    wq = np.ascontiguousarray(np.asarray(L0_attn_Wq, np.float32).reshape(D, HK))
    wk = np.ascontiguousarray(np.asarray(L0_attn_Wk, np.float32).reshape(D, HK))
    wv = np.ascontiguousarray(np.asarray(L0_attn_Wv, np.float32).reshape(D, HK))
    bq = np.ascontiguousarray(np.asarray(L0_attn_bq, np.float32).reshape(8, 128).T)
    bk = np.ascontiguousarray(np.asarray(L0_attn_bk, np.float32).reshape(8, 128).T)
    bv = np.ascontiguousarray(np.asarray(L0_attn_bv, np.float32).reshape(1, HK))
    wo = np.ascontiguousarray(np.asarray(L0_attn_Wo, np.float32).reshape(HK, D))
    bo = np.ascontiguousarray(np.asarray(L0_attn_bo, np.float32).reshape(1, D))
    gam = np.ascontiguousarray(np.asarray(L0_attn_gamma, np.float32).reshape(1, D))
    bet = np.ascontiguousarray(np.asarray(L0_attn_beta, np.float32).reshape(1, D))
    wg = np.ascontiguousarray(np.asarray(L0_ffn_w_gate, np.float32))
    wu = np.ascontiguousarray(np.asarray(L0_ffn_w_up, np.float32))
    wd = np.ascontiguousarray(np.asarray(L0_ffn_w_down, np.float32))

    if "nc" not in _NC_CACHE:
        _NC_CACHE["nc"] = build_nc()
    nc = _NC_CACHE["nc"]
    in_maps = []
    for b in range(B):
        xb = np.ascontiguousarray(x[b])
        in_maps.append({
            "x": xb, "xT": np.ascontiguousarray(xb.T),
            "wq": wq, "wk": wk, "wv": wv, "bq": bq, "bk": bk, "bv": bv,
            "wo": wo, "bo": bo, "gam": gam, "bet": bet,
            "wg": wg, "wu": wu, "wd": wd,
        })
    res = run_bass_kernel_spmd(nc, in_maps, core_ids=list(range(B)))
    return np.stack([np.asarray(r["out"], dtype=np.float32)
                     for r in res.results], axis=0)
